# version 10
# baseline (speedup 1.0000x reference)
import numpy as np
from contextlib import ExitStack

import concourse.bass as bass
import concourse.mybir as mybir

F32 = mybir.dt.float32
BF16 = mybir.dt.bfloat16
I8 = mybir.dt.int8
AF = mybir.ActivationFunctionType
ALU = mybir.AluOpType
AX = mybir.AxisListType

H, D = 16, 64
HID = 1024
FF = 3072
TILE = 512
EPS = 1e-6
LNEPS = 1e-5
SCALE127 = float(np.float32(1.0 / 127.0))


class Sched:
    ENGINES = ("sync", "tensor", "vector", "scalar", "gpsimd")

    def __init__(self):
        self.items = []
        self.ctr = {}
        self.last_wait = {}
        self.marks = []

    def mark(self, name):
        self.marks.append((name, len(self.items)))

    def tok(self, sem):
        return (sem, self.ctr.get(sem, 0))

    def op(self, engine, fn, waits=(), inc=None, inc_n=1):
        assert engine in self.ENGINES
        w = []
        for t in waits:
            if t is None:
                continue
            (sem, val) = t
            if val <= 0:
                continue
            assert val <= self.ctr.get(sem, 0), (sem, val, self.ctr.get(sem, 0))
            key = (engine, sem)
            if self.last_wait.get(key, 0) >= val:
                continue
            self.last_wait[key] = val
            w.append((sem, val))
        tok = None
        incs = []
        if inc is not None:
            self.ctr[inc] = self.ctr.get(inc, 0) + inc_n
            tok = (inc, self.ctr[inc])
            incs.append((inc, inc_n))
        self.items.append((engine, w, fn, incs))
        return tok

    def dma(self, engine, out, in_, waits=(), sem=None):
        sem = sem or ("d_" + engine)
        return self.op(
            engine,
            (lambda o, i: (lambda eng: eng.dma_start(out=o, in_=i)))(out, in_),
            waits=waits, inc=sem, inc_n=16)

    def replay(self, engine_name, eng, sems):
        for (e, waits, fn, incs) in self.items:
            if e != engine_name:
                continue
            for (sem, val) in waits:
                eng.wait_ge(sems[sem], val)
            instr = fn(eng)
            for (sem, n) in incs:
                assert instr is not None
                instr.then_inc(sems[sem], n)


class Psum:
    def __init__(self):
        self.free_tok = [[] for _ in range(8)]

    def use(self, b):
        return list(self.free_tok[b])

    def done(self, b, toks):
        if not isinstance(toks, list):
            toks = [toks]
        self.free_tok[b] = toks


def build_core_program(n_seq=4, tiles_per_seq=4, dbg=False, max_items=None):
    T_SEQ = tiles_per_seq * TILE
    NTOK = n_seq * T_SEQ
    nc = bass.Bass("TRN2", target_bir_lowering=False)
    S = Sched()
    PS = Psum()
    war = {}

    def wwait(name):
        return war.pop(name, [])

    def war_add(name, tok):
        war.setdefault(name, []).append(tok)

    xd = nc.dram_tensor("x", [NTOK, HID], F32, kind="ExternalInput").ap()
    outd = nc.dram_tensor("out", [NTOK, HID], F32, kind="ExternalOutput").ap()
    wq_d = nc.dram_tensor("wq", [2, 128, 8, 512], F32, kind="ExternalInput").ap()
    wk_d = nc.dram_tensor("wk", [2, 128, 8, 512], F32, kind="ExternalInput").ap()
    wv_d = nc.dram_tensor("wv", [2, 128, 8, 512], F32, kind="ExternalInput").ap()
    wo_d = nc.dram_tensor("wo", [2, 128, 8, 512], F32, kind="ExternalInput").ap()
    wb_d = nc.dram_tensor("wb", [128, 8, 16], F32, kind="ExternalInput").ap()
    bb_d = nc.dram_tensor("bb", [1, 16], F32, kind="ExternalInput").ap()
    bo_d = nc.dram_tensor("bo", [1, HID], F32, kind="ExternalInput").ap()
    sc_d = nc.dram_tensor("lnscale", [128, HID], F32, kind="ExternalInput").ap()
    wg_d = nc.dram_tensor("wg", [24, 128, 8, 128], I8, kind="ExternalInput").ap()
    wu_d = nc.dram_tensor("wu", [24, 128, 8, 128], I8, kind="ExternalInput").ap()
    wd_d = nc.dram_tensor("wd", [24, 128, HID], I8, kind="ExternalInput").ap()
    u_d = nc.dram_tensor("utri", [128, 128], F32, kind="ExternalInput").ap()
    bof_d = nc.dram_tensor("bo_full", [128, HID], F32, kind="ExternalInput").ap()
    whl_d = nc.dram_tensor("whl", [4, 2, 128, 2, 8, 512], BF16).ap()
    wgb_d = nc.dram_tensor("wgb", [24, 128, 8, 128], BF16).ap()
    wub_d = nc.dram_tensor("wub", [24, 128, 8, 128], BF16).ap()
    wdb_d = nc.dram_tensor("wdb", [24, 128, HID], BF16).ap()
    id_d = nc.dram_tensor("iden", [128, 128], F32, kind="ExternalInput").ap()
    cos_d = nc.dram_tensor("costab", [128, 16, 32], F32, kind="ExternalInput").ap()
    sin_d = nc.dram_tensor("sintab", [128, 16, 32], F32, kind="ExternalInput").ap()

    dbg_t = {}
    if dbg:
        for nm in ["xn1", "qt", "kt", "vt", "beta", "pp", "ppi", "den",
                   "at0", "attnF", "h", "ab", "x1", "xn2", "fout", "st"]:
            dbg_t[nm] = nc.dram_tensor("dbg_" + nm, [128, 4096], F32,
                                       kind="ExternalOutput").ap()

    ctx = ExitStack()

    def sbt(name, shape, dtype=F32):
        return ctx.enter_context(nc.sbuf_tensor(name, shape, dtype))

    G = [sbt(f"g{i}", [128, 4096]) for i in range(8)]
    wbuf = sbt("wbuf", [128, 2, 2, 2, 512], BF16)
    wgbuf = sbt("wgbuf", [128, 2, 8, 128], BF16)
    wubuf = sbt("wubuf", [128, 2, 8, 128], BF16)
    wdbuf = sbt("wdbuf", [128, 2, 512], BF16)
    wbtile = sbt("wbtile", [128, 2, 8, 16], BF16)
    bo_f = sbt("bo_f", [128, HID])
    qF = sbt("qF", [128, 8, 128])
    kF = sbt("kF", [128, 8, 128])
    AT = sbt("AT", [128, 8, 128])
    Sdup = sbt("Sdup", [128, 16, 64])
    rs1 = sbt("rs1", [128, 1024])
    sdump = rs1
    z0row = sbt("z0row", [1, HID])
    bb_row = sbt("bb_row", [1, 16])
    ones1 = sbt("ones1", [1, 128])
    scale_f = sbt("scale_f", [128, HID])
    utri = sbt("utri_s", [128, 128])
    iden = sbt("iden_s", [128, 128])
    costab = sbt("costab_s", [128, 16, 32])
    sintab = sbt("sintab_s", [128, 16, 32])
    bt_beta = sbt("bt_beta", [128, 64])
    bt_ln = sbt("bt_ln", [128, 4, 16])
    bt_P = sbt("bt_P", [128, 4, 16])
    bt_Pi = sbt("bt_Pi", [128, 4, 16])
    bt_PC = sbt("bt_PC", [128, 4, 16])
    pcrow = sbt("pcrow", [1, 64])
    ones_col = sbt("ones_col", [128, 1])
    cpool = sbt("cpool", [128, 8])
    den_s = sbt("den_s", [128, 16])
    drcp = sbt("drcp", [128, 16])
    qsc = sbt("qsc", [128, 16])
    st_sum = sbt("st_sum", [128, 8])
    st_sq = sbt("st_sq", [128, 8])
    st_mu = sbt("st_mu", [128, 4])
    st_t1 = sbt("st_t1", [128, 4])
    st_t2 = sbt("st_t2", [128, 4])
    st_r = sbt("st_r", [128, 4])

    ps = ctx.enter_context(nc.psum_tensor("ps", [128, 8, 512], F32))

    def vop(fn, waits=(), inc="v"):
        return S.op("vector", fn, waits=list(waits) + [S.tok("v")], inc=inc)

    def sop(fn, waits=(), inc="s"):
        return S.op("scalar", fn, waits=list(waits) + [S.tok("s")], inc=inc)

    def top(fn, waits=(), inc="t"):
        return S.op("tensor", fn, waits=waits, inc=inc)

    for (dst, src) in [(bb_row[:], bb_d[:, :]),
                       (bo_f[:], bof_d[:, :]), (scale_f[:], sc_d[:, :]),
                       (utri[:], u_d[:, :]), (iden[:], id_d[:, :]),
                       (costab[:], cos_d[:, :, :]), (sintab[:], sin_d[:, :, :])]:
        S.dma("sync", dst, src, sem="d_pre")
    t_pre = S.tok("d_pre")
    vop(lambda eng: eng.memset(ones1[:], 1.0))
    vop(lambda eng: eng.memset(ones_col[:], 1.0))
    for i, val in enumerate([LNEPS, EPS, -0.5, -1.0, 1.0 / HID, SCALE127]):
        vop(lambda eng, i=i, val=val: eng.memset(cpool[:, i:i + 1], float(val)))
    t_ones = S.tok("v")
    c_lneps = cpool[:, 0:1]
    c_rmseps = cpool[:, 1:2]
    c_mhalf = cpool[:, 2:3]
    c_mone = cpool[:, 3:4]
    c_invh = cpool[:, 4:5]
    c_s127 = cpool[:, 5:6]

    g0f = G[0][:]
    g1b = G[1][:].bitcast(BF16)
    t_ph0 = None
    for wi, wdram in enumerate([wq_d, wk_d, wv_d, wo_d]):
        for half in range(2):
            tin = S.dma("sync", g0f,
                        wdram[half, :, :, :].rearrange("p k n -> p (k n)"),
                        waits=[t_ph0] if t_ph0 else [t_pre], sem="d_pre")
            vop(lambda eng: eng.tensor_copy(g1b[:, 0:4096], g0f), waits=[tin])
            tv = vop(lambda eng: eng.tensor_tensor(
                g1b[:, 4096:8192], g0f, g1b[:, 0:4096], ALU.subtract))
            t_ph0 = S.dma("sync",
                          whl_d[wi, half, :, :, :, :]
                          .rearrange("p u k n -> p (u k n)"),
                          g1b[:, :], waits=[tv], sem="d_pre")
    g0b = G[0][:].bitcast(BF16)
    for (src_d, dst_d, four) in [(wg_d, wgb_d, True), (wu_d, wub_d, True),
                                 (wd_d, wdb_d, False)]:
        for m0 in range(0, 24, 8):
            if four:
                sap = src_d[m0:m0 + 8].rearrange("m p k n -> p m (k n)")
                dap = dst_d[m0:m0 + 8].rearrange("m p k n -> p m (k n)")
            else:
                sap = src_d[m0:m0 + 8].rearrange("m p n -> p m n")
                dap = dst_d[m0:m0 + 8].rearrange("m p n -> p m n")
            tin = S.dma("gpsimd", g0b[:, :], sap, waits=[t_ph0], sem="d_ff0")
            t_ph0 = S.dma("gpsimd", dap, g0b[:, :], waits=[tin], sem="d_ff0")
    twb = S.dma("sync", rs1[:, 0:128],
                wb_d[:, :, :].rearrange("p k h -> p (k h)"),
                waits=[t_ph0], sem="d_pre")
    vop(lambda eng: eng.tensor_copy(
        wbtile[:, 0, :, :].rearrange("p k h -> p (k h)"), rs1[:, 0:128]),
        waits=[twb])
    t_wb = vop(lambda eng: eng.tensor_tensor(
        wbtile[:, 1, :, :].rearrange("p k h -> p (k h)"), rs1[:, 0:128],
        wbtile[:, 0, :, :].rearrange("p k h -> p (k h)"), ALU.subtract))
    war_add("G0", S.tok("d_pre"))
    war_add("G1", S.tok("d_pre"))
    war_add("rs1", t_wb)

    cur_table = [None]

    def use_table(func):
        return

    cur_dbg = [False]

    def dump(name, src_ap, waits, warname=None):
        if not (dbg and cur_dbg[0]):
            return
        n = src_ap.shape[-1] if src_ap.ndim == 2 else int(
            np.prod(src_ap.shape[1:]))
        flat = src_ap if src_ap.ndim == 2 else src_ap.rearrange(
            "p " + " ".join(f"a{i}" for i in range(src_ap.ndim - 1))
            + " -> p (" + " ".join(f"a{i}" for i in range(src_ap.ndim - 1)) + ")")
        tok = S.op("gpsimd",
                   (lambda d, s_: (lambda eng: eng.dma_start(out=d, in_=s_)))(
                       dbg_t[name][:, 0:n], flat),
                   waits=list(waits) + [S.tok("d_dbg")], inc="d_dbg",
                   inc_n=16)
        if warname:
            war_add(warname, tok)
        return tok

    wbuf_free = [None, None]
    wgbuf_free = [None, None]
    wubuf_free = [None, None]
    wdbuf_free = [None, None]

    def ln_finalize(sl, eps_ap, t_sq_tok, t_sum_tok):
        vop(lambda eng: eng.tensor_scalar(
            st_mu[:, sl], st_sum[:, sl], 1.0 / HID, None, ALU.mult),
            waits=[t_sum_tok])
        vop(lambda eng: eng.tensor_scalar(
            st_t1[:, sl], st_sq[:, sl], 1.0 / HID, None, ALU.mult),
            waits=[t_sq_tok])
        vop(lambda eng: eng.tensor_tensor(
            st_t2[:, sl], st_mu[:, sl], st_mu[:, sl], ALU.mult))
        tv = vop(lambda eng: eng.tensor_tensor(
            st_t1[:, sl], st_t1[:, sl], st_t2[:, sl], ALU.subtract))
        sop(lambda eng: eng.activation(st_t1[:, sl], st_t1[:, sl],
                                       AF.Ln, bias=eps_ap),
            waits=[tv, t_ones])
        t = sop(lambda eng: eng.activation(st_r[:, sl], st_t1[:, sl],
                                           AF.Exp, scale=c_mhalf))
        return t

    def rms_finalize(sl, eps_ap, t_sq_tok):
        sop(lambda eng: eng.activation(st_t1[:, sl], st_sq[:, sl],
                                       AF.Ln, bias=eps_ap, scale=c_invh),
            waits=[t_sq_tok, t_ones])
        t = sop(lambda eng: eng.activation(st_r[:, sl], st_t1[:, sl],
                                           AF.Exp, scale=c_mhalf))
        return t

    LN_EXP = AF.Exp
    SILU = AF.Silu

    seq_tiles = [(sq, ti) for sq in range(n_seq) for ti in range(tiles_per_seq)]

    for gti, (sq, ti) in enumerate(seq_tiles):
        tok0 = sq * T_SEQ + ti * TILE
        DBG = dbg and gti == 0
        cur_dbg[0] = gti == 0

        g_x = G[0][:].rearrange("p (c f) -> p c f", f=HID)
        g_xn1 = G[1][:].rearrange("p (c f) -> p c f", f=HID)
        g_qt = G[3][:].rearrange("p (c f) -> p c f", f=HID)
        g_qt4 = G[3][:].rearrange("p (c h e) -> p c h e", h=H, e=D)
        g_kt = G[4][:].rearrange("p (c f) -> p c f", f=HID)
        g_kt4 = G[4][:].rearrange("p (c h e) -> p c h e", h=H, e=D)
        g_vt4 = G[5][:].rearrange("p (c h e) -> p c h e", h=H, e=D)
        g_attnF2 = G[6][:].bitcast(BF16).rearrange("p (u k n) -> p u k n",
                                                   u=2, k=8, n=TILE)
        g_h = G[7][:].rearrange("p (c f) -> p c f", f=HID)

        t_x = S.dma("gpsimd", g_x[:, :, :],
                    xd[tok0:tok0 + TILE, :].rearrange("(c p) f -> p c f", p=128),
                    waits=wwait("G0"), sem="d_x")
        use_table(LN_EXP)
        S.mark(f"tile{gti}_start")

        stw = wwait("st")
        for c in range(4):
            sop(lambda eng, c=c: eng.activation(
                sdump[:, :], g_x[:, c], AF.Identity,
                accum_out=st_sum[:, c:c + 1]),
                waits=[t_x] + stw + wwait("rs1"))
            sop(lambda eng, c=c: eng.activation(
                sdump[:, :], g_x[:, c], AF.Square,
                accum_out=st_sq[:, c:c + 1]))
        t_st = S.tok("s")
        t_ln1 = ln_finalize(slice(0, 4), c_lneps, t_st, t_st)
        if DBG:
            vop(lambda eng: eng.memset(AT[:, 0:2, :], 0.0), waits=[t_ln1])
            vop(lambda eng: eng.tensor_copy(AT[:, 0, 0:4], st_sum[:, 0:4]))
            vop(lambda eng: eng.tensor_copy(AT[:, 0, 8:12], st_sq[:, 0:4]))
            vop(lambda eng: eng.tensor_copy(AT[:, 0, 16:20], st_mu[:, 0:4]))
            tdd = vop(lambda eng: eng.tensor_copy(AT[:, 0, 20:24],
                                                  st_r[:, 0:4]))
            dump("st", AT[:, 0:2, :], [tdd], "AT")
        for c in range(4):
            vop(lambda eng, c=c: eng.tensor_scalar(
                g_xn1[:, c], g_x[:, c], st_mu[:, c:c + 1], st_r[:, c:c + 1],
                ALU.subtract, ALU.mult), waits=[t_ln1] + wwait("G1"))
        t_xn1 = S.tok("v")
        war_add("st", t_xn1)
        dump("xn1", G[1][:], [t_xn1], "G1")

        S.mark(f"tile{gti}_ln1")
        g_xn1F2 = G[2][:].bitcast(BF16).rearrange("p (u k n) -> p u k n",
                                                  u=2, k=8, n=TILE)
        for c in range(4):
            for half in range(2):
                b = half
                last = None
                for j in range(4):
                    f = half * 4 + j
                    last = top(lambda eng, c=c, f=f, b=b, j=j: eng.transpose(
                        ps[:, b, j * 128:(j + 1) * 128],
                        g_xn1[:, c, f * 128:(f + 1) * 128], iden[:]),
                        waits=[t_xn1, t_pre] + PS.use(b))
                th = sop(lambda eng, c=c, half=half, b=b: eng.activation(
                    g_xn1F2[:, 0, half * 4:half * 4 + 4,
                            c * 128:(c + 1) * 128],
                    ps[:, b, :].rearrange("p (j n) -> p j n", j=4),
                    AF.Identity),
                    waits=[last] + (wwait("G2") if (c == 0 and half == 0) else []))
                ev = vop(lambda eng, c=c, half=half, b=b: eng.tensor_tensor(
                    g_xn1F2[:, 1, half * 4:half * 4 + 4,
                            c * 128:(c + 1) * 128],
                    ps[:, b, :].rearrange("p (j n) -> p j n", j=4),
                    g_xn1F2[:, 0, half * 4:half * 4 + 4,
                            c * 128:(c + 1) * 128], ALU.subtract),
                    waits=[th])
                PS.done(b, ev)
        t_xn1F = S.tok("v")

        S.mark(f"tile{gti}_xn1F")
        for c in range(4):
            for k in range(8):
                for (ux, uw) in [(0, 0), (1, 0), (0, 1)]:
                    top(lambda eng, c=c, k=k, ux=ux, uw=uw: eng.matmul(
                        ps[:, 2, c * 16:c * 16 + 16],
                        g_xn1F2[:, ux, k, c * 128:(c + 1) * 128],
                        wbtile[:, uw, k, :],
                        start=(k == 0 and ux == 0 and uw == 0), stop=False),
                        waits=[t_xn1F, t_wb] + PS.use(2))
            top(lambda eng, c=c: eng.matmul(
                ps[:, 2, c * 16:c * 16 + 16], ones1[:], bb_row[:],
                start=False, stop=True), waits=[t_ones])
        t_bmm = S.tok("t")
        sop(lambda eng: eng.activation(bt_beta[:, :], ps[:, 2, 0:64],
                                       AF.Exp, scale=c_mone),
            waits=[t_bmm, t_ones] + wwait("bt_beta"))
        t_e = S.tok("s")
        ev = vop(lambda eng: eng.tensor_scalar(bt_beta[:, :], bt_beta[:, :],
                                               1.0, None, ALU.add), waits=[t_e])
        PS.done(2, ev)
        vop(lambda eng: eng.reciprocal(bt_beta[:, :], bt_beta[:, :]))
        vop(lambda eng: eng.tensor_scalar(bt_beta[:, :], bt_beta[:, :],
                                          0.8, 0.999, ALU.max, ALU.min))
        t_beta = S.tok("v")
        dump("beta", bt_beta[:, :], [t_beta], "bt_beta")
        sop(lambda eng: eng.activation(
            bt_ln[:].rearrange("p c h -> p (c h)"), bt_beta[:, :], AF.Ln),
            waits=[t_beta] + wwait("bt_ln"))
        t_lnb = S.tok("s")
        for c in range(4):
            top(lambda eng, c=c: eng.matmul(
                ps[:, 3, c * 16:c * 16 + 16], utri[:], bt_ln[:, c, :],
                start=True, stop=True),
                waits=[t_lnb] + PS.use(3))
        t_cum = S.tok("t")
        war_add("bt_ln", t_cum)
        sop(lambda eng: eng.activation(
            bt_P[:].rearrange("p c h -> p (c h)"), ps[:, 3, 0:64], AF.Exp),
            waits=[t_cum] + wwait("bt_P"))
        t_p = S.tok("s")
        ev = vop(lambda eng: eng.reciprocal(
            bt_Pi[:].rearrange("p c h -> p (c h)"),
            bt_P[:].rearrange("p c h -> p (c h)")), waits=[t_p])
        PS.done(3, ev)
        t_pi = S.tok("v")
        t_zp = S.dma("sync", pcrow[:, :].rearrange("q (c h) -> q c h", h=16),
                     bt_P[127:128, :, :], waits=[t_p] + wwait("pcrow"),
                     sem="d_zp")
        war_add("bt_P", t_zp)
        t_pcmm = top(lambda eng: eng.matmul(
            ps[:, 3, 0:64], ones1[:], pcrow[:, :], start=True, stop=True),
            waits=[t_zp, t_ones] + PS.use(3))
        t_pc = vop(lambda eng: eng.tensor_copy(
            bt_PC[:].rearrange("p c h -> p (c h)"), ps[:, 3, 0:64]),
            waits=[t_pcmm] + wwait("bt_PC"))
        PS.done(3, t_pc)
        war_add("pcrow", t_pcmm)
        dump("pp", bt_P[:].rearrange("p c h -> p (c h)"), [t_p], "bt_P")
        dump("ppi", bt_Pi[:].rearrange("p c h -> p (c h)"), [t_pi])

        S.mark(f"tile{gti}_beta")
        def proj(wi, dst3, extra_waits=()):
            ew = list(extra_waits)
            for half in range(2):
                wts = [None] * 4
                for c in range(4):
                    b = 2 + c
                    for t_ in PS.use(b):
                        ew.append(t_)
                last = None
                for k in range(8):
                    kg = k // 2
                    slot = kg % 2
                    if k % 2 == 0:
                        wts[kg] = S.dma(
                            "sync", wbuf[:, slot, :, :, :],
                            whl_d[wi, half, :, :, kg * 2:(kg + 1) * 2, :],
                            waits=[wbuf_free[slot], S.tok("d_pre")],
                            sem=f"d_w{slot}")
                    for c in range(4):
                        b = 2 + c
                        for (ux, uw) in [(0, 0), (1, 0), (0, 1)]:
                            last = top(
                                lambda eng, c=c, k=k, b=b, slot=slot, ux=ux,
                                uw=uw: eng.matmul(
                                    ps[:, b, :],
                                    g_xn1F2[:, ux, k, c * 128:(c + 1) * 128],
                                    wbuf[:, slot, uw, k % 2, :],
                                    start=(k == 0 and ux == 0 and uw == 0),
                                    stop=(k == 7 and uw == 1)),
                                waits=[wts[kg], t_xn1F] + ew)
                    if k % 2 == 1:
                        wbuf_free[slot] = last
                    ew = []
                for c in range(4):
                    b = 2 + c
                    ev = sop(lambda eng, c=c, b=b, half=half, dst3=dst3:
                             eng.activation(
                                 dst3[:, c, half * 512:(half + 1) * 512],
                                 ps[:, b, :], AF.Identity),
                             waits=[last])
                    PS.done(b, ev)
            return S.tok("s")

        t_q = proj(0, g_qt, wwait("G3"))
        t_k = proj(1, g_kt, wwait("G4"))

        scrA = G[6][:, 0:2048].rearrange("p (c h e) -> p c h e", h=H, e=32)
        scrB = G[6][:, 2048:4096].rearrange("p (c h e) -> p c h e", h=H, e=32)

        def rope(g4, dep):
            lo = g4[:, :, :, 0:32]
            hi = g4[:, :, :, 32:64]
            cosb = costab[:, ti * 4:ti * 4 + 4, :].unsqueeze(2).broadcast_to(
                [128, 4, H, 32])
            sinb = sintab[:, ti * 4:ti * 4 + 4, :].unsqueeze(2).broadcast_to(
                [128, 4, H, 32])
            vop(lambda eng: eng.tensor_tensor(scrA[:], lo, cosb, ALU.mult),
                waits=[dep, t_pre] + wwait("G6"))
            vop(lambda eng: eng.tensor_tensor(scrB[:], lo, sinb, ALU.mult))
            vop(lambda eng: eng.tensor_tensor(lo, hi, sinb, ALU.mult))
            vop(lambda eng: eng.tensor_tensor(hi, hi, cosb, ALU.mult))
            vop(lambda eng: eng.tensor_tensor(lo, scrA[:], lo, ALU.subtract))
            return vop(lambda eng: eng.tensor_tensor(hi, hi, scrB[:], ALU.add))

        def phi(gflat, dep):
            scr = G[6][:]
            vop(lambda eng, g=gflat: eng.tensor_scalar(
                scr[:, :], g, 0.0, None, ALU.min), waits=[dep])
            tv = S.tok("v")
            te = sop(lambda eng: eng.activation(scr[:, :], scr[:, :], AF.Exp),
                     waits=[tv])
            vop(lambda eng, g=gflat: eng.tensor_scalar(
                g, g, 0.0, None, ALU.max))
            return vop(lambda eng, g=gflat: eng.tensor_tensor(
                g, g, scr[:, :], ALU.add), waits=[te])

        S.mark(f"tile{gti}_qk")
        t_qr = rope(g_qt4, t_q)
        t_qp = phi(G[3][:], t_qr)
        t_v = proj(2, G[5][:].rearrange("p (c f) -> p c f", f=HID),
                   wwait("G5"))
        war_add("G2", wbuf_free[0])
        war_add("G2", wbuf_free[1])
        t_kr = rope(g_kt4, t_k)
        t_kp = phi(G[4][:], t_kr)
        t_ktd = vop(lambda eng: eng.tensor_tensor(
            g_kt4[:, :, :, :], g_kt4[:, :, :, :],
            bt_Pi[:].unsqueeze(3).broadcast_to([128, 4, H, D]), ALU.mult),
            waits=[t_kp, t_pi])
        dump("qt", G[3][:], [t_qp], "G3")
        dump("kt", G[4][:], [t_ktd], "G4")
        dump("vt", G[5][:], [t_v], "G5")

        S.mark(f"tile{gti}_ropephi")
        if ti == 0:
            t_s0 = vop(lambda eng: eng.memset(Sdup[:, :, :], 0.0),
                       waits=wwait("Sdup"))
            t_z0 = vop(lambda eng: eng.memset(z0row[:, :], 0.0),
                       waits=wwait("z0row"))
        else:
            t_s0 = war.get("Sdup_ready", [S.tok("v")])[0]
            t_z0 = war.get("z0_ready", [S.tok("v")])[0]
        t_supd = t_s0

        def transF(src3, dstF, dep, b0, warname):
            evs = []
            for half in range(2):
                b = b0 + half
                last = None
                for j in range(4):
                    f = half * 4 + j
                    last = top(lambda eng, f=f, b=b, j=j, src3=src3:
                               eng.transpose(
                                   ps[:, b, j * 128:(j + 1) * 128],
                                   src3[:, f * 128:(f + 1) * 128], iden[:]),
                               waits=[dep] + PS.use(b))
                ev = sop(lambda eng, half=half, b=b, dstF=dstF:
                         eng.activation(
                             dstF[:, half * 4:half * 4 + 4, :],
                             ps[:, b, :].rearrange("p (j n) -> p j n", j=4),
                             AF.Identity),
                         waits=[last] + wwait(warname))
                PS.done(b, ev)
                evs.append(ev)
            return evs[-1]

        def prep_kcz_z(c):
            nonlocal t_z0
            kcz = []
            for hh in range(2):
                bb_ = 6 + hh
                top(lambda eng, c=c, hh=hh, bb_=bb_: eng.matmul(
                    ps[:, bb_, :], utri[:],
                    g_kt[:, c, hh * 512:(hh + 1) * 512],
                    start=True, stop=False),
                    waits=[t_ktd, t_z0, t_ones] + PS.use(bb_))
                m = top(lambda eng, hh=hh, bb_=bb_: eng.matmul(
                    ps[:, bb_, :], ones1[:],
                    z0row[0:1, hh * 512:(hh + 1) * 512],
                    start=False, stop=True))
                kcz.append(m)
            war_add("z0row", kcz[-1])
            zmm = []
            for hh in range(2):
                bb_ = hh
                m = top(lambda eng, c=c, hh=hh, bb_=bb_: eng.matmul(
                    ps[0:1, bb_, :], ones_col[:],
                    g_kt[:, c, hh * 512:(hh + 1) * 512],
                    start=True, stop=True),
                    waits=[t_ktd, t_ones] + PS.use(bb_))
                zmm.append(m)
            vop(lambda eng: eng.tensor_tensor(
                z0row[0:1, :],
                ps[0:1, 0:2, :].rearrange("q b n -> q (b n)"),
                z0row[0:1, :], ALU.add),
                waits=[zmm[0], zmm[1], t_zp] + wwait("z0row"))
            t_z0 = vop(lambda eng, c=c: eng.tensor_tensor(
                z0row[0:1, :].rearrange("q (h e) -> q h e", e=64),
                z0row[0:1, :].rearrange("q (h e) -> q h e", e=64),
                pcrow[0:1, c * 16:(c + 1) * 16].unsqueeze(2)
                .broadcast_to([1, 16, 64]), ALU.mult))
            PS.done(0, t_z0)
            PS.done(1, t_z0)
            return kcz

        def prep_den(c, kcz):
            tden0 = vop(lambda eng, c=c: eng.tensor_tensor(
                rs1[:, 0:512], g_qt[:, c, 0:512], ps[:, 6, :], ALU.mult),
                waits=[kcz[0], t_qp] + wwait("rs1"))
            tden1 = vop(lambda eng, c=c: eng.tensor_tensor(
                rs1[:, 512:1024], g_qt[:, c, 512:1024], ps[:, 7, :],
                ALU.mult), waits=[kcz[1]])
            PS.done(6, tden0)
            PS.done(7, tden1)
            vop(lambda eng: eng.tensor_reduce(
                den_s[:, :],
                rs1[:, 0:1024].rearrange("p (h e) -> p h e", e=64),
                AX.X, ALU.add))
            vop(lambda eng, c=c: eng.tensor_tensor(
                den_s[:, :], den_s[:, :], bt_P[:, c, :], ALU.mult))
            vop(lambda eng: eng.tensor_scalar(
                den_s[:, :], den_s[:, :], float(EPS), None, ALU.add))
            vop(lambda eng: eng.reciprocal(drcp[:, :], den_s[:, :]))
            tq = vop(lambda eng, c=c: eng.tensor_tensor(
                qsc[:, :], bt_P[:, c, :], drcp[:, :], ALU.mult))
            if DBG and c == 0:
                dump("den", den_s[:, :], [tq])
            t_qhat = vop(lambda eng, c=c: eng.tensor_tensor(
                g_qt4[:, c], g_qt4[:, c],
                qsc[:].unsqueeze(2).broadcast_to([128, H, D]), ALU.mult),
                waits=[tq] + wwait("G3"))
            return t_qhat

        def scores_all(t_qF, t_kF):
            toksA, toksB = [], []
            for par in range(2):
                for h in range(par, 16, 2):
                    b0 = 0 if h < 8 else 2
                    hi_ = h % 8
                    bb_ = b0 + hi_ // 4
                    part0 = par * 64
                    extra = ([toksA[-1] if not toksB else toksB[-1]]
                             if (par == 1 and h == 1) else [])
                    m = top(lambda eng, h=h, bb_=bb_, hi_=hi_, part0=part0:
                            eng.matmul(
                                ps[:, bb_,
                                   (hi_ % 4) * 128:(hi_ % 4) * 128 + 128],
                                kF[part0:part0 + 64, h // 2, :],
                                qF[part0:part0 + 64, h // 2, :],
                                start=True, stop=True),
                            waits=[t_qF, t_kF] + PS.use(bb_) + extra)
                    (toksA if h < 8 else toksB).append(m)
            return toksA, toksB

        def mask_evac(b0, waits):
            ev = vop(lambda eng, b0=b0: eng.tensor_tensor(
                AT[:, :, :],
                ps[:, b0:b0 + 2, :].rearrange("p b n -> p (b n)")
                .rearrange("p (h n) -> p h n", n=128),
                utri[:].unsqueeze(1).broadcast_to([128, 8, 128]),
                ALU.mult), waits=waits + wwait("AT"))
            PS.done(b0, ev)
            PS.done(b0 + 1, ev)
            return ev

        def av_inter(c, h0, bk, evt):
            toks = []
            for p_ in range(4):
                he = h0 + 2 * p_
                ho = he + 1
                fc = (h0 // 2) + p_
                sl = slice(p_ * 128, p_ * 128 + 128)
                top(lambda eng, he=he, p_=p_, bk=bk, sl=sl, c=c:
                    eng.matmul(ps[0:64, bk, sl],
                               g_vt4[:, c, he, :], AT[:, 2 * p_, :],
                               start=True, stop=False),
                    waits=[evt, t_v, t_supd]
                    + (PS.use(bk) if p_ == 0 else []))
                t1 = top(lambda eng, he=he, fc=fc, bk=bk, sl=sl:
                         eng.matmul(ps[0:64, bk, sl],
                                    Sdup[0:64, he, :], qF[0:64, fc, :],
                                    start=False, stop=True))
                top(lambda eng, ho=ho, p_=p_, bk=bk, sl=sl, c=c:
                    eng.matmul(ps[64:128, bk, sl],
                               g_vt4[:, c, ho, :], AT[:, 2 * p_ + 1, :],
                               start=True, stop=False,
                               tile_position=(0, 64)))
                t2 = top(lambda eng, ho=ho, fc=fc, bk=bk, sl=sl:
                         eng.matmul(ps[64:128, bk, sl],
                                    Sdup[64:128, ho, :], qF[64:128, fc, :],
                                    start=False, stop=True,
                                    tile_position=(64, 64)))
                toks += [t1, t2]
            th = sop(lambda eng, h0=h0, bk=bk, c=c: eng.activation(
                g_attnF2[:, 0, (h0 // 2):(h0 // 2) + 4,
                         c * 128:(c + 1) * 128],
                ps[:, bk, :].rearrange("p (j n) -> p j n", j=4),
                AF.Identity),
                waits=[toks[-1]])
            ev = vop(lambda eng, h0=h0, bk=bk, c=c: eng.tensor_tensor(
                g_attnF2[:, 1, (h0 // 2):(h0 // 2) + 4,
                         c * 128:(c + 1) * 128],
                ps[:, bk, :].rearrange("p (j n) -> p j n", j=4),
                g_attnF2[:, 0, (h0 // 2):(h0 // 2) + 4,
                         c * 128:(c + 1) * 128], ALU.subtract),
                waits=[th])
            PS.done(bk, ev)
            return ev, toks

        kcz_n = prep_kcz_z(0)
        t_kF = transF(g_kt[:, 0, :], kF[:], t_ktd, 2, "kF")
        t_qhat = prep_den(0, kcz_n)
        t_qF = transF(g_qt[:, 0, :], qF[:], t_qhat, 0, "qF")
        for c in range(4):
            S.mark(f"tile{gti}_chunk{c}")
            mmA, mmB = scores_all(t_qF, t_kF)
            evA = mask_evac(0, [mmA[-1], mmB[-1]])
            if DBG and c == 0:
                dump("at0", AT[:, :, :], [evA], "AT")
            if c < 3:
                kcz_n = prep_kcz_z(c + 1)
            evAVa, mmAVa = av_inter(c, 0, 4, evA)
            war_add("AT", mmAVa[-1])
            evB = mask_evac(2, [mmB[-1]])
            if c < 3:
                t_qhat = prep_den(c + 1, kcz_n)
                t_kF = transF(g_kt[:, c + 1, :], kF[:], t_ktd, 2, "kF")
            evAVb, mmAVb = av_inter(c, 8, 5, evB)
            war_add("AT", mmAVb[-1])
            war_add("qF", mmAVa[-1])
            war_add("qF", mmAVb[-1])
            war_add("kF", mmA[-1])
            war_add("kF", mmB[-1])

            ds_toks = []
            for h in range(16):
                bb_ = 4 + h // 8
                sl = slice((h % 8) * 64, (h % 8) * 64 + 64)
                top(lambda eng, h=h, bb_=bb_, sl=sl, c=c: eng.matmul(
                    ps[0:64, bb_, sl], g_kt4[:, c, h, :], g_vt4[:, c, h, :],
                    start=True, stop=True),
                    waits=[t_ktd, t_v] + (PS.use(bb_) if h % 8 == 0 else []))
                m = top(lambda eng, h=h, bb_=bb_, sl=sl, c=c: eng.matmul(
                    ps[64:128, bb_, sl], g_kt4[:, c, h, :], g_vt4[:, c, h, :],
                    start=True, stop=True, tile_position=(0, 64)))
                ds_toks.append(m)
            vop(lambda eng: eng.tensor_tensor(
                rs1[:, 0:1024],
                Sdup[:].rearrange("p h e -> p (h e)"),
                ps[:, 4:6, :].rearrange("p b n -> p (b n)"), ALU.add),
                waits=[ds_toks[-1], mmAVa[-1], mmAVb[-1]] + wwait("rs1"))
            t_supd = vop(lambda eng, c=c: eng.tensor_tensor(
                Sdup[:, :, :],
                rs1[:, 0:1024].rearrange("p (h e) -> p h e", e=64),
                bt_PC[:, c, :].unsqueeze(2).broadcast_to([128, 16, 64]),
                ALU.mult))
            war_add("rs1", t_supd)
            PS.done(4, t_supd)
            PS.done(5, t_supd)
            if c < 3:
                t_qF = transF(g_qt[:, c + 1, :], qF[:], t_qhat, 0, "qF")

        war["Sdup_ready"] = [t_supd]
        war["z0_ready"] = [t_z0]
        war["Sdup"] = [t_supd]
        war_add("bt_PC", t_supd)
        dump("attnF", G[6][:], [S.tok("v")], "G6")

        S.mark(f"tile{gti}_chunks")
        t_attn_done = S.tok("v")
        ew = wwait("G7")
        h_ev = [None] * 4
        for c in range(4):
            vop(lambda eng, c=c: eng.tensor_tensor(
                g_xn1[:, c], g_xn1[:, c], bo_f[:, :], ALU.add),
                waits=[t_pre])
        t_xn1b = S.tok("v")
        for half in range(2):
            wts = [None] * 4
            pw = []
            for c in range(4):
                pw += PS.use(2 + c)
            last = None
            for k in range(8):
                kg = k // 2
                slot = kg % 2
                if k % 2 == 0:
                    wts[kg] = S.dma(
                        "sync", wbuf[:, slot, :, :, :],
                        whl_d[3, half, :, :, kg * 2:(kg + 1) * 2, :],
                        waits=[wbuf_free[slot], S.tok("d_pre")],
                        sem=f"d_w{slot}")
                for c in range(4):
                    b = 2 + c
                    for (ux, uw) in [(0, 0), (1, 0), (0, 1)]:
                        last = top(
                            lambda eng, c=c, k=k, b=b, slot=slot, ux=ux,
                            uw=uw: eng.matmul(
                                ps[:, b, :],
                                g_attnF2[:, ux, k, c * 128:(c + 1) * 128],
                                wbuf[:, slot, uw, k % 2, :],
                                start=(k == 0 and ux == 0 and uw == 0),
                                stop=(k == 7 and uw == 1)),
                            waits=[wts[kg], t_attn_done] + pw)
                if k % 2 == 1:
                    wbuf_free[slot] = last
                pw = []
            for c in range(4):
                b = 2 + c
                ev = vop(lambda eng, c=c, b=b, half=half:
                         eng.scalar_tensor_tensor(
                             g_h[:, c, half * 512:(half + 1) * 512],
                             ps[:, b, :], 1.0,
                             g_xn1[:, c, half * 512:(half + 1) * 512],
                             ALU.mult, ALU.add),
                         waits=[last, t_xn1b] + (ew if (c == 0 and half == 0)
                                                 else []))
                PS.done(b, ev)
                if half == 1:
                    h_ev[c] = ev
        war_add("G6", wbuf_free[0])
        war_add("G6", wbuf_free[1])
        t_h = S.tok("v")
        dump("h", G[7][:], [t_h], "G7")

        S.mark(f"tile{gti}_wo")
        stw = wwait("st")
        g_ab = G[3][:].rearrange("p (c f) -> p c f", f=HID)
        g_x1 = G[4][:].rearrange("p (c f) -> p c f", f=HID)
        g_xn2 = G[5][:].rearrange("p (c f) -> p c f", f=HID)
        g_xn2F2 = G[2][:].bitcast(BF16).rearrange("p (u k n) -> p u k n",
                                                  u=2, k=8, n=TILE)
        rsw = wwait("rs1")
        g3w = wwait("G3")
        g4w = wwait("G4")
        g5w = wwait("G5")
        g2w = wwait("G2")
        for c in range(4):
            cs = slice(c, c + 1)
            sop(lambda eng, c=c: eng.activation(
                sdump[:, :], g_h[:, c], AF.Square,
                accum_out=st_sq[:, c:c + 1]),
                waits=[h_ev[c]] + stw + rsw)
            t_rms = rms_finalize(cs, c_rmseps, S.tok("s"))
            t_ab = vop(lambda eng, c=c: eng.scalar_tensor_tensor(
                g_ab[:, c], g_h[:, c], st_r[:, c:c + 1], scale_f[:, :],
                ALU.mult, ALU.mult, accum_out=st_sum[:, c:c + 1]),
                waits=[t_rms, t_pre] + (g3w if c == 0 else []))
            sop(lambda eng, c=c: eng.activation(
                sdump[:, :], g_ab[:, c], AF.Square,
                accum_out=st_sq[:, c:c + 1]), waits=[t_ab])
            t_ln2 = ln_finalize(cs, c_lneps, S.tok("s"), t_ab)
            vop(lambda eng, c=c: eng.tensor_scalar(
                rs1[:, 0:1024], g_ab[:, c], st_mu[:, c:c + 1],
                st_r[:, c:c + 1], ALU.subtract, ALU.mult),
                waits=[t_ln2])
            t_x1 = vop(lambda eng, c=c: eng.scalar_tensor_tensor(
                g_x1[:, c], rs1[:, 0:1024], 1.0, g_x[:, c],
                ALU.mult, ALU.add, accum_out=st_sum[:, c:c + 1]),
                waits=(g4w if c == 0 else []))
            sop(lambda eng, c=c: eng.activation(
                sdump[:, :], g_x1[:, c], AF.Square,
                accum_out=st_sq[:, c:c + 1]), waits=[t_x1])
            t_ln3 = ln_finalize(cs, c_lneps, S.tok("s"), t_x1)
            t_xn2 = vop(lambda eng, c=c: eng.tensor_scalar(
                g_xn2[:, c], g_x1[:, c], st_mu[:, c:c + 1], st_r[:, c:c + 1],
                ALU.subtract, ALU.mult),
                waits=[t_ln3] + (g5w if c == 0 else []))
            for half in range(2):
                b = half
                last = None
                for j in range(4):
                    f = half * 4 + j
                    last = top(lambda eng, c=c, f=f, b=b, j=j: eng.transpose(
                        ps[:, b, j * 128:(j + 1) * 128],
                        g_xn2[:, c, f * 128:(f + 1) * 128], iden[:]),
                        waits=[t_xn2] + PS.use(b))
                th = sop(lambda eng, c=c, half=half, b=b: eng.activation(
                    g_xn2F2[:, 0, half * 4:half * 4 + 4,
                            c * 128:(c + 1) * 128],
                    ps[:, b, :].rearrange("p (j n) -> p j n", j=4),
                    AF.Identity),
                    waits=[last] + (g2w if (c == 0 and half == 0) else []))
                ev = vop(lambda eng, c=c, half=half, b=b: eng.tensor_tensor(
                    g_xn2F2[:, 1, half * 4:half * 4 + 4,
                            c * 128:(c + 1) * 128],
                    ps[:, b, :].rearrange("p (j n) -> p j n", j=4),
                    g_xn2F2[:, 0, half * 4:half * 4 + 4,
                            c * 128:(c + 1) * 128], ALU.subtract),
                    waits=[th])
                PS.done(b, ev)
        war_add("rs1", S.tok("v"))
        war_add("st", S.tok("v"))
        t_xn2F = S.tok("v")
        dump("ab", G[3][:], [S.tok("v")], "G3")
        dump("x1", G[4][:], [S.tok("v")], "G4")
        dump("xn2", G[5][:], [S.tok("v")], "G5")

        S.mark(f"tile{gti}_xn2F")
        use_table(SILU)
        hm_toks = [None] * 24

        def hm_view(m, u):
            gi = 1 + (0 if m < 8 else (5 if m < 16 else 6))
            return G[gi][:].bitcast(BF16).rearrange(
                "p (v j n) -> p v j n", v=2, n=TILE)[:, u, m % 8, :]

        for m in range(24):
            slot = m % 2
            bg = (m % 2) * 2
            bu = bg + 1
            twg = S.dma("sync", wgbuf[:, slot, :, :], wgb_d[m, :, :, :],
                        waits=[wgbuf_free[slot], S.tok("d_pre")],
                        sem=f"d_g{slot}")
            twu = S.dma("sync", wubuf[:, slot, :, :], wub_d[m, :, :, :],
                        waits=[wubuf_free[slot], S.tok("d_pre")],
                        sem=f"d_u{slot}")
            lastg = lastu = None
            for k in range(8):
                for u in range(2):
                    lastg = top(lambda eng, k=k, u=u, bg=bg, slot=slot:
                                eng.matmul(
                                    ps[:, bg, :], wgbuf[:, slot, k, :],
                                    g_xn2F2[:, u, k, :],
                                    start=(k == 0 and u == 0),
                                    stop=(k == 7 and u == 1)),
                                waits=[twg, t_xn2F] + PS.use(bg))
            for k in range(8):
                for u in range(2):
                    lastu = top(lambda eng, k=k, u=u, bu=bu, slot=slot:
                                eng.matmul(
                                    ps[:, bu, :], wubuf[:, slot, k, :],
                                    g_xn2F2[:, u, k, :],
                                    start=(k == 0 and u == 0),
                                    stop=(k == 7 and u == 1)),
                                waits=[twu] + PS.use(bu))
            wgbuf_free[slot] = lastg
            wubuf_free[slot] = lastu
            tsil = sop(lambda eng, bg=bg: eng.activation(
                sdump[:, 0:512], ps[:, bg, :], AF.Silu, scale=c_s127),
                waits=[lastg, t_ones] + wwait("rs1"))
            PS.done(bg, tsil)
            warkey = "G" + str(1 + (0 if m < 8 else (5 if m < 16 else 6)))
            vop(lambda eng, bu=bu: eng.tensor_tensor(
                rs1[:, 512:1024], sdump[:, 0:512], ps[:, bu, :], ALU.mult),
                waits=[tsil, lastu])
            vop(lambda eng, m=m: eng.tensor_copy(
                hm_view(m, 0), rs1[:, 512:1024]),
                waits=(wwait(warkey) if m % 8 == 0 else []))
            tm = vop(lambda eng, m=m: eng.tensor_tensor(
                hm_view(m, 1), rs1[:, 512:1024], hm_view(m, 0), ALU.subtract))
            war_add("rs1", tm)
            PS.done(bu, tm)
            hm_toks[m] = tm

        S.mark(f"tile{gti}_gateup")
        dn_last = {}
        for mk in range(24):
            for half in range(2):
                slot = (mk * 2 + half) % 2
                twd = S.dma("sync", wdbuf[:, slot, :],
                            wdb_d[mk, :, half * 512:(half + 1) * 512],
                            waits=[wdbuf_free[slot], S.tok("d_pre")],
                            sem=f"d_d{slot}")
                last = None
                for c in range(4):
                    b = c * 2 + half
                    for u in range(2):
                        last = top(lambda eng, mk=mk, c=c, u=u, b=b, slot=slot:
                                   eng.matmul(
                                       ps[:, b, :],
                                       hm_view(mk, u)[:, c * 128:(c + 1) * 128],
                                       wdbuf[:, slot, :],
                                       start=(mk == 0 and u == 0),
                                       stop=(mk == 23 and u == 1)),
                                   waits=([twd, hm_toks[mk]]
                                          + (PS.use(b) if mk == 0 else [])))
                    dn_last[(c, half)] = last
                wdbuf_free[slot] = last
        g_f = G[0][:].rearrange("p (c f) -> p c f", f=HID)
        for c in range(4):
            for half in range(2):
                b = c * 2 + half
                ev = vop(lambda eng, c=c, half=half, b=b: eng.tensor_scalar(
                    g_f[:, c, half * 512:(half + 1) * 512], ps[:, b, :],
                    0.0, 0.0, ALU.add, ALU.add,
                    accum_out=st_sum[:, c:c + 1] if half == 0 else
                    st_t2[:, c:c + 1]),
                    waits=[dn_last[(c, half)]]
                    + (wwait("G0") if (c == 0 and half == 0) else []))
                PS.done(b, ev)
        tvf = vop(lambda eng: eng.tensor_tensor(
            st_sum[:, 0:4], st_sum[:, 0:4], st_t2[:, 0:4], ALU.add))
        for c in range(4):
            sop(lambda eng, c=c: eng.activation(
                sdump[:, :], g_f[:, c], AF.Square,
                accum_out=st_sq[:, c:c + 1]), waits=[tvf] + wwait("rs1"))
        use_table(LN_EXP)
        t_ln4 = ln_finalize(slice(0, 4), c_lneps, S.tok("s"), tvf)
        dump("fout", G[0][:], [S.tok("v"), tvf], "G0")

        g_out = G[3][:].rearrange("p (c f) -> p c f", f=HID)
        for c in range(4):
            vop(lambda eng, c=c: eng.tensor_scalar(
                rs1[:, 0:1024], g_f[:, c], st_mu[:, c:c + 1], st_r[:, c:c + 1],
                ALU.subtract, ALU.mult), waits=[t_ln4] + wwait("rs1"))
            vop(lambda eng, c=c: eng.tensor_tensor(
                g_out[:, c], rs1[:, 0:1024], g_x1[:, c], ALU.add),
                waits=(wwait("G3") if c == 0 else []))
        t_out = S.tok("v")
        war_add("rs1", t_out)
        war_add("st", t_out)
        t_odma = S.op("gpsimd",
                      (lambda o, i: (lambda eng: eng.dma_start(out=o, in_=i)))(
                          outd[tok0:tok0 + TILE, :]
                          .rearrange("(c p) f -> p c f", p=128),
                          g_out[:, :, :]),
                      waits=[t_out], inc="d_out", inc_n=16)
        war_add("G3", t_odma)
        war_add("G0", S.tok("s"))
        war_add("G0", t_out)
        war_add("G1", wgbuf_free[0])
        war_add("G2", max(wgbuf_free, key=lambda t: t[1]))
        war_add("G2", max(wubuf_free, key=lambda t: t[1]))
        war_add("G4", t_out)
        war_add("G5", t_xn2F)
        war_add("G6", wdbuf_free[0])
        war_add("G6", wdbuf_free[1])
        war_add("G7", wdbuf_free[0])
        war_add("G7", wdbuf_free[1])
        war_add("G1", wdbuf_free[0])
        war_add("G1", wdbuf_free[1])

    if max_items is not None:
        S.items = S.items[:max_items]
    else:
        fin = [("d_out", S.ctr.get("d_out", 0))]
        if dbg:
            fin.append(("d_dbg", S.ctr.get("d_dbg", 0)))
        S.op("gpsimd", lambda eng: None, waits=fin)

    sem_names = ["d_pre", "d_ff0", "d_x", "d_zp", "d_w0", "d_w1", "d_g0",
                 "d_g1", "d_u0", "d_u1", "d_d0", "d_d1", "d_out", "d_dbg",
                 "t", "v", "s", "g"]
    with ctx:
        sems = {}
        with ExitStack() as sctx:
            for nm in sem_names:
                sems[nm] = sctx.enter_context(nc.semaphore(nm))
            with nc.Block() as block:
                @block.sync
                def _(eng):
                    S.replay("sync", eng, sems)

                @block.tensor
                def _(eng):
                    S.replay("tensor", eng, sems)

                @block.vector
                def _(eng):
                    S.replay("vector", eng, sems)

                @block.scalar
                def _(eng):
                    S.replay("scalar", eng, sems)

                @block.gpsimd
                def _(eng):
                    S.replay("gpsimd", eng, sems)

    nc._sched_marks = S.marks
    return nc


def prep_weights(Wq, Wk, Wv, Wb, bb, Wo, bo, scale, w_gate, w_up, w_down):
    def proj_layout(W):
        WT = np.ascontiguousarray(W.T)
        return np.ascontiguousarray(
            WT.reshape(8, 128, 2, 512).transpose(2, 1, 0, 3))

    out = {
        "wq": proj_layout(Wq), "wk": proj_layout(Wk), "wv": proj_layout(Wv),
        "wo": proj_layout(Wo),
        "wb": np.ascontiguousarray(Wb.T.reshape(8, 128, 16).transpose(1, 0, 2)),
        "bb": np.ascontiguousarray(bb.reshape(1, 16)),
        "bo": np.ascontiguousarray(bo.reshape(1, HID)),
        "bo_full": np.ascontiguousarray(
            np.broadcast_to(bo.reshape(1, HID), (128, HID))),
        "lnscale": np.ascontiguousarray(
            np.broadcast_to(scale.reshape(1, HID), (128, HID))),
    }
    wg8 = w_gate.astype(np.int8)
    wu8 = w_up.astype(np.int8)
    wd8 = w_down.astype(np.int8)
    assert np.array_equal(wg8.astype(np.int32), w_gate)
    out["wg"] = np.ascontiguousarray(
        wg8.T.reshape(8, 128, 24, 128).transpose(2, 1, 0, 3))
    out["wu"] = np.ascontiguousarray(
        wu8.T.reshape(8, 128, 24, 128).transpose(2, 1, 0, 3))
    out["wd"] = np.ascontiguousarray(wd8.T.reshape(24, 128, HID))
    out["utri"] = np.triu(np.ones((128, 128), np.float32))
    out["iden"] = np.eye(128, dtype=np.float32)
    inv = 1.0 / 10000.0 ** (np.arange(0, D, 2, dtype=np.float64) / D)
    t = np.arange(2048, dtype=np.float64)
    fr = t[:, None] * inv[None, :]
    out["costab"] = np.ascontiguousarray(
        np.cos(fr).astype(np.float32).reshape(16, 128, 32).transpose(1, 0, 2))
    out["sintab"] = np.ascontiguousarray(
        np.sin(fr).astype(np.float32).reshape(16, 128, 32).transpose(1, 0, 2))
    return out


_NC_CACHE = {}


def kernel(L0_attn_x, L0_attn_Wq, L0_attn_Wk, L0_attn_Wv, L0_attn_Wb,
           L0_attn_bb, L0_attn_Wo, L0_attn_bo, L0_attn_scale,
           L0_ffn_w_gate, L0_ffn_w_up, L0_ffn_w_down):
    from concourse.bass_utils import run_bass_kernel_spmd

    x = np.asarray(L0_attn_x, dtype=np.float32)
    B, T, _ = x.shape
    n_cores = 8
    per = B // n_cores
    if "nc" not in _NC_CACHE:
        _NC_CACHE["nc"] = build_core_program(n_seq=per, tiles_per_seq=T // TILE,
                                             dbg=False)
    nc = _NC_CACHE["nc"]
    wmaps = prep_weights(np.asarray(L0_attn_Wq), np.asarray(L0_attn_Wk),
                         np.asarray(L0_attn_Wv), np.asarray(L0_attn_Wb),
                         np.asarray(L0_attn_bb), np.asarray(L0_attn_Wo),
                         np.asarray(L0_attn_bo), np.asarray(L0_attn_scale),
                         np.asarray(L0_ffn_w_gate), np.asarray(L0_ffn_w_up),
                         np.asarray(L0_ffn_w_down))
    in_maps = []
    for i in range(n_cores):
        m = dict(wmaps)
        m["x"] = np.ascontiguousarray(
            x[i * per:(i + 1) * per].reshape(per * T, HID))
        in_maps.append(m)
    res = run_bass_kernel_spmd(nc, in_maps, core_ids=list(range(n_cores)))
    out = np.concatenate([r["out"].reshape(per, T, HID) for r in res.results],
                         axis=0)
    return out


# version 12
# speedup vs baseline: 1.0763x; 1.0763x over previous
import numpy as np
from contextlib import ExitStack

import concourse.bass as bass
import concourse.mybir as mybir

F32 = mybir.dt.float32
BF16 = mybir.dt.bfloat16
I8 = mybir.dt.int8
AF = mybir.ActivationFunctionType
ALU = mybir.AluOpType
AX = mybir.AxisListType

H, D = 16, 64
HID = 1024
FF = 3072
TILE = 512
EPS = 1e-6
LNEPS = 1e-5
SCALE127 = float(np.float32(1.0 / 127.0))


class Sched:
    ENGINES = ("sync", "tensor", "vector", "scalar", "gpsimd")

    def __init__(self):
        self.items = []
        self.ctr = {}
        self.last_wait = {}
        self.marks = []

    def mark(self, name):
        self.marks.append((name, len(self.items)))

    def tok(self, sem):
        return (sem, self.ctr.get(sem, 0))

    def op(self, engine, fn, waits=(), inc=None, inc_n=1):
        assert engine in self.ENGINES
        w = []
        for t in waits:
            if t is None:
                continue
            (sem, val) = t
            if val <= 0:
                continue
            assert val <= self.ctr.get(sem, 0), (sem, val, self.ctr.get(sem, 0))
            key = (engine, sem)
            if self.last_wait.get(key, 0) >= val:
                continue
            self.last_wait[key] = val
            w.append((sem, val))
        tok = None
        incs = []
        if inc is not None:
            self.ctr[inc] = self.ctr.get(inc, 0) + inc_n
            tok = (inc, self.ctr[inc])
            incs.append((inc, inc_n))
        self.items.append((engine, w, fn, incs))
        return tok

    def dma(self, engine, out, in_, waits=(), sem=None):
        sem = sem or ("d_" + engine)
        return self.op(
            engine,
            (lambda o, i: (lambda eng: eng.dma_start(out=o, in_=i)))(out, in_),
            waits=waits, inc=sem, inc_n=16)

    def replay(self, engine_name, eng, sems):
        for (e, waits, fn, incs) in self.items:
            if e != engine_name:
                continue
            for (sem, val) in waits:
                eng.wait_ge(sems[sem], val)
            instr = fn(eng)
            for (sem, n) in incs:
                assert instr is not None
                instr.then_inc(sems[sem], n)


class Psum:
    def __init__(self):
        self.free_tok = [[] for _ in range(8)]

    def use(self, b):
        return list(self.free_tok[b])

    def done(self, b, toks):
        if not isinstance(toks, list):
            toks = [toks]
        self.free_tok[b] = toks


def build_core_program(n_seq=4, tiles_per_seq=4, dbg=False, max_items=None):
    T_SEQ = tiles_per_seq * TILE
    NTOK = n_seq * T_SEQ
    nc = bass.Bass("TRN2", target_bir_lowering=False)
    S = Sched()
    PS = Psum()
    war = {}

    def wwait(name):
        return war.pop(name, [])

    def war_add(name, tok):
        war.setdefault(name, []).append(tok)

    xd = nc.dram_tensor("x", [NTOK, HID], F32, kind="ExternalInput").ap()
    outd = nc.dram_tensor("out", [NTOK, HID], F32, kind="ExternalOutput").ap()
    wq_d = nc.dram_tensor("wq", [2, 128, 8, 512], F32, kind="ExternalInput").ap()
    wk_d = nc.dram_tensor("wk", [2, 128, 8, 512], F32, kind="ExternalInput").ap()
    wv_d = nc.dram_tensor("wv", [2, 128, 8, 512], F32, kind="ExternalInput").ap()
    wo_d = nc.dram_tensor("wo", [2, 128, 8, 512], F32, kind="ExternalInput").ap()
    wb_d = nc.dram_tensor("wb", [128, 8, 16], F32, kind="ExternalInput").ap()
    bb_d = nc.dram_tensor("bb", [1, 16], F32, kind="ExternalInput").ap()
    bo_d = nc.dram_tensor("bo", [1, HID], F32, kind="ExternalInput").ap()
    sc_d = nc.dram_tensor("lnscale", [128, HID], F32, kind="ExternalInput").ap()
    wg_d = nc.dram_tensor("wg", [24, 128, 8, 128], I8, kind="ExternalInput").ap()
    wu_d = nc.dram_tensor("wu", [24, 128, 8, 128], I8, kind="ExternalInput").ap()
    wd_d = nc.dram_tensor("wd", [24, 128, HID], I8, kind="ExternalInput").ap()
    u_d = nc.dram_tensor("utri", [128, 128], F32, kind="ExternalInput").ap()
    bof_d = nc.dram_tensor("bo_full", [128, HID], F32, kind="ExternalInput").ap()
    whl_d = nc.dram_tensor("whl", [4, 2, 128, 2, 8, 512], BF16).ap()
    wgb_d = nc.dram_tensor("wgb", [24, 128, 8, 128], BF16).ap()
    wub_d = nc.dram_tensor("wub", [24, 128, 8, 128], BF16).ap()
    wdb_d = nc.dram_tensor("wdb", [24, 128, HID], BF16).ap()
    id_d = nc.dram_tensor("iden", [128, 128], F32, kind="ExternalInput").ap()
    cos_d = nc.dram_tensor("costab", [128, 16, 32], F32, kind="ExternalInput").ap()
    sin_d = nc.dram_tensor("sintab", [128, 16, 32], F32, kind="ExternalInput").ap()

    dbg_t = {}
    if dbg:
        for nm in ["xn1", "qt", "kt", "vt", "beta", "pp", "ppi", "den",
                   "at0", "attnF", "h", "ab", "x1", "xn2", "fout", "st"]:
            dbg_t[nm] = nc.dram_tensor("dbg_" + nm, [128, 4096], F32,
                                       kind="ExternalOutput").ap()

    ctx = ExitStack()

    def sbt(name, shape, dtype=F32):
        return ctx.enter_context(nc.sbuf_tensor(name, shape, dtype))

    G = [sbt(f"g{i}", [128, 4096]) for i in range(9)]
    wbuf = sbt("wbuf", [128, 2, 2, 2, 512], BF16)
    wgbuf = sbt("wgbuf", [128, 2, 8, 128], BF16)
    wubuf = sbt("wubuf", [128, 2, 8, 128], BF16)
    wdbuf = sbt("wdbuf", [128, 4, 512], BF16)
    wbtile = sbt("wbtile", [128, 2, 8, 16], BF16)
    bo_f = sbt("bo_f", [128, HID])
    qF = sbt("qF", [128, 8, 128])
    kF = sbt("kF", [128, 8, 128])
    AT = sbt("AT", [128, 8, 128])
    Sdup = sbt("Sdup", [128, 8, 64])
    rs1 = sbt("rs1", [128, 1024])
    sdump = rs1
    z0row = sbt("z0row", [1, HID])
    bb_row = sbt("bb_row", [1, 16])
    ones1 = sbt("ones1", [1, 128])
    scale_f = sbt("scale_f", [128, HID])
    utri = sbt("utri_s", [128, 128])
    iden = sbt("iden_s", [128, 128])
    costab = sbt("costab_s", [128, 16, 32])
    sintab = sbt("sintab_s", [128, 16, 32])
    bt_beta = sbt("bt_beta", [128, 64])
    bt_ln = sbt("bt_ln", [128, 4, 16])
    bt_P = sbt("bt_P", [128, 4, 16])
    bt_Pi = sbt("bt_Pi", [128, 4, 16])
    bt_PC = sbt("bt_PC", [128, 4, 16])
    pcrow = sbt("pcrow", [1, 64])
    ones_col = sbt("ones_col", [128, 1])
    cpool = sbt("cpool", [128, 8])
    den_s = sbt("den_s", [128, 16])
    drcp = sbt("drcp", [128, 16])
    qsc = sbt("qsc", [128, 16])
    st_sum = sbt("st_sum", [128, 8])
    st_sq = sbt("st_sq", [128, 8])
    st_mu = sbt("st_mu", [128, 8])
    st_t1 = sbt("st_t1", [128, 8])
    st_t2 = sbt("st_t2", [128, 8])
    st_r = sbt("st_r", [128, 8])

    ps = ctx.enter_context(nc.psum_tensor("ps", [128, 8, 512], F32))

    def vop(fn, waits=(), inc="v"):
        return S.op("vector", fn, waits=list(waits) + [S.tok("v")], inc=inc)

    def sop(fn, waits=(), inc="s"):
        return S.op("scalar", fn, waits=list(waits) + [S.tok("s")], inc=inc)

    def top(fn, waits=(), inc="t"):
        return S.op("tensor", fn, waits=waits, inc=inc)

    for (dst, src) in [(bb_row[:], bb_d[:, :]),
                       (bo_f[:], bof_d[:, :]), (scale_f[:], sc_d[:, :]),
                       (utri[:], u_d[:, :]), (iden[:], id_d[:, :]),
                       (costab[:], cos_d[:, :, :]), (sintab[:], sin_d[:, :, :])]:
        S.dma("sync", dst, src, sem="d_pre")
    t_pre = S.tok("d_pre")
    vop(lambda eng: eng.memset(ones1[:], 1.0))
    vop(lambda eng: eng.memset(ones_col[:], 1.0))
    for i, val in enumerate([LNEPS, EPS, -0.5, -1.0, 1.0 / HID, SCALE127]):
        vop(lambda eng, i=i, val=val: eng.memset(cpool[:, i:i + 1], float(val)))
    t_ones = S.tok("v")
    c_lneps = cpool[:, 0:1]
    c_rmseps = cpool[:, 1:2]
    c_mhalf = cpool[:, 2:3]
    c_mone = cpool[:, 3:4]
    c_invh = cpool[:, 4:5]
    c_s127 = cpool[:, 5:6]

    g0f = G[0][:]
    g1b = G[1][:].bitcast(BF16)
    t_ph0 = None
    for wi, wdram in enumerate([wq_d, wk_d, wv_d, wo_d]):
        for half in range(2):
            tin = S.dma("sync", g0f,
                        wdram[half, :, :, :].rearrange("p k n -> p (k n)"),
                        waits=[t_ph0] if t_ph0 else [t_pre], sem="d_pre")
            vop(lambda eng: eng.tensor_copy(g1b[:, 0:4096], g0f), waits=[tin])
            tv = vop(lambda eng: eng.tensor_tensor(
                g1b[:, 4096:8192], g0f, g1b[:, 0:4096], ALU.subtract))
            t_ph0 = S.dma("sync",
                          whl_d[wi, half, :, :, :, :]
                          .rearrange("p u k n -> p (u k n)"),
                          g1b[:, :], waits=[tv], sem="d_pre")
    g0b = G[0][:].bitcast(BF16)
    for (src_d, dst_d, four) in [(wg_d, wgb_d, True), (wu_d, wub_d, True),
                                 (wd_d, wdb_d, False)]:
        for m0 in range(0, 24, 8):
            if four:
                sap = src_d[m0:m0 + 8].rearrange("m p k n -> p m (k n)")
                dap = dst_d[m0:m0 + 8].rearrange("m p k n -> p m (k n)")
            else:
                sap = src_d[m0:m0 + 8].rearrange("m p n -> p m n")
                dap = dst_d[m0:m0 + 8].rearrange("m p n -> p m n")
            tin = S.dma("gpsimd", g0b[:, :], sap, waits=[t_ph0], sem="d_ff0")
            t_ph0 = S.dma("gpsimd", dap, g0b[:, :], waits=[tin], sem="d_ff0")
    twb = S.dma("sync", rs1[:, 0:128],
                wb_d[:, :, :].rearrange("p k h -> p (k h)"),
                waits=[t_ph0], sem="d_pre")
    vop(lambda eng: eng.tensor_copy(
        wbtile[:, 0, :, :].rearrange("p k h -> p (k h)"), rs1[:, 0:128]),
        waits=[twb])
    t_wb = vop(lambda eng: eng.tensor_tensor(
        wbtile[:, 1, :, :].rearrange("p k h -> p (k h)"), rs1[:, 0:128],
        wbtile[:, 0, :, :].rearrange("p k h -> p (k h)"), ALU.subtract))
    war_add("G0", S.tok("d_pre"))
    war_add("G1", S.tok("d_pre"))
    war_add("rs1", t_wb)

    cur_table = [None]

    def use_table(func):
        return

    cur_dbg = [False]

    def dump(name, src_ap, waits, warname=None):
        if not (dbg and cur_dbg[0]):
            return
        n = src_ap.shape[-1] if src_ap.ndim == 2 else int(
            np.prod(src_ap.shape[1:]))
        flat = src_ap if src_ap.ndim == 2 else src_ap.rearrange(
            "p " + " ".join(f"a{i}" for i in range(src_ap.ndim - 1))
            + " -> p (" + " ".join(f"a{i}" for i in range(src_ap.ndim - 1)) + ")")
        tok = S.op("gpsimd",
                   (lambda d, s_: (lambda eng: eng.dma_start(out=d, in_=s_)))(
                       dbg_t[name][:, 0:n], flat),
                   waits=list(waits) + [S.tok("d_dbg")], inc="d_dbg",
                   inc_n=16)
        if warname:
            war_add(warname, tok)
        return tok

    wbuf_free = [None, None]
    wgbuf_free = [None, None]
    wubuf_free = [None, None]
    wdbuf_free = [None, None, None, None]

    def ln_finalize(sl, eps_ap, t_sq_tok, t_sum_tok, src=None):
        src = src if src is not None else sl
        vop(lambda eng: eng.tensor_scalar(
            st_mu[:, sl], st_sum[:, src], 1.0 / HID, None, ALU.mult),
            waits=[t_sum_tok])
        vop(lambda eng: eng.tensor_scalar(
            st_t1[:, sl], st_sq[:, src], 1.0 / HID, None, ALU.mult),
            waits=[t_sq_tok])
        vop(lambda eng: eng.tensor_tensor(
            st_t2[:, sl], st_mu[:, sl], st_mu[:, sl], ALU.mult))
        tv = vop(lambda eng: eng.tensor_tensor(
            st_t1[:, sl], st_t1[:, sl], st_t2[:, sl], ALU.subtract))
        sop(lambda eng: eng.activation(st_t1[:, sl], st_t1[:, sl],
                                       AF.Ln, bias=eps_ap),
            waits=[tv, t_ones])
        t = sop(lambda eng: eng.activation(st_r[:, sl], st_t1[:, sl],
                                           AF.Exp, scale=c_mhalf))
        return t

    def rms_finalize(sl, eps_ap, t_sq_tok):
        sop(lambda eng: eng.activation(st_t1[:, sl], st_sq[:, sl],
                                       AF.Ln, bias=eps_ap, scale=c_invh),
            waits=[t_sq_tok, t_ones])
        t = sop(lambda eng: eng.activation(st_r[:, sl], st_t1[:, sl],
                                           AF.Exp, scale=c_mhalf))
        return t

    LN_EXP = AF.Exp
    SILU = AF.Silu

    seq_tiles = [(sq, ti) for sq in range(n_seq) for ti in range(tiles_per_seq)]

    prefetched = [None]

    def emit_ln1(g2, waits_extra):
        tok0n = (g2 // tiles_per_seq) * T_SEQ + (g2 % tiles_per_seq) * TILE
        gx = G[8][:].rearrange("p (c f) -> p c f", f=HID)
        gxn1 = G[0][:].rearrange("p (c f) -> p c f", f=HID)
        px = S.dma("gpsimd", gx[:, :, :],
                   xd[tok0n:tok0n + TILE, :]
                   .rearrange("(c p) f -> p c f", p=128),
                   waits=waits_extra, sem="d_x")
        for c in range(4):
            sop(lambda eng, c=c, gx=gx: eng.activation(
                sdump[:, :], gx[:, c], AF.Identity,
                accum_out=st_sum[:, 4 + c:5 + c]),
                waits=[px] + (wwait("st47") if c == 0 else [])
                + (wwait("rs1") if c == 0 else []))
            sop(lambda eng, c=c, gx=gx: eng.activation(
                sdump[:, :], gx[:, c], AF.Square,
                accum_out=st_sq[:, 4 + c:5 + c]))
        t_st = S.tok("s")
        t_ln1 = ln_finalize(slice(4, 8), c_lneps, t_st, t_st)
        war_add("st47", t_ln1)
        tnorm = None
        for c in range(4):
            tnorm = vop(lambda eng, c=c, gx=gx, gxn1=gxn1: eng.tensor_scalar(
                gxn1[:, c], gx[:, c], st_mu[:, 4 + c:5 + c],
                st_r[:, 4 + c:5 + c], ALU.subtract, ALU.mult),
                waits=[t_ln1] + wwait("G0"))
        prefetched[0] = (px, tnorm)

    front = [None]

    def emit_front(g2):
        old_dbg = cur_dbg[0]
        cur_dbg[0] = g2 == 0
        t_xn1 = prefetched[0][1]
        gxn1 = G[0][:].rearrange("p (c f) -> p c f", f=HID)
        gF2 = G[2][:].bitcast(BF16).rearrange("p (u k n) -> p u k n",
                                              u=2, k=8, n=TILE)
        for c in range(4):
            for half in range(2):
                b = half
                last = None
                for j in range(4):
                    f = half * 4 + j
                    last = top(lambda eng, c=c, f=f, b=b, j=j: eng.transpose(
                        ps[:, b, j * 128:(j + 1) * 128],
                        gxn1[:, c, f * 128:(f + 1) * 128], iden[:]),
                        waits=[t_xn1, t_pre] + PS.use(b))
                th = sop(lambda eng, c=c, half=half, b=b: eng.activation(
                    gF2[:, 0, half * 4:half * 4 + 4, c * 128:(c + 1) * 128],
                    ps[:, b, :].rearrange("p (j n) -> p j n", j=4),
                    AF.Identity),
                    waits=[last] + (wwait("G2") if (c == 0 and half == 0)
                                    else []))
                ev = vop(lambda eng, c=c, half=half, b=b: eng.tensor_tensor(
                    gF2[:, 1, half * 4:half * 4 + 4, c * 128:(c + 1) * 128],
                    ps[:, b, :].rearrange("p (j n) -> p j n", j=4),
                    gF2[:, 0, half * 4:half * 4 + 4, c * 128:(c + 1) * 128],
                    ALU.subtract),
                    waits=[th])
                PS.done(b, ev)
        t_xn1F = S.tok("v")
        war_add("G0", S.tok("t"))
        for c in range(4):
            for k in range(8):
                for (ux, uw) in [(0, 0), (1, 0), (0, 1)]:
                    top(lambda eng, c=c, k=k, ux=ux, uw=uw: eng.matmul(
                        ps[:, 2, c * 16:c * 16 + 16],
                        gF2[:, ux, k, c * 128:(c + 1) * 128],
                        wbtile[:, uw, k, :],
                        start=(k == 0 and ux == 0 and uw == 0), stop=False),
                        waits=[t_xn1F, t_wb] + PS.use(2))
            top(lambda eng, c=c: eng.matmul(
                ps[:, 2, c * 16:c * 16 + 16], ones1[:], bb_row[:],
                start=False, stop=True), waits=[t_ones])
        t_bmm = S.tok("t")
        sop(lambda eng: eng.activation(bt_beta[:, :], ps[:, 2, 0:64],
                                       AF.Exp, scale=c_mone),
            waits=[t_bmm, t_ones] + wwait("bt_beta"))
        t_e = S.tok("s")
        ev = vop(lambda eng: eng.tensor_scalar(bt_beta[:, :], bt_beta[:, :],
                                               1.0, None, ALU.add), waits=[t_e])
        PS.done(2, ev)
        vop(lambda eng: eng.reciprocal(bt_beta[:, :], bt_beta[:, :]))
        vop(lambda eng: eng.tensor_scalar(bt_beta[:, :], bt_beta[:, :],
                                          0.8, 0.999, ALU.max, ALU.min))
        t_beta = S.tok("v")
        dump("beta", bt_beta[:, :], [t_beta], "bt_beta")
        sop(lambda eng: eng.activation(
            bt_ln[:].rearrange("p c h -> p (c h)"), bt_beta[:, :], AF.Ln),
            waits=[t_beta] + wwait("bt_ln"))
        t_lnb = S.tok("s")
        cur_dbg[0] = old_dbg
        front[0] = (t_xn1F, t_lnb)

    for gti, (sq, ti) in enumerate(seq_tiles):
        tok0 = sq * T_SEQ + ti * TILE
        DBG = dbg and gti == 0
        cur_dbg[0] = gti == 0

        g_x = G[8][:].rearrange("p (c f) -> p c f", f=HID)
        g_xn1 = G[0][:].rearrange("p (c f) -> p c f", f=HID)
        g_qt = G[3][:].rearrange("p (c f) -> p c f", f=HID)
        g_qt4 = G[3][:].rearrange("p (c h e) -> p c h e", h=H, e=D)
        g_kt = G[4][:].rearrange("p (c f) -> p c f", f=HID)
        g_kt4 = G[4][:].rearrange("p (c h e) -> p c h e", h=H, e=D)
        g_vt4 = G[5][:].rearrange("p (c h e) -> p c h e", h=H, e=D)
        g_attnF2 = G[6][:].bitcast(BF16).rearrange("p (u k n) -> p u k n",
                                                   u=2, k=8, n=TILE)
        g_h = G[7][:].rearrange("p (c f) -> p c f", f=HID)

        use_table(LN_EXP)
        S.mark(f"tile{gti}_start")
        if gti == 0:
            emit_ln1(0, [])
            emit_front(0)
        t_x, t_xn1 = prefetched[0]
        g_xn1F2 = G[2][:].bitcast(BF16).rearrange("p (u k n) -> p u k n",
                                                  u=2, k=8, n=TILE)
        t_xn1F, t_lnb = front[0]

        S.mark(f"tile{gti}_xn1F")
        def proj(wi, dst3, evac_waits=()):
            evw = list(evac_waits)
            ew = []
            for half in range(2):
                wts = [None] * 4
                for c in range(4):
                    b = 2 + c
                    for t_ in PS.use(b):
                        ew.append(t_)
                last = None
                for k in range(8):
                    kg = k // 2
                    slot = kg % 2
                    if k % 2 == 0:
                        wts[kg] = S.dma(
                            "sync", wbuf[:, slot, :, :, :],
                            whl_d[wi, half, :, :, kg * 2:(kg + 1) * 2, :],
                            waits=[wbuf_free[slot], S.tok("d_pre")],
                            sem=f"d_w{slot}")
                    for c in range(4):
                        b = 2 + c
                        for (ux, uw) in [(0, 0), (1, 0), (0, 1)]:
                            last = top(
                                lambda eng, c=c, k=k, b=b, slot=slot, ux=ux,
                                uw=uw: eng.matmul(
                                    ps[:, b, :],
                                    g_xn1F2[:, ux, k, c * 128:(c + 1) * 128],
                                    wbuf[:, slot, uw, k % 2, :],
                                    start=(k == 0 and ux == 0 and uw == 0),
                                    stop=(k == 7 and uw == 1)),
                                waits=[wts[kg], t_xn1F] + ew)
                    if k % 2 == 1:
                        wbuf_free[slot] = last
                    ew = []
                for c in range(4):
                    b = 2 + c
                    ev = sop(lambda eng, c=c, b=b, half=half, dst3=dst3:
                             eng.activation(
                                 dst3[:, c, half * 512:(half + 1) * 512],
                                 ps[:, b, :], AF.Identity),
                             waits=[last] + evw)
                    evw = []
                    PS.done(b, ev)
            return S.tok("s")

        t_q = proj(0, g_qt, wwait("G3"))
        for c in range(4):
            top(lambda eng, c=c: eng.matmul(
                ps[:, 3, c * 16:c * 16 + 16], utri[:], bt_ln[:, c, :],
                start=True, stop=True),
                waits=[t_lnb] + PS.use(3))
        t_cum = S.tok("t")
        war_add("bt_ln", t_cum)
        sop(lambda eng: eng.activation(
            bt_P[:].rearrange("p c h -> p (c h)"), ps[:, 3, 0:64], AF.Exp),
            waits=[t_cum] + wwait("bt_P"))
        t_p = S.tok("s")
        PS.done(3, t_p)
        vop(lambda eng: eng.reciprocal(
            bt_Pi[:].rearrange("p c h -> p (c h)"),
            bt_P[:].rearrange("p c h -> p (c h)")), waits=[t_p])
        t_pi = S.tok("v")
        t_zp = S.dma("sync", pcrow[:, :].rearrange("q (c h) -> q c h", h=16),
                     bt_P[127:128, :, :], waits=[t_p] + wwait("pcrow"),
                     sem="d_zp")
        war_add("bt_P", t_zp)
        t_k = proj(1, g_kt, wwait("G4"))
        t_pcmm = top(lambda eng: eng.matmul(
            ps[:, 3, 0:64], ones1[:], pcrow[:, :], start=True, stop=True),
            waits=[t_zp, t_ones] + PS.use(3))
        t_pc = vop(lambda eng: eng.tensor_copy(
            bt_PC[:].rearrange("p c h -> p (c h)"), ps[:, 3, 0:64]),
            waits=[t_pcmm] + wwait("bt_PC"))
        PS.done(3, t_pc)
        war_add("pcrow", t_pcmm)
        dump("pp", bt_P[:].rearrange("p c h -> p (c h)"), [t_p], "bt_P")
        dump("ppi", bt_Pi[:].rearrange("p c h -> p (c h)"), [t_pi])
        S.mark(f"tile{gti}_beta")

        scrA = G[6][:, 0:2048].rearrange("p (c h e) -> p c h e", h=H, e=32)
        scrB = G[6][:, 2048:4096].rearrange("p (c h e) -> p c h e", h=H, e=32)

        def rope(g4, dep):
            lo = g4[:, :, :, 0:32]
            hi = g4[:, :, :, 32:64]
            cosb = costab[:, ti * 4:ti * 4 + 4, :].unsqueeze(2).broadcast_to(
                [128, 4, H, 32])
            sinb = sintab[:, ti * 4:ti * 4 + 4, :].unsqueeze(2).broadcast_to(
                [128, 4, H, 32])
            vop(lambda eng: eng.tensor_tensor(scrA[:], lo, cosb, ALU.mult),
                waits=[dep, t_pre] + wwait("G6"))
            vop(lambda eng: eng.tensor_tensor(scrB[:], lo, sinb, ALU.mult))
            vop(lambda eng: eng.tensor_tensor(lo, hi, sinb, ALU.mult))
            vop(lambda eng: eng.tensor_tensor(hi, hi, cosb, ALU.mult))
            vop(lambda eng: eng.tensor_tensor(lo, scrA[:], lo, ALU.subtract))
            return vop(lambda eng: eng.tensor_tensor(hi, hi, scrB[:], ALU.add))

        def phi(gflat, dep):
            scr = G[6][:]
            vop(lambda eng, g=gflat: eng.tensor_scalar(
                scr[:, :], g, 0.0, None, ALU.min), waits=[dep])
            tv = S.tok("v")
            te = sop(lambda eng: eng.activation(scr[:, :], scr[:, :], AF.Exp),
                     waits=[tv])
            vop(lambda eng, g=gflat: eng.tensor_scalar(
                g, g, 0.0, None, ALU.max))
            return vop(lambda eng, g=gflat: eng.tensor_tensor(
                g, g, scr[:, :], ALU.add), waits=[te])

        S.mark(f"tile{gti}_qk")
        t_qr = rope(g_qt4, t_q)
        t_qp = phi(G[3][:], t_qr)
        t_v = proj(2, G[5][:].rearrange("p (c f) -> p c f", f=HID),
                   wwait("G5"))
        war_add("G2", wbuf_free[0])
        war_add("G2", wbuf_free[1])
        t_kr = rope(g_kt4, t_k)
        t_kp = phi(G[4][:], t_kr)
        t_ktd = vop(lambda eng: eng.tensor_tensor(
            g_kt4[:, :, :, :], g_kt4[:, :, :, :],
            bt_Pi[:].unsqueeze(3).broadcast_to([128, 4, H, D]), ALU.mult),
            waits=[t_kp, t_pi])
        dump("qt", G[3][:], [t_qp], "G3")
        dump("kt", G[4][:], [t_ktd], "G4")
        dump("vt", G[5][:], [t_v], "G5")

        S.mark(f"tile{gti}_ropephi")
        if ti == 0:
            t_s0 = vop(lambda eng: eng.memset(Sdup[:, :, :], 0.0),
                       waits=wwait("Sdup"))
            t_z0 = vop(lambda eng: eng.memset(z0row[:, :], 0.0),
                       waits=wwait("z0row"))
        else:
            t_s0 = war.get("Sdup_ready", [S.tok("v")])[0]
            t_z0 = war.get("z0_ready", [S.tok("v")])[0]
        t_supd = t_s0

        def transF(src3, dstF, dep, b0, warname):
            evs = []
            for half in range(2):
                b = b0 + half
                last = None
                for j in range(4):
                    f = half * 4 + j
                    last = top(lambda eng, f=f, b=b, j=j, src3=src3:
                               eng.transpose(
                                   ps[:, b, j * 128:(j + 1) * 128],
                                   src3[:, f * 128:(f + 1) * 128], iden[:]),
                               waits=[dep] + PS.use(b))
                ev = sop(lambda eng, half=half, b=b, dstF=dstF:
                         eng.activation(
                             dstF[:, half * 4:half * 4 + 4, :],
                             ps[:, b, :].rearrange("p (j n) -> p j n", j=4),
                             AF.Identity),
                         waits=[last] + wwait(warname))
                PS.done(b, ev)
                evs.append(ev)
            return evs[-1]

        def prep_kcz_z(c):
            nonlocal t_z0
            kcz = []
            for hh in range(2):
                bb_ = 6 + hh
                top(lambda eng, c=c, hh=hh, bb_=bb_: eng.matmul(
                    ps[:, bb_, :], utri[:],
                    g_kt[:, c, hh * 512:(hh + 1) * 512],
                    start=True, stop=False),
                    waits=[t_ktd, t_z0, t_ones] + PS.use(bb_))
                m = top(lambda eng, hh=hh, bb_=bb_: eng.matmul(
                    ps[:, bb_, :], ones1[:],
                    z0row[0:1, hh * 512:(hh + 1) * 512],
                    start=False, stop=True))
                kcz.append(m)
            war_add("z0row", kcz[-1])
            zmm = []
            for hh in range(2):
                bb_ = hh
                m = top(lambda eng, c=c, hh=hh, bb_=bb_: eng.matmul(
                    ps[0:1, bb_, :], ones_col[:],
                    g_kt[:, c, hh * 512:(hh + 1) * 512],
                    start=True, stop=True),
                    waits=[t_ktd, t_ones] + PS.use(bb_))
                zmm.append(m)
            vop(lambda eng: eng.tensor_tensor(
                z0row[0:1, :],
                ps[0:1, 0:2, :].rearrange("q b n -> q (b n)"),
                z0row[0:1, :], ALU.add),
                waits=[zmm[0], zmm[1], t_zp] + wwait("z0row"))
            t_z0 = vop(lambda eng, c=c: eng.tensor_tensor(
                z0row[0:1, :].rearrange("q (h e) -> q h e", e=64),
                z0row[0:1, :].rearrange("q (h e) -> q h e", e=64),
                pcrow[0:1, c * 16:(c + 1) * 16].unsqueeze(2)
                .broadcast_to([1, 16, 64]), ALU.mult))
            PS.done(0, t_z0)
            PS.done(1, t_z0)
            return kcz

        def prep_den(c, kcz):
            tden0 = vop(lambda eng, c=c: eng.tensor_tensor(
                rs1[:, 0:512], g_qt[:, c, 0:512], ps[:, 6, :], ALU.mult),
                waits=[kcz[0], t_qp] + wwait("rs1"))
            tden1 = vop(lambda eng, c=c: eng.tensor_tensor(
                rs1[:, 512:1024], g_qt[:, c, 512:1024], ps[:, 7, :],
                ALU.mult), waits=[kcz[1]])
            PS.done(6, tden0)
            PS.done(7, tden1)
            vop(lambda eng: eng.tensor_reduce(
                den_s[:, :],
                rs1[:, 0:1024].rearrange("p (h e) -> p h e", e=64),
                AX.X, ALU.add))
            vop(lambda eng, c=c: eng.tensor_tensor(
                den_s[:, :], den_s[:, :], bt_P[:, c, :], ALU.mult))
            vop(lambda eng: eng.tensor_scalar(
                den_s[:, :], den_s[:, :], float(EPS), None, ALU.add))
            vop(lambda eng: eng.reciprocal(drcp[:, :], den_s[:, :]))
            tq = vop(lambda eng, c=c: eng.tensor_tensor(
                qsc[:, :], bt_P[:, c, :], drcp[:, :], ALU.mult))
            if DBG and c == 0:
                dump("den", den_s[:, :], [tq])
            t_qhat = vop(lambda eng, c=c: eng.tensor_tensor(
                g_qt4[:, c], g_qt4[:, c],
                qsc[:].unsqueeze(2).broadcast_to([128, H, D]), ALU.mult),
                waits=[tq] + wwait("G3"))
            return t_qhat

        def scores_all(t_qF, t_kF):
            toksA, toksB = [], []
            for par in range(2):
                for h in range(par, 16, 2):
                    b0 = 0 if h < 8 else 2
                    hi_ = h % 8
                    bb_ = b0 + hi_ // 4
                    part0 = par * 64
                    extra = ([toksA[-1] if not toksB else toksB[-1]]
                             if (par == 1 and h == 1) else [])
                    m = top(lambda eng, h=h, bb_=bb_, hi_=hi_, part0=part0:
                            eng.matmul(
                                ps[:, bb_,
                                   (hi_ % 4) * 128:(hi_ % 4) * 128 + 128],
                                kF[part0:part0 + 64, h // 2, :],
                                qF[part0:part0 + 64, h // 2, :],
                                start=True, stop=True),
                            waits=[t_qF, t_kF] + PS.use(bb_) + extra)
                    (toksA if h < 8 else toksB).append(m)
            return toksA, toksB

        def mask_evac(b0, waits):
            ev = vop(lambda eng, b0=b0: eng.tensor_tensor(
                AT[:, :, :],
                ps[:, b0:b0 + 2, :].rearrange("p b n -> p (b n)")
                .rearrange("p (h n) -> p h n", n=128),
                utri[:].unsqueeze(1).broadcast_to([128, 8, 128]),
                ALU.mult), waits=waits + wwait("AT"))
            PS.done(b0, ev)
            PS.done(b0 + 1, ev)
            return ev

        def av_inter(c, h0, bk, evt):
            toks = []
            for p_ in range(4):
                he = h0 + 2 * p_
                ho = he + 1
                fc = (h0 // 2) + p_
                sl = slice(p_ * 128, p_ * 128 + 128)
                top(lambda eng, he=he, p_=p_, bk=bk, sl=sl, c=c:
                    eng.matmul(ps[0:64, bk, sl],
                               g_vt4[:, c, he, :], AT[:, 2 * p_, :],
                               start=True, stop=False),
                    waits=[evt, t_v, t_supd]
                    + (PS.use(bk) if p_ == 0 else []))
                t1 = top(lambda eng, he=he, fc=fc, bk=bk, sl=sl:
                         eng.matmul(ps[0:64, bk, sl],
                                    Sdup[0:64, fc, :], qF[0:64, fc, :],
                                    start=False, stop=True))
                top(lambda eng, ho=ho, p_=p_, bk=bk, sl=sl, c=c:
                    eng.matmul(ps[64:128, bk, sl],
                               g_vt4[:, c, ho, :], AT[:, 2 * p_ + 1, :],
                               start=True, stop=False,
                               tile_position=(0, 64)))
                t2 = top(lambda eng, ho=ho, fc=fc, bk=bk, sl=sl:
                         eng.matmul(ps[64:128, bk, sl],
                                    Sdup[64:128, fc, :], qF[64:128, fc, :],
                                    start=False, stop=True,
                                    tile_position=(64, 64)))
                toks += [t1, t2]
            th = sop(lambda eng, h0=h0, bk=bk, c=c: eng.activation(
                g_attnF2[:, 0, (h0 // 2):(h0 // 2) + 4,
                         c * 128:(c + 1) * 128],
                ps[:, bk, :].rearrange("p (j n) -> p j n", j=4),
                AF.Identity),
                waits=[toks[-1]])
            ev = vop(lambda eng, h0=h0, bk=bk, c=c: eng.tensor_tensor(
                g_attnF2[:, 1, (h0 // 2):(h0 // 2) + 4,
                         c * 128:(c + 1) * 128],
                ps[:, bk, :].rearrange("p (j n) -> p j n", j=4),
                g_attnF2[:, 0, (h0 // 2):(h0 // 2) + 4,
                         c * 128:(c + 1) * 128], ALU.subtract),
                waits=[th])
            PS.done(bk, ev)
            return ev, toks

        kcz_n = prep_kcz_z(0)
        t_kF = transF(g_kt[:, 0, :], kF[:], t_ktd, 2, "kF")
        t_qhat = prep_den(0, kcz_n)
        t_qF = transF(g_qt[:, 0, :], qF[:], t_qhat, 0, "qF")
        for c in range(4):
            S.mark(f"tile{gti}_chunk{c}")
            mmA, mmB = scores_all(t_qF, t_kF)
            evA = mask_evac(0, [mmA[-1], mmB[-1]])
            if DBG and c == 0:
                dump("at0", AT[:, :, :], [evA], "AT")
            if c < 3:
                kcz_n = prep_kcz_z(c + 1)
            evAVa, mmAVa = av_inter(c, 0, 4, evA)
            war_add("AT", mmAVa[-1])
            evB = mask_evac(2, [mmB[-1]])
            if c < 3:
                t_qhat = prep_den(c + 1, kcz_n)
                t_kF = transF(g_kt[:, c + 1, :], kF[:], t_ktd, 2, "kF")
            evAVb, mmAVb = av_inter(c, 8, 5, evB)
            war_add("AT", mmAVb[-1])
            war_add("qF", mmAVa[-1])
            war_add("qF", mmAVb[-1])
            war_add("kF", mmA[-1])
            war_add("kF", mmB[-1])

            ds_toks = []
            for h in range(16):
                sl = slice((h // 2) * 64, (h // 2) * 64 + 64)
                if h % 2 == 0:
                    m = top(lambda eng, h=h, sl=sl, c=c: eng.matmul(
                        ps[0:64, 4, sl], g_kt4[:, c, h, :], g_vt4[:, c, h, :],
                        start=True, stop=True),
                        waits=[t_ktd, t_v] + (PS.use(4) if h == 0 else []))
                else:
                    m = top(lambda eng, h=h, sl=sl, c=c: eng.matmul(
                        ps[64:128, 4, sl], g_kt4[:, c, h, :],
                        g_vt4[:, c, h, :],
                        start=True, stop=True, tile_position=(0, 64)))
                ds_toks.append(m)
            vop(lambda eng: eng.tensor_tensor(
                rs1[:, 0:512],
                Sdup[:].rearrange("p j e -> p (j e)"),
                ps[:, 4, :], ALU.add),
                waits=[ds_toks[-1], mmAVa[-1], mmAVb[-1]] + wwait("rs1"))
            rs3 = rs1[:, 0:512].rearrange("p (j e) -> p j e", e=64)
            pce = bt_PC[:, c, :].rearrange("p (j two) -> p j two", two=2)
            vop(lambda eng, rs3=rs3, pce=pce: eng.tensor_tensor(
                Sdup[0:64, :, :], rs3[0:64],
                pce[0:64, :, 0].unsqueeze(2).broadcast_to([64, 8, 64]),
                ALU.mult))
            t_supd = vop(lambda eng, rs3=rs3, pce=pce: eng.tensor_tensor(
                Sdup[64:128, :, :], rs3[64:128],
                pce[64:128, :, 1].unsqueeze(2).broadcast_to([64, 8, 64]),
                ALU.mult))
            war_add("rs1", t_supd)
            PS.done(4, t_supd)
            if c < 3:
                t_qF = transF(g_qt[:, c + 1, :], qF[:], t_qhat, 0, "qF")

        war["Sdup_ready"] = [t_supd]
        war["z0_ready"] = [t_z0]
        war["Sdup"] = [t_supd]
        war_add("bt_PC", t_supd)
        dump("attnF", G[6][:], [S.tok("v")], "G6")

        S.mark(f"tile{gti}_chunks")
        t_attn_done = S.tok("v")
        ew = wwait("G7")
        h_ev = [None] * 4
        for c in range(4):
            vop(lambda eng, c=c: eng.tensor_tensor(
                g_xn1[:, c], g_xn1[:, c], bo_f[:, :], ALU.add),
                waits=[t_pre])
        t_xn1b = S.tok("v")
        for half in range(2):
            wts = [None] * 4
            pw = []
            for c in range(4):
                pw += PS.use(2 + c)
            last = None
            for k in range(8):
                kg = k // 2
                slot = kg % 2
                if k % 2 == 0:
                    wts[kg] = S.dma(
                        "sync", wbuf[:, slot, :, :, :],
                        whl_d[3, half, :, :, kg * 2:(kg + 1) * 2, :],
                        waits=[wbuf_free[slot], S.tok("d_pre")],
                        sem=f"d_w{slot}")
                for c in range(4):
                    b = 2 + c
                    for (ux, uw) in [(0, 0), (1, 0), (0, 1)]:
                        last = top(
                            lambda eng, c=c, k=k, b=b, slot=slot, ux=ux,
                            uw=uw: eng.matmul(
                                ps[:, b, :],
                                g_attnF2[:, ux, k, c * 128:(c + 1) * 128],
                                wbuf[:, slot, uw, k % 2, :],
                                start=(k == 0 and ux == 0 and uw == 0),
                                stop=(k == 7 and uw == 1)),
                            waits=[wts[kg], t_attn_done] + pw)
                if k % 2 == 1:
                    wbuf_free[slot] = last
                pw = []
            for c in range(4):
                b = 2 + c
                ev = vop(lambda eng, c=c, b=b, half=half:
                         eng.scalar_tensor_tensor(
                             g_h[:, c, half * 512:(half + 1) * 512],
                             ps[:, b, :], 1.0,
                             g_xn1[:, c, half * 512:(half + 1) * 512],
                             ALU.mult, ALU.add),
                         waits=[last, t_xn1b] + (ew if (c == 0 and half == 0)
                                                 else []))
                PS.done(b, ev)
                if half == 1:
                    h_ev[c] = ev
        war_add("G6", wbuf_free[0])
        war_add("G6", wbuf_free[1])
        t_h = S.tok("v")
        dump("h", G[7][:], [t_h], "G7")

        S.mark(f"tile{gti}_wo")
        stw = wwait("st")
        g_ab = G[3][:].rearrange("p (c f) -> p c f", f=HID)
        g_x1 = G[4][:].rearrange("p (c f) -> p c f", f=HID)
        g_xn2 = G[5][:].rearrange("p (c f) -> p c f", f=HID)
        g_xn2F2 = G[2][:].bitcast(BF16).rearrange("p (u k n) -> p u k n",
                                                  u=2, k=8, n=TILE)
        rsw = wwait("rs1")
        g3w = wwait("G3")
        g4w = wwait("G4")
        g5w = wwait("G5")
        g2w = wwait("G2")
        for c in range(4):
            cs = slice(c, c + 1)
            sop(lambda eng, c=c: eng.activation(
                sdump[:, :], g_h[:, c], AF.Square,
                accum_out=st_sq[:, c:c + 1]),
                waits=[h_ev[c]] + stw + rsw)
            t_rms = rms_finalize(cs, c_rmseps, S.tok("s"))
            t_ab = vop(lambda eng, c=c: eng.scalar_tensor_tensor(
                g_ab[:, c], g_h[:, c], st_r[:, c:c + 1], scale_f[:, :],
                ALU.mult, ALU.mult, accum_out=st_sum[:, c:c + 1]),
                waits=[t_rms, t_pre] + (g3w if c == 0 else []))
            sop(lambda eng, c=c: eng.activation(
                sdump[:, :], g_ab[:, c], AF.Square,
                accum_out=st_sq[:, c:c + 1]), waits=[t_ab])
            t_ln2 = ln_finalize(cs, c_lneps, S.tok("s"), t_ab)
            vop(lambda eng, c=c: eng.tensor_scalar(
                rs1[:, 0:1024], g_ab[:, c], st_mu[:, c:c + 1],
                st_r[:, c:c + 1], ALU.subtract, ALU.mult),
                waits=[t_ln2])
            t_x1 = vop(lambda eng, c=c: eng.scalar_tensor_tensor(
                g_x1[:, c], rs1[:, 0:1024], 1.0, g_x[:, c],
                ALU.mult, ALU.add, accum_out=st_sum[:, c:c + 1]),
                waits=(g4w if c == 0 else []))
            if c == 3:
                t_x1tok = t_x1
            sop(lambda eng, c=c: eng.activation(
                sdump[:, :], g_x1[:, c], AF.Square,
                accum_out=st_sq[:, c:c + 1]), waits=[t_x1])
            t_ln3 = ln_finalize(cs, c_lneps, S.tok("s"), t_x1)
            t_xn2 = vop(lambda eng, c=c: eng.tensor_scalar(
                g_xn2[:, c], g_x1[:, c], st_mu[:, c:c + 1], st_r[:, c:c + 1],
                ALU.subtract, ALU.mult),
                waits=[t_ln3] + (g5w if c == 0 else []))
            for half in range(2):
                b = half
                last = None
                for j in range(4):
                    f = half * 4 + j
                    last = top(lambda eng, c=c, f=f, b=b, j=j: eng.transpose(
                        ps[:, b, j * 128:(j + 1) * 128],
                        g_xn2[:, c, f * 128:(f + 1) * 128], iden[:]),
                        waits=[t_xn2] + PS.use(b))
                th = sop(lambda eng, c=c, half=half, b=b: eng.activation(
                    g_xn2F2[:, 0, half * 4:half * 4 + 4,
                            c * 128:(c + 1) * 128],
                    ps[:, b, :].rearrange("p (j n) -> p j n", j=4),
                    AF.Identity),
                    waits=[last] + (g2w if (c == 0 and half == 0) else []))
                ev = vop(lambda eng, c=c, half=half, b=b: eng.tensor_tensor(
                    g_xn2F2[:, 1, half * 4:half * 4 + 4,
                            c * 128:(c + 1) * 128],
                    ps[:, b, :].rearrange("p (j n) -> p j n", j=4),
                    g_xn2F2[:, 0, half * 4:half * 4 + 4,
                            c * 128:(c + 1) * 128], ALU.subtract),
                    waits=[th])
                PS.done(b, ev)
        war_add("rs1", S.tok("v"))
        war_add("st", S.tok("v"))
        t_xn2F = S.tok("v")
        dump("ab", G[3][:], [S.tok("v")], "G3")
        dump("x1", G[4][:], [S.tok("v")], "G4")
        dump("xn2", G[5][:], [S.tok("v")], "G5")

        S.mark(f"tile{gti}_xn2F")
        use_table(SILU)
        hm_toks = [None] * 24

        def hm_view(m, u):
            gi = 1 + (0 if m < 8 else (5 if m < 16 else 6))
            return G[gi][:].bitcast(BF16).rearrange(
                "p (v j n) -> p v j n", v=2, n=TILE)[:, u, m % 8, :]

        for m in range(24):
            slot = m % 2
            bg = (m % 2) * 2
            bu = bg + 1
            twg = S.dma("sync", wgbuf[:, slot, :, :], wgb_d[m, :, :, :],
                        waits=[wgbuf_free[slot], S.tok("d_pre")],
                        sem=f"d_g{slot}")
            twu = S.dma("sync", wubuf[:, slot, :, :], wub_d[m, :, :, :],
                        waits=[wubuf_free[slot], S.tok("d_pre")],
                        sem=f"d_u{slot}")
            lastg = lastu = None
            for k in range(8):
                for u in range(2):
                    lastg = top(lambda eng, k=k, u=u, bg=bg, slot=slot:
                                eng.matmul(
                                    ps[:, bg, :], wgbuf[:, slot, k, :],
                                    g_xn2F2[:, u, k, :],
                                    start=(k == 0 and u == 0),
                                    stop=(k == 7 and u == 1)),
                                waits=[twg, t_xn2F] + PS.use(bg))
            for k in range(8):
                for u in range(2):
                    lastu = top(lambda eng, k=k, u=u, bu=bu, slot=slot:
                                eng.matmul(
                                    ps[:, bu, :], wubuf[:, slot, k, :],
                                    g_xn2F2[:, u, k, :],
                                    start=(k == 0 and u == 0),
                                    stop=(k == 7 and u == 1)),
                                waits=[twu] + PS.use(bu))
            wgbuf_free[slot] = lastg
            wubuf_free[slot] = lastu
            tsil = sop(lambda eng, bg=bg: eng.activation(
                sdump[:, 0:512], ps[:, bg, :], AF.Silu, scale=c_s127),
                waits=[lastg, t_ones] + wwait("rs1"))
            PS.done(bg, tsil)
            warkey = "G" + str(1 + (0 if m < 8 else (5 if m < 16 else 6)))
            vop(lambda eng, bu=bu: eng.tensor_tensor(
                rs1[:, 512:1024], sdump[:, 0:512], ps[:, bu, :], ALU.mult),
                waits=[tsil, lastu])
            vop(lambda eng, m=m: eng.tensor_copy(
                hm_view(m, 0), rs1[:, 512:1024]),
                waits=(wwait(warkey) if m % 8 == 0 else []))
            tm = vop(lambda eng, m=m: eng.tensor_tensor(
                hm_view(m, 1), rs1[:, 512:1024], hm_view(m, 0), ALU.subtract))
            war_add("rs1", tm)
            PS.done(bu, tm)
            hm_toks[m] = tm

        war_add("G2", max(wgbuf_free, key=lambda t: t[1]))
        war_add("G2", max(wubuf_free, key=lambda t: t[1]))
        if gti + 1 < len(seq_tiles):
            emit_ln1(gti + 1, [t_x1tok])

        S.mark(f"tile{gti}_gateup")
        dn_last = {}
        for mk in range(24):
            for half in range(2):
                slot = (mk * 2 + half) % 4
                twd = S.dma("sync", wdbuf[:, slot, :],
                            wdb_d[mk, :, half * 512:(half + 1) * 512],
                            waits=[wdbuf_free[slot], S.tok("d_pre")],
                            sem=f"d_d{slot}")
                last = None
                for c in range(4):
                    b = c * 2 + half
                    for u in range(2):
                        last = top(lambda eng, mk=mk, c=c, u=u, b=b, slot=slot:
                                   eng.matmul(
                                       ps[:, b, :],
                                       hm_view(mk, u)[:, c * 128:(c + 1) * 128],
                                       wdbuf[:, slot, :],
                                       start=(mk == 0 and u == 0),
                                       stop=(mk == 23 and u == 1)),
                                   waits=([twd, hm_toks[mk]]
                                          + (PS.use(b) if mk == 0 else [])))
                    dn_last[(c, half)] = last
                wdbuf_free[slot] = last
        g_f = G[3][:].rearrange("p (c f) -> p c f", f=HID)
        g3w2 = wwait("G3")
        for c in range(4):
            evs = sop(lambda eng, c=c: eng.activation(
                g_f[:, c, 0:512], ps[:, c * 2, :], AF.Identity,
                accum_out=st_sum[:, c:c + 1]),
                waits=[dn_last[(c, 0)]] + (g3w2 if c == 0 else []))
            PS.done(c * 2, evs)
            evv = vop(lambda eng, c=c: eng.tensor_scalar(
                g_f[:, c, 512:1024], ps[:, c * 2 + 1, :],
                0.0, 0.0, ALU.add, ALU.add,
                accum_out=st_t2[:, c:c + 1]),
                waits=[dn_last[(c, 1)]] + (g3w2 if c == 0 else []))
            PS.done(c * 2 + 1, evv)
        t_sevac = S.tok("s")
        tvf = vop(lambda eng: eng.tensor_tensor(
            st_sum[:, 0:4], st_sum[:, 0:4], st_t2[:, 0:4], ALU.add),
            waits=[t_sevac])
        if gti + 1 < len(seq_tiles):
            emit_front(gti + 1)
        for c in range(4):
            sop(lambda eng, c=c: eng.activation(
                sdump[:, :], g_f[:, c], AF.Square,
                accum_out=st_sq[:, c:c + 1]), waits=[tvf] + wwait("rs1"))
        use_table(LN_EXP)
        t_ln4 = ln_finalize(slice(0, 4), c_lneps, S.tok("s"), tvf)
        dump("fout", G[3][:], [S.tok("v"), tvf], "G3")

        g_out = G[3][:].rearrange("p (c f) -> p c f", f=HID)
        for c in range(4):
            vop(lambda eng, c=c: eng.tensor_scalar(
                rs1[:, 0:1024], g_f[:, c], st_mu[:, c:c + 1], st_r[:, c:c + 1],
                ALU.subtract, ALU.mult), waits=[t_ln4] + wwait("rs1"))
            vop(lambda eng, c=c: eng.tensor_tensor(
                g_out[:, c], rs1[:, 0:1024], g_x1[:, c], ALU.add),
                waits=(wwait("G3") if c == 0 else []))
        t_out = S.tok("v")
        war_add("rs1", t_out)
        war_add("st", t_out)
        t_odma = S.op("gpsimd",
                      (lambda o, i: (lambda eng: eng.dma_start(out=o, in_=i)))(
                          outd[tok0:tok0 + TILE, :]
                          .rearrange("(c p) f -> p c f", p=128),
                          g_out[:, :, :]),
                      waits=[t_out], inc="d_out", inc_n=16)
        war_add("G3", t_odma)
        war_add("G3", S.tok("s"))
        war_add("G3", t_out)
        war_add("G1", wgbuf_free[0])
        war_add("G4", t_out)
        war_add("G5", t_xn2F)
        for sl_ in range(4):
            war_add("G6", wdbuf_free[sl_])
            war_add("G7", wdbuf_free[sl_])
            war_add("G1", wdbuf_free[sl_])

    if max_items is not None:
        S.items = S.items[:max_items]
    else:
        fin = [("d_out", S.ctr.get("d_out", 0))]
        if dbg:
            fin.append(("d_dbg", S.ctr.get("d_dbg", 0)))
        S.op("gpsimd", lambda eng: None, waits=fin)

    sem_names = ["d_pre", "d_ff0", "d_x", "d_zp", "d_w0", "d_w1", "d_g0",
                 "d_g1", "d_u0", "d_u1", "d_d0", "d_d1", "d_d2", "d_d3",
                 "d_out", "d_dbg",
                 "t", "v", "s", "g"]
    with ctx:
        sems = {}
        with ExitStack() as sctx:
            for nm in sem_names:
                sems[nm] = sctx.enter_context(nc.semaphore(nm))
            with nc.Block() as block:
                @block.sync
                def _(eng):
                    S.replay("sync", eng, sems)

                @block.tensor
                def _(eng):
                    S.replay("tensor", eng, sems)

                @block.vector
                def _(eng):
                    S.replay("vector", eng, sems)

                @block.scalar
                def _(eng):
                    S.replay("scalar", eng, sems)

                @block.gpsimd
                def _(eng):
                    S.replay("gpsimd", eng, sems)

    nc._sched_marks = S.marks
    return nc


def prep_weights(Wq, Wk, Wv, Wb, bb, Wo, bo, scale, w_gate, w_up, w_down):
    def proj_layout(W):
        WT = np.ascontiguousarray(W.T)
        return np.ascontiguousarray(
            WT.reshape(8, 128, 2, 512).transpose(2, 1, 0, 3))

    out = {
        "wq": proj_layout(Wq), "wk": proj_layout(Wk), "wv": proj_layout(Wv),
        "wo": proj_layout(Wo),
        "wb": np.ascontiguousarray(Wb.T.reshape(8, 128, 16).transpose(1, 0, 2)),
        "bb": np.ascontiguousarray(bb.reshape(1, 16)),
        "bo": np.ascontiguousarray(bo.reshape(1, HID)),
        "bo_full": np.ascontiguousarray(
            np.broadcast_to(bo.reshape(1, HID), (128, HID))),
        "lnscale": np.ascontiguousarray(
            np.broadcast_to(scale.reshape(1, HID), (128, HID))),
    }
    wg8 = w_gate.astype(np.int8)
    wu8 = w_up.astype(np.int8)
    wd8 = w_down.astype(np.int8)
    assert np.array_equal(wg8.astype(np.int32), w_gate)
    out["wg"] = np.ascontiguousarray(
        wg8.T.reshape(8, 128, 24, 128).transpose(2, 1, 0, 3))
    out["wu"] = np.ascontiguousarray(
        wu8.T.reshape(8, 128, 24, 128).transpose(2, 1, 0, 3))
    out["wd"] = np.ascontiguousarray(wd8.T.reshape(24, 128, HID))
    out["utri"] = np.triu(np.ones((128, 128), np.float32))
    out["iden"] = np.eye(128, dtype=np.float32)
    inv = 1.0 / 10000.0 ** (np.arange(0, D, 2, dtype=np.float64) / D)
    t = np.arange(2048, dtype=np.float64)
    fr = t[:, None] * inv[None, :]
    out["costab"] = np.ascontiguousarray(
        np.cos(fr).astype(np.float32).reshape(16, 128, 32).transpose(1, 0, 2))
    out["sintab"] = np.ascontiguousarray(
        np.sin(fr).astype(np.float32).reshape(16, 128, 32).transpose(1, 0, 2))
    return out


_NC_CACHE = {}


def kernel(L0_attn_x, L0_attn_Wq, L0_attn_Wk, L0_attn_Wv, L0_attn_Wb,
           L0_attn_bb, L0_attn_Wo, L0_attn_bo, L0_attn_scale,
           L0_ffn_w_gate, L0_ffn_w_up, L0_ffn_w_down):
    from concourse.bass_utils import run_bass_kernel_spmd

    x = np.asarray(L0_attn_x, dtype=np.float32)
    B, T, _ = x.shape
    n_cores = 8
    per = B // n_cores
    if "nc" not in _NC_CACHE:
        _NC_CACHE["nc"] = build_core_program(n_seq=per, tiles_per_seq=T // TILE,
                                             dbg=False)
    nc = _NC_CACHE["nc"]
    wmaps = prep_weights(np.asarray(L0_attn_Wq), np.asarray(L0_attn_Wk),
                         np.asarray(L0_attn_Wv), np.asarray(L0_attn_Wb),
                         np.asarray(L0_attn_bb), np.asarray(L0_attn_Wo),
                         np.asarray(L0_attn_bo), np.asarray(L0_attn_scale),
                         np.asarray(L0_ffn_w_gate), np.asarray(L0_ffn_w_up),
                         np.asarray(L0_ffn_w_down))
    in_maps = []
    for i in range(n_cores):
        m = dict(wmaps)
        m["x"] = np.ascontiguousarray(
            x[i * per:(i + 1) * per].reshape(per * T, HID))
        in_maps.append(m)
    res = run_bass_kernel_spmd(nc, in_maps, core_ids=list(range(n_cores)))
    out = np.concatenate([r["out"].reshape(per, T, HID) for r in res.results],
                         axis=0)
    return out


# version 13
# speedup vs baseline: 1.1023x; 1.0242x over previous
import numpy as np
from contextlib import ExitStack

import concourse.bass as bass
import concourse.mybir as mybir

F32 = mybir.dt.float32
BF16 = mybir.dt.bfloat16
I8 = mybir.dt.int8
AF = mybir.ActivationFunctionType
ALU = mybir.AluOpType
AX = mybir.AxisListType

H, D = 16, 64
HID = 1024
FF = 3072
TILE = 512
EPS = 1e-6
LNEPS = 1e-5
SCALE127 = float(np.float32(1.0 / 127.0))


class Sched:
    ENGINES = ("sync", "tensor", "vector", "scalar", "gpsimd")

    def __init__(self):
        self.items = []
        self.ctr = {}
        self.last_wait = {}
        self.marks = []

    def mark(self, name):
        self.marks.append((name, len(self.items)))

    def tok(self, sem):
        return (sem, self.ctr.get(sem, 0))

    def op(self, engine, fn, waits=(), inc=None, inc_n=1):
        assert engine in self.ENGINES
        w = []
        for t in waits:
            if t is None:
                continue
            (sem, val) = t
            if val <= 0:
                continue
            assert val <= self.ctr.get(sem, 0), (sem, val, self.ctr.get(sem, 0))
            key = (engine, sem)
            if self.last_wait.get(key, 0) >= val:
                continue
            self.last_wait[key] = val
            w.append((sem, val))
        tok = None
        incs = []
        if inc is not None:
            self.ctr[inc] = self.ctr.get(inc, 0) + inc_n
            tok = (inc, self.ctr[inc])
            incs.append((inc, inc_n))
        self.items.append((engine, w, fn, incs))
        return tok

    def dma(self, engine, out, in_, waits=(), sem=None):
        sem = sem or ("d_" + engine)
        return self.op(
            engine,
            (lambda o, i: (lambda eng: eng.dma_start(out=o, in_=i)))(out, in_),
            waits=waits, inc=sem, inc_n=16)

    def replay(self, engine_name, eng, sems):
        for (e, waits, fn, incs) in self.items:
            if e != engine_name:
                continue
            for (sem, val) in waits:
                eng.wait_ge(sems[sem], val)
            instr = fn(eng)
            for (sem, n) in incs:
                assert instr is not None
                instr.then_inc(sems[sem], n)


class Psum:
    def __init__(self):
        self.free_tok = [[] for _ in range(8)]

    def use(self, b):
        return list(self.free_tok[b])

    def done(self, b, toks):
        if not isinstance(toks, list):
            toks = [toks]
        self.free_tok[b] = toks


def build_core_program(n_seq=4, tiles_per_seq=4, dbg=False, max_items=None):
    T_SEQ = tiles_per_seq * TILE
    NTOK = n_seq * T_SEQ
    nc = bass.Bass("TRN2", target_bir_lowering=False)
    S = Sched()
    PS = Psum()
    war = {}

    def wwait(name):
        return war.pop(name, [])

    def war_add(name, tok):
        war.setdefault(name, []).append(tok)

    xd = nc.dram_tensor("x", [NTOK, HID], F32, kind="ExternalInput").ap()
    outd = nc.dram_tensor("out", [NTOK, HID], F32, kind="ExternalOutput").ap()
    wq_d = nc.dram_tensor("wq", [2, 128, 8, 512], F32, kind="ExternalInput").ap()
    wk_d = nc.dram_tensor("wk", [2, 128, 8, 512], F32, kind="ExternalInput").ap()
    wv_d = nc.dram_tensor("wv", [2, 128, 8, 512], F32, kind="ExternalInput").ap()
    wo_d = nc.dram_tensor("wo", [2, 128, 8, 512], F32, kind="ExternalInput").ap()
    wb_d = nc.dram_tensor("wb", [128, 8, 16], F32, kind="ExternalInput").ap()
    bb_d = nc.dram_tensor("bb", [1, 16], F32, kind="ExternalInput").ap()
    bo_d = nc.dram_tensor("bo", [1, HID], F32, kind="ExternalInput").ap()
    sc_d = nc.dram_tensor("lnscale", [128, HID], F32, kind="ExternalInput").ap()
    wg_d = nc.dram_tensor("wg", [24, 128, 8, 128], I8, kind="ExternalInput").ap()
    wu_d = nc.dram_tensor("wu", [24, 128, 8, 128], I8, kind="ExternalInput").ap()
    wd_d = nc.dram_tensor("wd", [24, 128, HID], I8, kind="ExternalInput").ap()
    u_d = nc.dram_tensor("utri", [128, 128], F32, kind="ExternalInput").ap()
    bof_d = nc.dram_tensor("bo_full", [128, HID], F32, kind="ExternalInput").ap()
    whl_d = nc.dram_tensor("whl", [4, 2, 128, 2, 8, 512], BF16).ap()
    wgb_d = nc.dram_tensor("wgb", [24, 128, 8, 128], BF16).ap()
    wub_d = nc.dram_tensor("wub", [24, 128, 8, 128], BF16).ap()
    wdb_d = nc.dram_tensor("wdb", [24, 128, HID], BF16).ap()
    id_d = nc.dram_tensor("iden", [128, 128], F32, kind="ExternalInput").ap()
    cos_d = nc.dram_tensor("costab", [128, 16, 32], F32, kind="ExternalInput").ap()
    sin_d = nc.dram_tensor("sintab", [128, 16, 32], F32, kind="ExternalInput").ap()

    dbg_t = {}
    if dbg:
        for nm in ["xn1", "qt", "kt", "vt", "beta", "pp", "ppi", "den",
                   "at0", "attnF", "h", "ab", "x1", "xn2", "fout", "st"]:
            dbg_t[nm] = nc.dram_tensor("dbg_" + nm, [128, 4096], F32,
                                       kind="ExternalOutput").ap()

    ctx = ExitStack()

    def sbt(name, shape, dtype=F32):
        return ctx.enter_context(nc.sbuf_tensor(name, shape, dtype))

    G = [sbt(f"g{i}", [128, 4096]) for i in range(9)]
    wbuf = sbt("wbuf", [128, 2, 2, 2, 512], BF16)
    wgbuf = sbt("wgbuf", [128, 2, 8, 128], BF16)
    wubuf = sbt("wubuf", [128, 2, 8, 128], BF16)
    wdbuf = sbt("wdbuf", [128, 4, 512], BF16)
    wbtile = sbt("wbtile", [128, 2, 8, 16], BF16)
    bo_f = sbt("bo_f", [128, HID])
    qF = sbt("qF", [128, 8, 128])
    kF = sbt("kF", [128, 8, 128])
    AT = sbt("AT", [128, 8, 128])
    Sdup = sbt("Sdup", [128, 8, 64])
    rs1 = sbt("rs1", [128, 1024])
    sdump = rs1
    z0row = sbt("z0row", [1, HID])
    bb_row = sbt("bb_row", [1, 16])
    ones1 = sbt("ones1", [1, 128])
    scale_f = sbt("scale_f", [128, HID])
    utri = sbt("utri_s", [128, 128])
    iden = sbt("iden_s", [128, 128])
    costab = sbt("costab_s", [128, 16, 32])
    sintab = sbt("sintab_s", [128, 16, 32])
    bt_beta = sbt("bt_beta", [128, 64])
    bt_ln = sbt("bt_ln", [128, 4, 16])
    bt_P = sbt("bt_P", [128, 4, 16])
    bt_Pi = sbt("bt_Pi", [128, 4, 16])
    bt_PC = sbt("bt_PC", [128, 4, 16])
    pcrow = sbt("pcrow", [1, 64])
    ones_col = sbt("ones_col", [128, 1])
    cpool = sbt("cpool", [128, 8])
    den_s = sbt("den_s", [128, 16])
    drcp = sbt("drcp", [128, 16])
    qsc = sbt("qsc", [128, 16])
    st_sum = sbt("st_sum", [128, 8])
    st_sq = sbt("st_sq", [128, 8])
    st_mu = sbt("st_mu", [128, 8])
    st_t1 = sbt("st_t1", [128, 8])
    st_t2 = sbt("st_t2", [128, 8])
    st_r = sbt("st_r", [128, 8])

    ps = ctx.enter_context(nc.psum_tensor("ps", [128, 8, 512], F32))

    def vop(fn, waits=(), inc="v"):
        return S.op("vector", fn, waits=list(waits) + [S.tok("v")], inc=inc)

    def sop(fn, waits=(), inc="s"):
        return S.op("scalar", fn, waits=list(waits) + [S.tok("s")], inc=inc)

    def top(fn, waits=(), inc="t"):
        return S.op("tensor", fn, waits=waits, inc=inc)

    for (dst, src) in [(bb_row[:], bb_d[:, :]),
                       (bo_f[:], bof_d[:, :]), (scale_f[:], sc_d[:, :]),
                       (utri[:], u_d[:, :]), (iden[:], id_d[:, :]),
                       (costab[:], cos_d[:, :, :]), (sintab[:], sin_d[:, :, :])]:
        S.dma("sync", dst, src, sem="d_pre")
    t_pre = S.tok("d_pre")
    vop(lambda eng: eng.memset(ones1[:], 1.0))
    vop(lambda eng: eng.memset(ones_col[:], 1.0))
    for i, val in enumerate([LNEPS, EPS, -0.5, -1.0, 1.0 / HID, SCALE127]):
        vop(lambda eng, i=i, val=val: eng.memset(cpool[:, i:i + 1], float(val)))
    t_ones = S.tok("v")
    c_lneps = cpool[:, 0:1]
    c_rmseps = cpool[:, 1:2]
    c_mhalf = cpool[:, 2:3]
    c_mone = cpool[:, 3:4]
    c_invh = cpool[:, 4:5]
    c_s127 = cpool[:, 5:6]

    g0f = G[0][:]
    g1b = G[1][:].bitcast(BF16)
    t_ph0 = None
    for wi, wdram in enumerate([wq_d, wk_d, wv_d, wo_d]):
        for half in range(2):
            tin = S.dma("sync", g0f,
                        wdram[half, :, :, :].rearrange("p k n -> p (k n)"),
                        waits=[t_ph0] if t_ph0 else [t_pre], sem="d_pre")
            vop(lambda eng: eng.tensor_copy(g1b[:, 0:4096], g0f), waits=[tin])
            tv = vop(lambda eng: eng.tensor_tensor(
                g1b[:, 4096:8192], g0f, g1b[:, 0:4096], ALU.subtract))
            t_ph0 = S.dma("sync",
                          whl_d[wi, half, :, :, :, :]
                          .rearrange("p u k n -> p (u k n)"),
                          g1b[:, :], waits=[tv], sem="d_pre")
    g0b = G[0][:].bitcast(BF16)
    for (src_d, dst_d, four) in [(wg_d, wgb_d, True), (wu_d, wub_d, True),
                                 (wd_d, wdb_d, False)]:
        for m0 in range(0, 24, 8):
            if four:
                sap = src_d[m0:m0 + 8].rearrange("m p k n -> p m (k n)")
                dap = dst_d[m0:m0 + 8].rearrange("m p k n -> p m (k n)")
            else:
                sap = src_d[m0:m0 + 8].rearrange("m p n -> p m n")
                dap = dst_d[m0:m0 + 8].rearrange("m p n -> p m n")
            tin = S.dma("gpsimd", g0b[:, :], sap, waits=[t_ph0], sem="d_ff0")
            t_ph0 = S.dma("gpsimd", dap, g0b[:, :], waits=[tin], sem="d_ff0")
    twb = S.dma("sync", rs1[:, 0:128],
                wb_d[:, :, :].rearrange("p k h -> p (k h)"),
                waits=[t_ph0], sem="d_pre")
    vop(lambda eng: eng.tensor_copy(
        wbtile[:, 0, :, :].rearrange("p k h -> p (k h)"), rs1[:, 0:128]),
        waits=[twb])
    t_wb = vop(lambda eng: eng.tensor_tensor(
        wbtile[:, 1, :, :].rearrange("p k h -> p (k h)"), rs1[:, 0:128],
        wbtile[:, 0, :, :].rearrange("p k h -> p (k h)"), ALU.subtract))
    war_add("G0", S.tok("d_pre"))
    war_add("G1", S.tok("d_pre"))
    war_add("rs1", t_wb)

    cur_table = [None]

    def use_table(func):
        return

    cur_dbg = [False]

    def dump(name, src_ap, waits, warname=None):
        if not (dbg and cur_dbg[0]):
            return
        n = src_ap.shape[-1] if src_ap.ndim == 2 else int(
            np.prod(src_ap.shape[1:]))
        flat = src_ap if src_ap.ndim == 2 else src_ap.rearrange(
            "p " + " ".join(f"a{i}" for i in range(src_ap.ndim - 1))
            + " -> p (" + " ".join(f"a{i}" for i in range(src_ap.ndim - 1)) + ")")
        tok = S.op("gpsimd",
                   (lambda d, s_: (lambda eng: eng.dma_start(out=d, in_=s_)))(
                       dbg_t[name][:, 0:n], flat),
                   waits=list(waits) + [S.tok("d_dbg")], inc="d_dbg",
                   inc_n=16)
        if warname:
            war_add(warname, tok)
        return tok

    wbuf_free = [None, None]
    wgbuf_free = [None, None]
    wubuf_free = [None, None]
    wdbuf_free = [None, None, None, None]

    def ln_finalize(sl, eps_ap, t_sq_tok, t_sum_tok, src=None):
        src = src if src is not None else sl
        vop(lambda eng: eng.tensor_scalar(
            st_mu[:, sl], st_sum[:, src], 1.0 / HID, None, ALU.mult),
            waits=[t_sum_tok])
        vop(lambda eng: eng.tensor_scalar(
            st_t1[:, sl], st_sq[:, src], 1.0 / HID, None, ALU.mult),
            waits=[t_sq_tok])
        vop(lambda eng: eng.tensor_tensor(
            st_t2[:, sl], st_mu[:, sl], st_mu[:, sl], ALU.mult))
        tv = vop(lambda eng: eng.tensor_tensor(
            st_t1[:, sl], st_t1[:, sl], st_t2[:, sl], ALU.subtract))
        sop(lambda eng: eng.activation(st_t1[:, sl], st_t1[:, sl],
                                       AF.Ln, bias=eps_ap),
            waits=[tv, t_ones])
        t = sop(lambda eng: eng.activation(st_r[:, sl], st_t1[:, sl],
                                           AF.Exp, scale=c_mhalf))
        return t

    def rms_finalize(sl, eps_ap, t_sq_tok):
        sop(lambda eng: eng.activation(st_t1[:, sl], st_sq[:, sl],
                                       AF.Ln, bias=eps_ap, scale=c_invh),
            waits=[t_sq_tok, t_ones])
        t = sop(lambda eng: eng.activation(st_r[:, sl], st_t1[:, sl],
                                           AF.Exp, scale=c_mhalf))
        return t

    LN_EXP = AF.Exp
    SILU = AF.Silu

    seq_tiles = [(sq, ti) for sq in range(n_seq) for ti in range(tiles_per_seq)]

    prefetched = [None]

    def emit_ln1(g2, waits_extra):
        tok0n = (g2 // tiles_per_seq) * T_SEQ + (g2 % tiles_per_seq) * TILE
        gx = G[8][:].rearrange("p (c f) -> p c f", f=HID)
        gxn1 = G[0][:].rearrange("p (c f) -> p c f", f=HID)
        px = S.dma("gpsimd", gx[:, :, :],
                   xd[tok0n:tok0n + TILE, :]
                   .rearrange("(c p) f -> p c f", p=128),
                   waits=waits_extra, sem="d_x")
        for c in range(4):
            sop(lambda eng, c=c, gx=gx: eng.activation(
                sdump[:, :], gx[:, c], AF.Identity,
                accum_out=st_sum[:, 4 + c:5 + c]),
                waits=[px] + (wwait("st47") if c == 0 else [])
                + (wwait("rs1") if c == 0 else []))
            sop(lambda eng, c=c, gx=gx: eng.activation(
                sdump[:, :], gx[:, c], AF.Square,
                accum_out=st_sq[:, 4 + c:5 + c]))
        t_st = S.tok("s")
        t_ln1 = ln_finalize(slice(4, 8), c_lneps, t_st, t_st)
        war_add("st47", t_ln1)
        tnorm = None
        for c in range(4):
            tnorm = vop(lambda eng, c=c, gx=gx, gxn1=gxn1: eng.tensor_scalar(
                gxn1[:, c], gx[:, c], st_mu[:, 4 + c:5 + c],
                st_r[:, 4 + c:5 + c], ALU.subtract, ALU.mult),
                waits=[t_ln1] + wwait("G0"))
        prefetched[0] = (px, tnorm)

    front = [None]

    def emit_front(g2):
        old_dbg = cur_dbg[0]
        cur_dbg[0] = g2 == 0
        t_xn1 = prefetched[0][1]
        gxn1 = G[0][:].rearrange("p (c f) -> p c f", f=HID)
        gF2 = G[2][:].bitcast(BF16).rearrange("p (u k n) -> p u k n",
                                              u=2, k=8, n=TILE)
        for c in range(4):
            for half in range(2):
                b = half
                last = None
                for j in range(4):
                    f = half * 4 + j
                    last = top(lambda eng, c=c, f=f, b=b, j=j: eng.transpose(
                        ps[:, b, j * 128:(j + 1) * 128],
                        gxn1[:, c, f * 128:(f + 1) * 128], iden[:]),
                        waits=[t_xn1, t_pre] + PS.use(b))
                th = sop(lambda eng, c=c, half=half, b=b: eng.activation(
                    gF2[:, 0, half * 4:half * 4 + 4, c * 128:(c + 1) * 128],
                    ps[:, b, :].rearrange("p (j n) -> p j n", j=4),
                    AF.Identity),
                    waits=[last] + (wwait("G2") if (c == 0 and half == 0)
                                    else []))
                ev = vop(lambda eng, c=c, half=half, b=b: eng.tensor_tensor(
                    gF2[:, 1, half * 4:half * 4 + 4, c * 128:(c + 1) * 128],
                    ps[:, b, :].rearrange("p (j n) -> p j n", j=4),
                    gF2[:, 0, half * 4:half * 4 + 4, c * 128:(c + 1) * 128],
                    ALU.subtract),
                    waits=[th])
                PS.done(b, ev)
        t_xn1F = S.tok("v")
        war_add("G0", S.tok("t"))
        for c in range(4):
            for k in range(8):
                for (ux, uw) in [(0, 0), (1, 0), (0, 1)]:
                    top(lambda eng, c=c, k=k, ux=ux, uw=uw: eng.matmul(
                        ps[:, 2, c * 16:c * 16 + 16],
                        gF2[:, ux, k, c * 128:(c + 1) * 128],
                        wbtile[:, uw, k, :],
                        start=(k == 0 and ux == 0 and uw == 0), stop=False),
                        waits=[t_xn1F, t_wb] + PS.use(2))
            top(lambda eng, c=c: eng.matmul(
                ps[:, 2, c * 16:c * 16 + 16], ones1[:], bb_row[:],
                start=False, stop=True), waits=[t_ones])
        t_bmm = S.tok("t")
        sop(lambda eng: eng.activation(bt_beta[:, :], ps[:, 2, 0:64],
                                       AF.Exp, scale=c_mone),
            waits=[t_bmm, t_ones] + wwait("bt_beta"))
        t_e = S.tok("s")
        ev = vop(lambda eng: eng.tensor_scalar(bt_beta[:, :], bt_beta[:, :],
                                               1.0, None, ALU.add), waits=[t_e])
        PS.done(2, ev)
        vop(lambda eng: eng.reciprocal(bt_beta[:, :], bt_beta[:, :]))
        vop(lambda eng: eng.tensor_scalar(bt_beta[:, :], bt_beta[:, :],
                                          0.8, 0.999, ALU.max, ALU.min))
        t_beta = S.tok("v")
        dump("beta", bt_beta[:, :], [t_beta], "bt_beta")
        sop(lambda eng: eng.activation(
            bt_ln[:].rearrange("p c h -> p (c h)"), bt_beta[:, :], AF.Ln),
            waits=[t_beta] + wwait("bt_ln"))
        t_lnb = S.tok("s")
        cur_dbg[0] = old_dbg
        front[0] = (t_xn1F, t_lnb)

    for gti, (sq, ti) in enumerate(seq_tiles):
        tok0 = sq * T_SEQ + ti * TILE
        DBG = dbg and gti == 0
        cur_dbg[0] = gti == 0

        g_x = G[8][:].rearrange("p (c f) -> p c f", f=HID)
        g_xn1 = G[0][:].rearrange("p (c f) -> p c f", f=HID)
        g_qt = G[3][:].rearrange("p (c f) -> p c f", f=HID)
        g_qt4 = G[3][:].rearrange("p (c h e) -> p c h e", h=H, e=D)
        g_kt = G[4][:].rearrange("p (c f) -> p c f", f=HID)
        g_kt4 = G[4][:].rearrange("p (c h e) -> p c h e", h=H, e=D)
        g_vt4 = G[5][:].rearrange("p (c h e) -> p c h e", h=H, e=D)
        g_attnF2 = G[6][:].bitcast(BF16).rearrange("p (u k n) -> p u k n",
                                                   u=2, k=8, n=TILE)
        g_h = G[7][:].rearrange("p (c f) -> p c f", f=HID)

        use_table(LN_EXP)
        S.mark(f"tile{gti}_start")
        if gti == 0:
            emit_ln1(0, [])
            emit_front(0)
        t_x, t_xn1 = prefetched[0]
        g_xn1F2 = G[2][:].bitcast(BF16).rearrange("p (u k n) -> p u k n",
                                                  u=2, k=8, n=TILE)
        t_xn1F, t_lnb = front[0]

        S.mark(f"tile{gti}_xn1F")
        def proj(wi, dst3, evac_waits=()):
            evw = list(evac_waits)
            ew = []
            for half in range(2):
                wts = [None] * 4
                for c in range(4):
                    b = 2 + c
                    for t_ in PS.use(b):
                        ew.append(t_)
                last = None
                for k in range(8):
                    kg = k // 2
                    slot = kg % 2
                    if k % 2 == 0:
                        wts[kg] = S.dma(
                            "sync", wbuf[:, slot, :, :, :],
                            whl_d[wi, half, :, :, kg * 2:(kg + 1) * 2, :],
                            waits=[wbuf_free[slot], S.tok("d_pre")],
                            sem=f"d_w{slot}")
                    for c in range(4):
                        b = 2 + c
                        for (ux, uw) in [(0, 0), (1, 0), (0, 1)]:
                            last = top(
                                lambda eng, c=c, k=k, b=b, slot=slot, ux=ux,
                                uw=uw: eng.matmul(
                                    ps[:, b, :],
                                    g_xn1F2[:, ux, k, c * 128:(c + 1) * 128],
                                    wbuf[:, slot, uw, k % 2, :],
                                    start=(k == 0 and ux == 0 and uw == 0),
                                    stop=(k == 7 and uw == 1)),
                                waits=[wts[kg], t_xn1F] + ew)
                    if k % 2 == 1:
                        wbuf_free[slot] = last
                    ew = []
                for c in range(4):
                    b = 2 + c
                    ev = sop(lambda eng, c=c, b=b, half=half, dst3=dst3:
                             eng.activation(
                                 dst3[:, c, half * 512:(half + 1) * 512],
                                 ps[:, b, :], AF.Identity),
                             waits=[last] + evw)
                    evw = []
                    PS.done(b, ev)
            return S.tok("s")

        t_k = proj(1, g_kt, wwait("G4"))
        for c in range(4):
            top(lambda eng, c=c: eng.matmul(
                ps[:, 3, c * 16:c * 16 + 16], utri[:], bt_ln[:, c, :],
                start=True, stop=True),
                waits=[t_lnb] + PS.use(3))
        t_cum = S.tok("t")
        war_add("bt_ln", t_cum)
        sop(lambda eng: eng.activation(
            bt_P[:].rearrange("p c h -> p (c h)"), ps[:, 3, 0:64], AF.Exp),
            waits=[t_cum] + wwait("bt_P"))
        t_p = S.tok("s")
        PS.done(3, t_p)
        vop(lambda eng: eng.reciprocal(
            bt_Pi[:].rearrange("p c h -> p (c h)"),
            bt_P[:].rearrange("p c h -> p (c h)")), waits=[t_p])
        t_pi = S.tok("v")
        t_zp = S.dma("sync", pcrow[:, :].rearrange("q (c h) -> q c h", h=16),
                     bt_P[127:128, :, :], waits=[t_p] + wwait("pcrow"),
                     sem="d_zp")
        war_add("bt_P", t_zp)
        dump("pp", bt_P[:].rearrange("p c h -> p (c h)"), [t_p], "bt_P")
        dump("ppi", bt_Pi[:].rearrange("p c h -> p (c h)"), [t_pi])
        S.mark(f"tile{gti}_beta")

        scrA = G[6][:, 0:2048].rearrange("p (c h e) -> p c h e", h=H, e=32)
        scrB = G[6][:, 2048:4096].rearrange("p (c h e) -> p c h e", h=H, e=32)

        def rope(g4, dep):
            lo = g4[:, :, :, 0:32]
            hi = g4[:, :, :, 32:64]
            cosb = costab[:, ti * 4:ti * 4 + 4, :].unsqueeze(2).broadcast_to(
                [128, 4, H, 32])
            sinb = sintab[:, ti * 4:ti * 4 + 4, :].unsqueeze(2).broadcast_to(
                [128, 4, H, 32])
            vop(lambda eng: eng.tensor_tensor(scrA[:], lo, cosb, ALU.mult),
                waits=[dep, t_pre] + wwait("G6"))
            vop(lambda eng: eng.tensor_tensor(scrB[:], lo, sinb, ALU.mult))
            vop(lambda eng: eng.tensor_tensor(lo, hi, sinb, ALU.mult))
            vop(lambda eng: eng.tensor_tensor(hi, hi, cosb, ALU.mult))
            vop(lambda eng: eng.tensor_tensor(lo, scrA[:], lo, ALU.subtract))
            return vop(lambda eng: eng.tensor_tensor(hi, hi, scrB[:], ALU.add))

        def phi(gflat, scr, dep):
            vop(lambda eng, g=gflat: eng.tensor_scalar(
                scr, g, 0.0, None, ALU.min), waits=[dep])
            tv = S.tok("v")
            te = sop(lambda eng, scr=scr: eng.activation(scr, scr, AF.Exp),
                     waits=[tv])
            vop(lambda eng, g=gflat: eng.tensor_scalar(
                g, g, 0.0, None, ALU.max))
            return vop(lambda eng, g=gflat, scr=scr: eng.tensor_tensor(
                g, g, scr, ALU.add), waits=[te])

        def rope_c(g4, c, dep, scr2, extra=()):
            sA = scr2[:, 0:512].rearrange("p (h e) -> p h e", e=32)
            sB = scr2[:, 512:1024].rearrange("p (h e) -> p h e", e=32)
            lo = g4[:, c, :, 0:32]
            hi = g4[:, c, :, 32:64]
            cosb = costab[:, ti * 4 + c, :].unsqueeze(1).broadcast_to(
                [128, H, 32])
            sinb = sintab[:, ti * 4 + c, :].unsqueeze(1).broadcast_to(
                [128, H, 32])
            vop(lambda eng: eng.tensor_tensor(sA, lo, cosb, ALU.mult),
                waits=[dep, t_pre] + list(extra))
            vop(lambda eng: eng.tensor_tensor(sB, lo, sinb, ALU.mult))
            vop(lambda eng: eng.tensor_tensor(lo, hi, sinb, ALU.mult))
            vop(lambda eng: eng.tensor_tensor(hi, hi, cosb, ALU.mult))
            vop(lambda eng: eng.tensor_tensor(lo, sA, lo, ALU.subtract))
            return vop(lambda eng: eng.tensor_tensor(hi, hi, sB, ALU.add))

        S.mark(f"tile{gti}_qk")
        t_kr = rope(g_kt4, t_k)
        t_kp = phi(G[4][:], G[6][:], t_kr)
        t_ktd = vop(lambda eng: eng.tensor_tensor(
            g_kt4[:, :, :, :], g_kt4[:, :, :, :],
            bt_Pi[:].unsqueeze(3).broadcast_to([128, 4, H, D]), ALU.mult),
            waits=[t_kp, t_pi])
        t_q = proj(0, g_qt, wwait("G3"))
        t_pcmm = top(lambda eng: eng.matmul(
            ps[:, 3, 0:64], ones1[:], pcrow[:, :], start=True, stop=True),
            waits=[t_zp, t_ones] + PS.use(3))
        t_pc = vop(lambda eng: eng.tensor_copy(
            bt_PC[:].rearrange("p c h -> p (c h)"), ps[:, 3, 0:64]),
            waits=[t_pcmm] + wwait("bt_PC"))
        PS.done(3, t_pc)
        war_add("pcrow", t_pcmm)
        t_qp_c = [None] * 4
        tq_r = rope_c(g_qt4, 0, t_q, G[6][:, 0:1024])
        t_qp_c[0] = phi(g_qt[:, 0, :], G[6][:, 1024:2048], tq_r)
        dump("kt", G[4][:], [t_ktd], "G4")

        S.mark(f"tile{gti}_ropephi")
        if ti == 0:
            t_s0 = vop(lambda eng: eng.memset(Sdup[:, :, :], 0.0),
                       waits=wwait("Sdup"))
            t_z0 = vop(lambda eng: eng.memset(z0row[:, :], 0.0),
                       waits=wwait("z0row"))
        else:
            t_s0 = war.get("Sdup_ready", [S.tok("v")])[0]
            t_z0 = war.get("z0_ready", [S.tok("v")])[0]
        t_supd = t_s0

        def transF(src3, dstF, dep, b0, warname):
            evs = []
            for half in range(2):
                b = b0 + half
                last = None
                for j in range(4):
                    f = half * 4 + j
                    last = top(lambda eng, f=f, b=b, j=j, src3=src3:
                               eng.transpose(
                                   ps[:, b, j * 128:(j + 1) * 128],
                                   src3[:, f * 128:(f + 1) * 128], iden[:]),
                               waits=[dep] + PS.use(b))
                ev = sop(lambda eng, half=half, b=b, dstF=dstF:
                         eng.activation(
                             dstF[:, half * 4:half * 4 + 4, :],
                             ps[:, b, :].rearrange("p (j n) -> p j n", j=4),
                             AF.Identity),
                         waits=[last] + wwait(warname))
                PS.done(b, ev)
                evs.append(ev)
            return evs[-1]

        def prep_kcz_z(c):
            nonlocal t_z0
            kcz = []
            for hh in range(2):
                bb_ = 6 + hh
                top(lambda eng, c=c, hh=hh, bb_=bb_: eng.matmul(
                    ps[:, bb_, :], utri[:],
                    g_kt[:, c, hh * 512:(hh + 1) * 512],
                    start=True, stop=False),
                    waits=[t_ktd, t_z0, t_ones] + PS.use(bb_))
                m = top(lambda eng, hh=hh, bb_=bb_: eng.matmul(
                    ps[:, bb_, :], ones1[:],
                    z0row[0:1, hh * 512:(hh + 1) * 512],
                    start=False, stop=True))
                kcz.append(m)
            war_add("z0row", kcz[-1])
            zmm = []
            for hh in range(2):
                bb_ = hh
                m = top(lambda eng, c=c, hh=hh, bb_=bb_: eng.matmul(
                    ps[0:1, bb_, :], ones_col[:],
                    g_kt[:, c, hh * 512:(hh + 1) * 512],
                    start=True, stop=True),
                    waits=[t_ktd, t_ones] + PS.use(bb_))
                zmm.append(m)
            vop(lambda eng: eng.tensor_tensor(
                z0row[0:1, :],
                ps[0:1, 0:2, :].rearrange("q b n -> q (b n)"),
                z0row[0:1, :], ALU.add),
                waits=[zmm[0], zmm[1], t_zp] + wwait("z0row"))
            t_z0 = vop(lambda eng, c=c: eng.tensor_tensor(
                z0row[0:1, :].rearrange("q (h e) -> q h e", e=64),
                z0row[0:1, :].rearrange("q (h e) -> q h e", e=64),
                pcrow[0:1, c * 16:(c + 1) * 16].unsqueeze(2)
                .broadcast_to([1, 16, 64]), ALU.mult))
            PS.done(0, t_z0)
            PS.done(1, t_z0)
            return kcz

        def prep_den(c, kcz):
            tden0 = vop(lambda eng, c=c: eng.tensor_tensor(
                rs1[:, 0:512], g_qt[:, c, 0:512], ps[:, 6, :], ALU.mult),
                waits=[kcz[0], t_qp_c[c]] + wwait("rs1"))
            tden1 = vop(lambda eng, c=c: eng.tensor_tensor(
                rs1[:, 512:1024], g_qt[:, c, 512:1024], ps[:, 7, :],
                ALU.mult), waits=[kcz[1]])
            PS.done(6, tden0)
            PS.done(7, tden1)
            vop(lambda eng: eng.tensor_reduce(
                den_s[:, :],
                rs1[:, 0:1024].rearrange("p (h e) -> p h e", e=64),
                AX.X, ALU.add))
            vop(lambda eng, c=c: eng.tensor_tensor(
                den_s[:, :], den_s[:, :], bt_P[:, c, :], ALU.mult))
            vop(lambda eng: eng.tensor_scalar(
                den_s[:, :], den_s[:, :], float(EPS), None, ALU.add))
            vop(lambda eng: eng.reciprocal(drcp[:, :], den_s[:, :]))
            tq = vop(lambda eng, c=c: eng.tensor_tensor(
                qsc[:, :], bt_P[:, c, :], drcp[:, :], ALU.mult))
            if DBG and c == 0:
                dump("den", den_s[:, :], [tq])
            t_qhat = vop(lambda eng, c=c: eng.tensor_tensor(
                g_qt4[:, c], g_qt4[:, c],
                qsc[:].unsqueeze(2).broadcast_to([128, H, D]), ALU.mult),
                waits=[tq] + wwait("G3"))
            return t_qhat

        def scores_all(t_qF, t_kF):
            toksA, toksB = [], []
            for par in range(2):
                for h in range(par, 16, 2):
                    b0 = 0 if h < 8 else 2
                    hi_ = h % 8
                    bb_ = b0 + hi_ // 4
                    part0 = par * 64
                    extra = ([toksA[-1] if not toksB else toksB[-1]]
                             if (par == 1 and h == 1) else [])
                    m = top(lambda eng, h=h, bb_=bb_, hi_=hi_, part0=part0:
                            eng.matmul(
                                ps[:, bb_,
                                   (hi_ % 4) * 128:(hi_ % 4) * 128 + 128],
                                kF[part0:part0 + 64, h // 2, :],
                                qF[part0:part0 + 64, h // 2, :],
                                start=True, stop=True),
                            waits=[t_qF, t_kF] + PS.use(bb_) + extra)
                    (toksA if h < 8 else toksB).append(m)
            return toksA, toksB

        def mask_evac(b0, waits):
            ev = vop(lambda eng, b0=b0: eng.tensor_tensor(
                AT[:, :, :],
                ps[:, b0:b0 + 2, :].rearrange("p b n -> p (b n)")
                .rearrange("p (h n) -> p h n", n=128),
                utri[:].unsqueeze(1).broadcast_to([128, 8, 128]),
                ALU.mult), waits=waits + wwait("AT"))
            PS.done(b0, ev)
            PS.done(b0 + 1, ev)
            return ev

        def av_inter(c, h0, bk, evt):
            toks = []
            for p_ in range(4):
                he = h0 + 2 * p_
                ho = he + 1
                fc = (h0 // 2) + p_
                sl = slice(p_ * 128, p_ * 128 + 128)
                top(lambda eng, he=he, p_=p_, bk=bk, sl=sl, c=c:
                    eng.matmul(ps[0:64, bk, sl],
                               g_vt4[:, c, he, :], AT[:, 2 * p_, :],
                               start=True, stop=False),
                    waits=[evt, t_v, t_supd]
                    + (PS.use(bk) if p_ == 0 else []))
                t1 = top(lambda eng, he=he, fc=fc, bk=bk, sl=sl:
                         eng.matmul(ps[0:64, bk, sl],
                                    Sdup[0:64, fc, :], qF[0:64, fc, :],
                                    start=False, stop=True))
                top(lambda eng, ho=ho, p_=p_, bk=bk, sl=sl, c=c:
                    eng.matmul(ps[64:128, bk, sl],
                               g_vt4[:, c, ho, :], AT[:, 2 * p_ + 1, :],
                               start=True, stop=False,
                               tile_position=(0, 64)))
                t2 = top(lambda eng, ho=ho, fc=fc, bk=bk, sl=sl:
                         eng.matmul(ps[64:128, bk, sl],
                                    Sdup[64:128, fc, :], qF[64:128, fc, :],
                                    start=False, stop=True,
                                    tile_position=(64, 64)))
                toks += [t1, t2]
            th = sop(lambda eng, h0=h0, bk=bk, c=c: eng.activation(
                g_attnF2[:, 0, (h0 // 2):(h0 // 2) + 4,
                         c * 128:(c + 1) * 128],
                ps[:, bk, :].rearrange("p (j n) -> p j n", j=4),
                AF.Identity),
                waits=[toks[-1]])
            ev = vop(lambda eng, h0=h0, bk=bk, c=c: eng.tensor_tensor(
                g_attnF2[:, 1, (h0 // 2):(h0 // 2) + 4,
                         c * 128:(c + 1) * 128],
                ps[:, bk, :].rearrange("p (j n) -> p j n", j=4),
                g_attnF2[:, 0, (h0 // 2):(h0 // 2) + 4,
                         c * 128:(c + 1) * 128], ALU.subtract),
                waits=[th])
            PS.done(bk, ev)
            return ev, toks

        kcz_n = prep_kcz_z(0)
        t_kF = transF(g_kt[:, 0, :], kF[:], t_ktd, 2, "kF")
        t_v = proj(2, G[5][:].rearrange("p (c f) -> p c f", f=HID),
                   wwait("G5"))
        war_add("G2", wbuf_free[0])
        war_add("G2", wbuf_free[1])
        dump("vt", G[5][:], [t_v], "G5")
        t_qhat = prep_den(0, kcz_n)
        t_qF = transF(g_qt[:, 0, :], qF[:], t_qhat, 0, "qF")
        for c in range(4):
            S.mark(f"tile{gti}_chunk{c}")
            mmA, mmB = scores_all(t_qF, t_kF)
            evA = mask_evac(0, [mmA[-1], mmB[-1]])
            if DBG and c == 0:
                dump("at0", AT[:, :, :], [evA], "AT")
            if c < 3:
                kcz_n = prep_kcz_z(c + 1)
            evAVa, mmAVa = av_inter(c, 0, 4, evA)
            war_add("AT", mmAVa[-1])
            evB = mask_evac(2, [mmB[-1]])
            if c < 3:
                tq_r = rope_c(g_qt4, c + 1, t_q, G[7][:, 0:1024],
                              extra=(wwait("G7") if c == 0 else []))
                t_qp_c[c + 1] = phi(g_qt[:, c + 1, :], G[7][:, 1024:2048],
                                    tq_r)
                t_qhat = prep_den(c + 1, kcz_n)
                t_kF = transF(g_kt[:, c + 1, :], kF[:], t_ktd, 2, "kF")
            evAVb, mmAVb = av_inter(c, 8, 5, evB)
            war_add("AT", mmAVb[-1])
            war_add("qF", mmAVa[-1])
            war_add("qF", mmAVb[-1])
            war_add("kF", mmA[-1])
            war_add("kF", mmB[-1])

            ds_toks = []
            for h in range(16):
                sl = slice((h // 2) * 64, (h // 2) * 64 + 64)
                if h % 2 == 0:
                    m = top(lambda eng, h=h, sl=sl, c=c: eng.matmul(
                        ps[0:64, 4, sl], g_kt4[:, c, h, :], g_vt4[:, c, h, :],
                        start=True, stop=True),
                        waits=[t_ktd, t_v] + (PS.use(4) if h == 0 else []))
                else:
                    m = top(lambda eng, h=h, sl=sl, c=c: eng.matmul(
                        ps[64:128, 4, sl], g_kt4[:, c, h, :],
                        g_vt4[:, c, h, :],
                        start=True, stop=True, tile_position=(0, 64)))
                ds_toks.append(m)
            vop(lambda eng: eng.tensor_tensor(
                rs1[:, 0:512],
                Sdup[:].rearrange("p j e -> p (j e)"),
                ps[:, 4, :], ALU.add),
                waits=[ds_toks[-1], mmAVa[-1], mmAVb[-1]] + wwait("rs1"))
            rs3 = rs1[:, 0:512].rearrange("p (j e) -> p j e", e=64)
            pce = bt_PC[:, c, :].rearrange("p (j two) -> p j two", two=2)
            vop(lambda eng, rs3=rs3, pce=pce: eng.tensor_tensor(
                Sdup[0:64, :, :], rs3[0:64],
                pce[0:64, :, 0].unsqueeze(2).broadcast_to([64, 8, 64]),
                ALU.mult))
            t_supd = vop(lambda eng, rs3=rs3, pce=pce: eng.tensor_tensor(
                Sdup[64:128, :, :], rs3[64:128],
                pce[64:128, :, 1].unsqueeze(2).broadcast_to([64, 8, 64]),
                ALU.mult))
            war_add("rs1", t_supd)
            PS.done(4, t_supd)
            if c < 3:
                t_qF = transF(g_qt[:, c + 1, :], qF[:], t_qhat, 0, "qF")

        war["Sdup_ready"] = [t_supd]
        war["z0_ready"] = [t_z0]
        war["Sdup"] = [t_supd]
        war_add("bt_PC", t_supd)
        dump("attnF", G[6][:], [S.tok("v")], "G6")

        S.mark(f"tile{gti}_chunks")
        t_attn_done = S.tok("v")
        ew = wwait("G7")
        h_ev = [None] * 4
        for c in range(4):
            vop(lambda eng, c=c: eng.tensor_tensor(
                g_xn1[:, c], g_xn1[:, c], bo_f[:, :], ALU.add),
                waits=[t_pre])
        t_xn1b = S.tok("v")
        for half in range(2):
            wts = [None] * 4
            pw = []
            for c in range(4):
                pw += PS.use(2 + c)
            last = None
            for k in range(8):
                kg = k // 2
                slot = kg % 2
                if k % 2 == 0:
                    wts[kg] = S.dma(
                        "sync", wbuf[:, slot, :, :, :],
                        whl_d[3, half, :, :, kg * 2:(kg + 1) * 2, :],
                        waits=[wbuf_free[slot], S.tok("d_pre")],
                        sem=f"d_w{slot}")
                for c in range(4):
                    b = 2 + c
                    for (ux, uw) in [(0, 0), (1, 0), (0, 1)]:
                        last = top(
                            lambda eng, c=c, k=k, b=b, slot=slot, ux=ux,
                            uw=uw: eng.matmul(
                                ps[:, b, :],
                                g_attnF2[:, ux, k, c * 128:(c + 1) * 128],
                                wbuf[:, slot, uw, k % 2, :],
                                start=(k == 0 and ux == 0 and uw == 0),
                                stop=(k == 7 and uw == 1)),
                            waits=[wts[kg], t_attn_done] + pw)
                if k % 2 == 1:
                    wbuf_free[slot] = last
                pw = []
            for c in range(4):
                b = 2 + c
                ev = vop(lambda eng, c=c, b=b, half=half:
                         eng.scalar_tensor_tensor(
                             g_h[:, c, half * 512:(half + 1) * 512],
                             ps[:, b, :], 1.0,
                             g_xn1[:, c, half * 512:(half + 1) * 512],
                             ALU.mult, ALU.add),
                         waits=[last, t_xn1b] + (ew if (c == 0 and half == 0)
                                                 else []))
                PS.done(b, ev)
                if half == 1:
                    h_ev[c] = ev
        war_add("G6", wbuf_free[0])
        war_add("G6", wbuf_free[1])
        t_h = S.tok("v")
        dump("h", G[7][:], [t_h], "G7")

        S.mark(f"tile{gti}_wo")
        stw = wwait("st")
        g_ab = G[3][:].rearrange("p (c f) -> p c f", f=HID)
        g_x1 = G[4][:].rearrange("p (c f) -> p c f", f=HID)
        g_xn2 = G[5][:].rearrange("p (c f) -> p c f", f=HID)
        g_xn2F2 = G[2][:].bitcast(BF16).rearrange("p (u k n) -> p u k n",
                                                  u=2, k=8, n=TILE)
        rsw = wwait("rs1")
        g3w = wwait("G3")
        g4w = wwait("G4")
        g5w = wwait("G5")
        g2w = wwait("G2")
        for c in range(4):
            cs = slice(c, c + 1)
            sop(lambda eng, c=c: eng.activation(
                sdump[:, :], g_h[:, c], AF.Square,
                accum_out=st_sq[:, c:c + 1]),
                waits=[h_ev[c]] + stw + rsw)
            t_rms = rms_finalize(cs, c_rmseps, S.tok("s"))
            t_ab = vop(lambda eng, c=c: eng.scalar_tensor_tensor(
                g_ab[:, c], g_h[:, c], st_r[:, c:c + 1], scale_f[:, :],
                ALU.mult, ALU.mult, accum_out=st_sum[:, c:c + 1]),
                waits=[t_rms, t_pre] + (g3w if c == 0 else []))
            sop(lambda eng, c=c: eng.activation(
                sdump[:, :], g_ab[:, c], AF.Square,
                accum_out=st_sq[:, c:c + 1]), waits=[t_ab])
            t_ln2 = ln_finalize(cs, c_lneps, S.tok("s"), t_ab)
            vop(lambda eng, c=c: eng.tensor_scalar(
                rs1[:, 0:1024], g_ab[:, c], st_mu[:, c:c + 1],
                st_r[:, c:c + 1], ALU.subtract, ALU.mult),
                waits=[t_ln2])
            t_x1 = vop(lambda eng, c=c: eng.scalar_tensor_tensor(
                g_x1[:, c], rs1[:, 0:1024], 1.0, g_x[:, c],
                ALU.mult, ALU.add, accum_out=st_sum[:, c:c + 1]),
                waits=(g4w if c == 0 else []))
            if c == 3:
                t_x1tok = t_x1
            sop(lambda eng, c=c: eng.activation(
                sdump[:, :], g_x1[:, c], AF.Square,
                accum_out=st_sq[:, c:c + 1]), waits=[t_x1])
            t_ln3 = ln_finalize(cs, c_lneps, S.tok("s"), t_x1)
            t_xn2 = vop(lambda eng, c=c: eng.tensor_scalar(
                g_xn2[:, c], g_x1[:, c], st_mu[:, c:c + 1], st_r[:, c:c + 1],
                ALU.subtract, ALU.mult),
                waits=[t_ln3] + (g5w if c == 0 else []))
            for half in range(2):
                b = half
                last = None
                for j in range(4):
                    f = half * 4 + j
                    last = top(lambda eng, c=c, f=f, b=b, j=j: eng.transpose(
                        ps[:, b, j * 128:(j + 1) * 128],
                        g_xn2[:, c, f * 128:(f + 1) * 128], iden[:]),
                        waits=[t_xn2] + PS.use(b))
                th = sop(lambda eng, c=c, half=half, b=b: eng.activation(
                    g_xn2F2[:, 0, half * 4:half * 4 + 4,
                            c * 128:(c + 1) * 128],
                    ps[:, b, :].rearrange("p (j n) -> p j n", j=4),
                    AF.Identity),
                    waits=[last] + (g2w if (c == 0 and half == 0) else []))
                ev = vop(lambda eng, c=c, half=half, b=b: eng.tensor_tensor(
                    g_xn2F2[:, 1, half * 4:half * 4 + 4,
                            c * 128:(c + 1) * 128],
                    ps[:, b, :].rearrange("p (j n) -> p j n", j=4),
                    g_xn2F2[:, 0, half * 4:half * 4 + 4,
                            c * 128:(c + 1) * 128], ALU.subtract),
                    waits=[th])
                PS.done(b, ev)
        war_add("rs1", S.tok("v"))
        war_add("st", S.tok("v"))
        t_xn2F = S.tok("v")
        dump("ab", G[3][:], [S.tok("v")], "G3")
        dump("x1", G[4][:], [S.tok("v")], "G4")
        dump("xn2", G[5][:], [S.tok("v")], "G5")

        S.mark(f"tile{gti}_xn2F")
        use_table(SILU)
        hm_toks = [None] * 24

        def hm_view(m, u):
            gi = 1 + (0 if m < 8 else (5 if m < 16 else 6))
            return G[gi][:].bitcast(BF16).rearrange(
                "p (v j n) -> p v j n", v=2, n=TILE)[:, u, m % 8, :]

        for m in range(24):
            slot = m % 2
            bg = (m % 2) * 2
            bu = bg + 1
            twg = S.dma("sync", wgbuf[:, slot, :, :], wgb_d[m, :, :, :],
                        waits=[wgbuf_free[slot], S.tok("d_pre")],
                        sem=f"d_g{slot}")
            twu = S.dma("sync", wubuf[:, slot, :, :], wub_d[m, :, :, :],
                        waits=[wubuf_free[slot], S.tok("d_pre")],
                        sem=f"d_u{slot}")
            lastg = lastu = None
            for k in range(8):
                for u in range(2):
                    lastg = top(lambda eng, k=k, u=u, bg=bg, slot=slot:
                                eng.matmul(
                                    ps[:, bg, :], wgbuf[:, slot, k, :],
                                    g_xn2F2[:, u, k, :],
                                    start=(k == 0 and u == 0),
                                    stop=(k == 7 and u == 1)),
                                waits=[twg, t_xn2F] + PS.use(bg))
            for k in range(8):
                for u in range(2):
                    lastu = top(lambda eng, k=k, u=u, bu=bu, slot=slot:
                                eng.matmul(
                                    ps[:, bu, :], wubuf[:, slot, k, :],
                                    g_xn2F2[:, u, k, :],
                                    start=(k == 0 and u == 0),
                                    stop=(k == 7 and u == 1)),
                                waits=[twu] + PS.use(bu))
            wgbuf_free[slot] = lastg
            wubuf_free[slot] = lastu
            tsil = sop(lambda eng, bg=bg: eng.activation(
                sdump[:, 0:512], ps[:, bg, :], AF.Silu, scale=c_s127),
                waits=[lastg, t_ones] + wwait("rs1"))
            PS.done(bg, tsil)
            warkey = "G" + str(1 + (0 if m < 8 else (5 if m < 16 else 6)))
            vop(lambda eng, bu=bu: eng.tensor_tensor(
                rs1[:, 512:1024], sdump[:, 0:512], ps[:, bu, :], ALU.mult),
                waits=[tsil, lastu])
            vop(lambda eng, m=m: eng.tensor_copy(
                hm_view(m, 0), rs1[:, 512:1024]),
                waits=(wwait(warkey) if m % 8 == 0 else []))
            tm = vop(lambda eng, m=m: eng.tensor_tensor(
                hm_view(m, 1), rs1[:, 512:1024], hm_view(m, 0), ALU.subtract))
            war_add("rs1", tm)
            PS.done(bu, tm)
            hm_toks[m] = tm

        war_add("G2", max(wgbuf_free, key=lambda t: t[1]))
        war_add("G2", max(wubuf_free, key=lambda t: t[1]))
        if gti + 1 < len(seq_tiles):
            emit_ln1(gti + 1, [t_x1tok])

        S.mark(f"tile{gti}_gateup")
        dn_last = {}
        for mk in range(24):
            for half in range(2):
                slot = (mk * 2 + half) % 4
                twd = S.dma("sync", wdbuf[:, slot, :],
                            wdb_d[mk, :, half * 512:(half + 1) * 512],
                            waits=[wdbuf_free[slot], S.tok("d_pre")],
                            sem=f"d_d{slot}")
                last = None
                for c in range(4):
                    b = c * 2 + half
                    for u in range(2):
                        last = top(lambda eng, mk=mk, c=c, u=u, b=b, slot=slot:
                                   eng.matmul(
                                       ps[:, b, :],
                                       hm_view(mk, u)[:, c * 128:(c + 1) * 128],
                                       wdbuf[:, slot, :],
                                       start=(mk == 0 and u == 0),
                                       stop=(mk == 23 and u == 1)),
                                   waits=([twd, hm_toks[mk]]
                                          + (PS.use(b) if mk == 0 else [])))
                    dn_last[(c, half)] = last
                wdbuf_free[slot] = last
        g_f = G[3][:].rearrange("p (c f) -> p c f", f=HID)
        g3w2 = wwait("G3")
        for c in range(4):
            evs = sop(lambda eng, c=c: eng.activation(
                g_f[:, c, 0:512], ps[:, c * 2, :], AF.Identity,
                accum_out=st_sum[:, c:c + 1]),
                waits=[dn_last[(c, 0)]] + (g3w2 if c == 0 else []))
            PS.done(c * 2, evs)
            evv = vop(lambda eng, c=c: eng.tensor_scalar(
                g_f[:, c, 512:1024], ps[:, c * 2 + 1, :],
                0.0, 0.0, ALU.add, ALU.add,
                accum_out=st_t2[:, c:c + 1]),
                waits=[dn_last[(c, 1)]] + (g3w2 if c == 0 else []))
            PS.done(c * 2 + 1, evv)
        t_sevac = S.tok("s")
        tvf = vop(lambda eng: eng.tensor_tensor(
            st_sum[:, 0:4], st_sum[:, 0:4], st_t2[:, 0:4], ALU.add),
            waits=[t_sevac])
        if gti + 1 < len(seq_tiles):
            emit_front(gti + 1)
        for c in range(4):
            sop(lambda eng, c=c: eng.activation(
                sdump[:, :], g_f[:, c], AF.Square,
                accum_out=st_sq[:, c:c + 1]), waits=[tvf] + wwait("rs1"))
        use_table(LN_EXP)
        t_ln4 = ln_finalize(slice(0, 4), c_lneps, S.tok("s"), tvf)
        dump("fout", G[3][:], [S.tok("v"), tvf], "G3")

        g_out = G[3][:].rearrange("p (c f) -> p c f", f=HID)
        for c in range(4):
            vop(lambda eng, c=c: eng.tensor_scalar(
                rs1[:, 0:1024], g_f[:, c], st_mu[:, c:c + 1], st_r[:, c:c + 1],
                ALU.subtract, ALU.mult), waits=[t_ln4] + wwait("rs1"))
            vop(lambda eng, c=c: eng.tensor_tensor(
                g_out[:, c], rs1[:, 0:1024], g_x1[:, c], ALU.add),
                waits=(wwait("G3") if c == 0 else []))
        t_out = S.tok("v")
        war_add("rs1", t_out)
        war_add("st", t_out)
        t_odma = S.op("gpsimd",
                      (lambda o, i: (lambda eng: eng.dma_start(out=o, in_=i)))(
                          outd[tok0:tok0 + TILE, :]
                          .rearrange("(c p) f -> p c f", p=128),
                          g_out[:, :, :]),
                      waits=[t_out], inc="d_out", inc_n=16)
        war_add("G3", t_odma)
        war_add("G3", S.tok("s"))
        war_add("G3", t_out)
        war_add("G1", wgbuf_free[0])
        war_add("G4", t_out)
        war_add("G5", t_xn2F)
        for sl_ in range(4):
            war_add("G6", wdbuf_free[sl_])
            war_add("G7", wdbuf_free[sl_])
            war_add("G1", wdbuf_free[sl_])

    if max_items is not None:
        S.items = S.items[:max_items]
    else:
        fin = [("d_out", S.ctr.get("d_out", 0))]
        if dbg:
            fin.append(("d_dbg", S.ctr.get("d_dbg", 0)))
        S.op("gpsimd", lambda eng: None, waits=fin)

    sem_names = ["d_pre", "d_ff0", "d_x", "d_zp", "d_w0", "d_w1", "d_g0",
                 "d_g1", "d_u0", "d_u1", "d_d0", "d_d1", "d_d2", "d_d3",
                 "d_out", "d_dbg",
                 "t", "v", "s", "g"]
    with ctx:
        sems = {}
        with ExitStack() as sctx:
            for nm in sem_names:
                sems[nm] = sctx.enter_context(nc.semaphore(nm))
            with nc.Block() as block:
                @block.sync
                def _(eng):
                    S.replay("sync", eng, sems)

                @block.tensor
                def _(eng):
                    S.replay("tensor", eng, sems)

                @block.vector
                def _(eng):
                    S.replay("vector", eng, sems)

                @block.scalar
                def _(eng):
                    S.replay("scalar", eng, sems)

                @block.gpsimd
                def _(eng):
                    S.replay("gpsimd", eng, sems)

    nc._sched_marks = S.marks
    return nc


def prep_weights(Wq, Wk, Wv, Wb, bb, Wo, bo, scale, w_gate, w_up, w_down):
    def proj_layout(W):
        WT = np.ascontiguousarray(W.T)
        return np.ascontiguousarray(
            WT.reshape(8, 128, 2, 512).transpose(2, 1, 0, 3))

    out = {
        "wq": proj_layout(Wq), "wk": proj_layout(Wk), "wv": proj_layout(Wv),
        "wo": proj_layout(Wo),
        "wb": np.ascontiguousarray(Wb.T.reshape(8, 128, 16).transpose(1, 0, 2)),
        "bb": np.ascontiguousarray(bb.reshape(1, 16)),
        "bo": np.ascontiguousarray(bo.reshape(1, HID)),
        "bo_full": np.ascontiguousarray(
            np.broadcast_to(bo.reshape(1, HID), (128, HID))),
        "lnscale": np.ascontiguousarray(
            np.broadcast_to(scale.reshape(1, HID), (128, HID))),
    }
    wg8 = w_gate.astype(np.int8)
    wu8 = w_up.astype(np.int8)
    wd8 = w_down.astype(np.int8)
    assert np.array_equal(wg8.astype(np.int32), w_gate)
    out["wg"] = np.ascontiguousarray(
        wg8.T.reshape(8, 128, 24, 128).transpose(2, 1, 0, 3))
    out["wu"] = np.ascontiguousarray(
        wu8.T.reshape(8, 128, 24, 128).transpose(2, 1, 0, 3))
    out["wd"] = np.ascontiguousarray(wd8.T.reshape(24, 128, HID))
    out["utri"] = np.triu(np.ones((128, 128), np.float32))
    out["iden"] = np.eye(128, dtype=np.float32)
    inv = 1.0 / 10000.0 ** (np.arange(0, D, 2, dtype=np.float64) / D)
    t = np.arange(2048, dtype=np.float64)
    fr = t[:, None] * inv[None, :]
    out["costab"] = np.ascontiguousarray(
        np.cos(fr).astype(np.float32).reshape(16, 128, 32).transpose(1, 0, 2))
    out["sintab"] = np.ascontiguousarray(
        np.sin(fr).astype(np.float32).reshape(16, 128, 32).transpose(1, 0, 2))
    return out


_NC_CACHE = {}


def kernel(L0_attn_x, L0_attn_Wq, L0_attn_Wk, L0_attn_Wv, L0_attn_Wb,
           L0_attn_bb, L0_attn_Wo, L0_attn_bo, L0_attn_scale,
           L0_ffn_w_gate, L0_ffn_w_up, L0_ffn_w_down):
    from concourse.bass_utils import run_bass_kernel_spmd

    x = np.asarray(L0_attn_x, dtype=np.float32)
    B, T, _ = x.shape
    n_cores = 8
    per = B // n_cores
    if "nc" not in _NC_CACHE:
        _NC_CACHE["nc"] = build_core_program(n_seq=per, tiles_per_seq=T // TILE,
                                             dbg=False)
    nc = _NC_CACHE["nc"]
    wmaps = prep_weights(np.asarray(L0_attn_Wq), np.asarray(L0_attn_Wk),
                         np.asarray(L0_attn_Wv), np.asarray(L0_attn_Wb),
                         np.asarray(L0_attn_bb), np.asarray(L0_attn_Wo),
                         np.asarray(L0_attn_bo), np.asarray(L0_attn_scale),
                         np.asarray(L0_ffn_w_gate), np.asarray(L0_ffn_w_up),
                         np.asarray(L0_ffn_w_down))
    in_maps = []
    for i in range(n_cores):
        m = dict(wmaps)
        m["x"] = np.ascontiguousarray(
            x[i * per:(i + 1) * per].reshape(per * T, HID))
        in_maps.append(m)
    res = run_bass_kernel_spmd(nc, in_maps, core_ids=list(range(n_cores)))
    out = np.concatenate([r["out"].reshape(per, T, HID) for r in res.results],
                         axis=0)
    return out


# version 14
# speedup vs baseline: 1.1030x; 1.0006x over previous
import numpy as np
from contextlib import ExitStack

import concourse.bass as bass
import concourse.mybir as mybir

F32 = mybir.dt.float32
BF16 = mybir.dt.bfloat16
I8 = mybir.dt.int8
AF = mybir.ActivationFunctionType
ALU = mybir.AluOpType
AX = mybir.AxisListType

H, D = 16, 64
HID = 1024
FF = 3072
TILE = 512
EPS = 1e-6
LNEPS = 1e-5
SCALE127 = float(np.float32(1.0 / 127.0))


class Sched:
    ENGINES = ("sync", "tensor", "vector", "scalar", "gpsimd")

    def __init__(self):
        self.items = []
        self.ctr = {}
        self.last_wait = {}
        self.marks = []

    def mark(self, name):
        self.marks.append((name, len(self.items)))

    def tok(self, sem):
        return (sem, self.ctr.get(sem, 0))

    def op(self, engine, fn, waits=(), inc=None, inc_n=1):
        assert engine in self.ENGINES
        w = []
        for t in waits:
            if t is None:
                continue
            (sem, val) = t
            if val <= 0:
                continue
            assert val <= self.ctr.get(sem, 0), (sem, val, self.ctr.get(sem, 0))
            key = (engine, sem)
            if self.last_wait.get(key, 0) >= val:
                continue
            self.last_wait[key] = val
            w.append((sem, val))
        tok = None
        incs = []
        if inc is not None:
            self.ctr[inc] = self.ctr.get(inc, 0) + inc_n
            tok = (inc, self.ctr[inc])
            incs.append((inc, inc_n))
        self.items.append((engine, w, fn, incs))
        return tok

    def dma(self, engine, out, in_, waits=(), sem=None):
        sem = sem or ("d_" + engine)
        return self.op(
            engine,
            (lambda o, i: (lambda eng: eng.dma_start(out=o, in_=i)))(out, in_),
            waits=waits, inc=sem, inc_n=16)

    def replay(self, engine_name, eng, sems):
        for (e, waits, fn, incs) in self.items:
            if e != engine_name:
                continue
            for (sem, val) in waits:
                eng.wait_ge(sems[sem], val)
            instr = fn(eng)
            for (sem, n) in incs:
                assert instr is not None
                instr.then_inc(sems[sem], n)


class Psum:
    def __init__(self):
        self.free_tok = [[] for _ in range(8)]

    def use(self, b):
        return list(self.free_tok[b])

    def done(self, b, toks):
        if not isinstance(toks, list):
            toks = [toks]
        self.free_tok[b] = toks


def build_core_program(n_seq=4, tiles_per_seq=4, dbg=False, max_items=None):
    T_SEQ = tiles_per_seq * TILE
    NTOK = n_seq * T_SEQ
    nc = bass.Bass("TRN2", target_bir_lowering=False)
    S = Sched()
    PS = Psum()
    war = {}

    def wwait(name):
        return war.pop(name, [])

    def war_add(name, tok):
        war.setdefault(name, []).append(tok)

    xd = nc.dram_tensor("x", [NTOK, HID], F32, kind="ExternalInput").ap()
    outd = nc.dram_tensor("out", [NTOK, HID], F32, kind="ExternalOutput").ap()
    wq_d = nc.dram_tensor("wq", [2, 128, 8, 512], F32, kind="ExternalInput").ap()
    wk_d = nc.dram_tensor("wk", [2, 128, 8, 512], F32, kind="ExternalInput").ap()
    wv_d = nc.dram_tensor("wv", [2, 128, 8, 512], F32, kind="ExternalInput").ap()
    wo_d = nc.dram_tensor("wo", [2, 128, 8, 512], F32, kind="ExternalInput").ap()
    wb_d = nc.dram_tensor("wb", [128, 8, 16], F32, kind="ExternalInput").ap()
    bb_d = nc.dram_tensor("bb", [1, 16], F32, kind="ExternalInput").ap()
    bo_d = nc.dram_tensor("bo", [1, HID], F32, kind="ExternalInput").ap()
    sc_d = nc.dram_tensor("lnscale", [128, HID], F32, kind="ExternalInput").ap()
    wg_d = nc.dram_tensor("wg", [24, 128, 8, 128], I8, kind="ExternalInput").ap()
    wu_d = nc.dram_tensor("wu", [24, 128, 8, 128], I8, kind="ExternalInput").ap()
    wd_d = nc.dram_tensor("wd", [24, 128, HID], I8, kind="ExternalInput").ap()
    u_d = nc.dram_tensor("utri", [128, 128], F32, kind="ExternalInput").ap()
    bof_d = nc.dram_tensor("bo_full", [128, HID], F32, kind="ExternalInput").ap()
    whl_d = nc.dram_tensor("whl", [4, 2, 128, 2, 8, 512], BF16).ap()
    wgb_d = nc.dram_tensor("wgb", [24, 128, 8, 128], BF16).ap()
    wub_d = nc.dram_tensor("wub", [24, 128, 8, 128], BF16).ap()
    wdb_d = nc.dram_tensor("wdb", [24, 128, HID], BF16).ap()
    id_d = nc.dram_tensor("iden", [128, 128], F32, kind="ExternalInput").ap()
    cos_d = nc.dram_tensor("costab", [128, 16, 32], F32, kind="ExternalInput").ap()
    sin_d = nc.dram_tensor("sintab", [128, 16, 32], F32, kind="ExternalInput").ap()

    dbg_t = {}
    if dbg:
        for nm in ["xn1", "qt", "kt", "vt", "beta", "pp", "ppi", "den",
                   "at0", "attnF", "h", "ab", "x1", "xn2", "fout", "st"]:
            dbg_t[nm] = nc.dram_tensor("dbg_" + nm, [128, 4096], F32,
                                       kind="ExternalOutput").ap()

    ctx = ExitStack()

    def sbt(name, shape, dtype=F32):
        return ctx.enter_context(nc.sbuf_tensor(name, shape, dtype))

    G = [sbt(f"g{i}", [128, 4096]) for i in range(9)]
    wbuf = sbt("wbuf", [128, 2, 2, 2, 512], BF16)
    wgbuf = sbt("wgbuf", [128, 2, 8, 128], BF16)
    wubuf = sbt("wubuf", [128, 2, 8, 128], BF16)
    wdbuf = sbt("wdbuf", [128, 4, 512], BF16)
    wbtile = sbt("wbtile", [128, 2, 8, 16], BF16)
    bo_f = sbt("bo_f", [128, HID])
    qF = sbt("qF", [128, 8, 128])
    kF = sbt("kF", [128, 8, 128])
    AT = sbt("AT", [128, 8, 128])
    Sdup = sbt("Sdup", [128, 8, 64])
    rs1 = sbt("rs1", [128, 1024])
    sdump = rs1
    z0row = sbt("z0row", [1, HID])
    bb_row = sbt("bb_row", [1, 16])
    ones1 = sbt("ones1", [1, 128])
    scale_f = sbt("scale_f", [128, HID])
    utri = sbt("utri_s", [128, 128])
    iden = sbt("iden_s", [128, 128])
    costab = sbt("costab_s", [128, 16, 32])
    sintab = sbt("sintab_s", [128, 16, 32])
    bt_beta = sbt("bt_beta", [128, 64])
    bt_ln = sbt("bt_ln", [128, 4, 16])
    bt_P = sbt("bt_P", [128, 4, 16])
    bt_Pi = sbt("bt_Pi", [128, 4, 16])
    bt_PC = sbt("bt_PC", [128, 4, 16])
    pcrow = sbt("pcrow", [1, 64])
    ones_col = sbt("ones_col", [128, 1])
    cpool = sbt("cpool", [128, 8])
    den_s = sbt("den_s", [128, 16])
    drcp = sbt("drcp", [128, 16])
    qsc = sbt("qsc", [128, 16])
    st_sum = sbt("st_sum", [128, 8])
    st_sq = sbt("st_sq", [128, 8])
    st_mu = sbt("st_mu", [128, 8])
    st_t1 = sbt("st_t1", [128, 8])
    st_t2 = sbt("st_t2", [128, 8])
    st_r = sbt("st_r", [128, 8])

    ps = ctx.enter_context(nc.psum_tensor("ps", [128, 8, 512], F32))

    def vop(fn, waits=(), inc="v"):
        return S.op("vector", fn, waits=list(waits) + [S.tok("v")], inc=inc)

    def sop(fn, waits=(), inc="s"):
        return S.op("scalar", fn, waits=list(waits) + [S.tok("s")], inc=inc)

    def top(fn, waits=(), inc="t"):
        return S.op("tensor", fn, waits=waits, inc=inc)

    for (dst, src) in [(bb_row[:], bb_d[:, :]),
                       (bo_f[:], bof_d[:, :]), (scale_f[:], sc_d[:, :]),
                       (utri[:], u_d[:, :]), (iden[:], id_d[:, :]),
                       (costab[:], cos_d[:, :, :]), (sintab[:], sin_d[:, :, :])]:
        S.dma("sync", dst, src, sem="d_pre")
    t_pre = S.tok("d_pre")
    vop(lambda eng: eng.memset(ones1[:], 1.0))
    vop(lambda eng: eng.memset(ones_col[:], 1.0))
    for i, val in enumerate([LNEPS, EPS, -0.5, -1.0, 1.0 / HID, SCALE127]):
        vop(lambda eng, i=i, val=val: eng.memset(cpool[:, i:i + 1], float(val)))
    t_ones = S.tok("v")
    c_lneps = cpool[:, 0:1]
    c_rmseps = cpool[:, 1:2]
    c_mhalf = cpool[:, 2:3]
    c_mone = cpool[:, 3:4]
    c_invh = cpool[:, 4:5]
    c_s127 = cpool[:, 5:6]

    g0f = G[0][:]
    g1b = G[1][:].bitcast(BF16)
    t_ph0 = None
    for wi, wdram in enumerate([wq_d, wk_d, wv_d, wo_d]):
        for half in range(2):
            tin = S.dma("sync", g0f,
                        wdram[half, :, :, :].rearrange("p k n -> p (k n)"),
                        waits=[t_ph0] if t_ph0 else [t_pre], sem="d_pre")
            vop(lambda eng: eng.tensor_copy(g1b[:, 0:4096], g0f), waits=[tin])
            tv = vop(lambda eng: eng.tensor_tensor(
                g1b[:, 4096:8192], g0f, g1b[:, 0:4096], ALU.subtract))
            t_ph0 = S.dma("sync",
                          whl_d[wi, half, :, :, :, :]
                          .rearrange("p u k n -> p (u k n)"),
                          g1b[:, :], waits=[tv], sem="d_pre")
    g0b = G[0][:].bitcast(BF16)
    for (src_d, dst_d, four) in [(wg_d, wgb_d, True), (wu_d, wub_d, True),
                                 (wd_d, wdb_d, False)]:
        for m0 in range(0, 24, 8):
            if four:
                sap = src_d[m0:m0 + 8].rearrange("m p k n -> p m (k n)")
                dap = dst_d[m0:m0 + 8].rearrange("m p k n -> p m (k n)")
            else:
                sap = src_d[m0:m0 + 8].rearrange("m p n -> p m n")
                dap = dst_d[m0:m0 + 8].rearrange("m p n -> p m n")
            tin = S.dma("gpsimd", g0b[:, :], sap, waits=[t_ph0], sem="d_ff0")
            t_ph0 = S.dma("gpsimd", dap, g0b[:, :], waits=[tin], sem="d_ff0")
    twb = S.dma("sync", rs1[:, 0:128],
                wb_d[:, :, :].rearrange("p k h -> p (k h)"),
                waits=[t_ph0], sem="d_pre")
    vop(lambda eng: eng.tensor_copy(
        wbtile[:, 0, :, :].rearrange("p k h -> p (k h)"), rs1[:, 0:128]),
        waits=[twb])
    t_wb = vop(lambda eng: eng.tensor_tensor(
        wbtile[:, 1, :, :].rearrange("p k h -> p (k h)"), rs1[:, 0:128],
        wbtile[:, 0, :, :].rearrange("p k h -> p (k h)"), ALU.subtract))
    war_add("G0", S.tok("d_pre"))
    war_add("G1", S.tok("d_pre"))
    war_add("rs1", t_wb)

    cur_table = [None]

    def use_table(func):
        return

    cur_dbg = [False]

    def dump(name, src_ap, waits, warname=None):
        if not (dbg and cur_dbg[0]):
            return
        n = src_ap.shape[-1] if src_ap.ndim == 2 else int(
            np.prod(src_ap.shape[1:]))
        flat = src_ap if src_ap.ndim == 2 else src_ap.rearrange(
            "p " + " ".join(f"a{i}" for i in range(src_ap.ndim - 1))
            + " -> p (" + " ".join(f"a{i}" for i in range(src_ap.ndim - 1)) + ")")
        tok = S.op("gpsimd",
                   (lambda d, s_: (lambda eng: eng.dma_start(out=d, in_=s_)))(
                       dbg_t[name][:, 0:n], flat),
                   waits=list(waits) + [S.tok("d_dbg")], inc="d_dbg",
                   inc_n=16)
        if warname:
            war_add(warname, tok)
        return tok

    wbuf_free = [None, None]
    wgbuf_free = [None, None]
    wubuf_free = [None, None]
    wdbuf_free = [None, None, None, None]

    def ln_finalize(sl, eps_ap, t_sq_tok, t_sum_tok, src=None):
        src = src if src is not None else sl
        vop(lambda eng: eng.tensor_scalar(
            st_mu[:, sl], st_sum[:, src], 1.0 / HID, None, ALU.mult),
            waits=[t_sum_tok])
        vop(lambda eng: eng.tensor_scalar(
            st_t1[:, sl], st_sq[:, src], 1.0 / HID, None, ALU.mult),
            waits=[t_sq_tok])
        vop(lambda eng: eng.tensor_tensor(
            st_t2[:, sl], st_mu[:, sl], st_mu[:, sl], ALU.mult))
        tv = vop(lambda eng: eng.tensor_tensor(
            st_t1[:, sl], st_t1[:, sl], st_t2[:, sl], ALU.subtract))
        sop(lambda eng: eng.activation(st_t1[:, sl], st_t1[:, sl],
                                       AF.Ln, bias=eps_ap),
            waits=[tv, t_ones])
        t = sop(lambda eng: eng.activation(st_r[:, sl], st_t1[:, sl],
                                           AF.Exp, scale=c_mhalf))
        return t

    def rms_finalize(sl, eps_ap, t_sq_tok):
        sop(lambda eng: eng.activation(st_t1[:, sl], st_sq[:, sl],
                                       AF.Ln, bias=eps_ap, scale=c_invh),
            waits=[t_sq_tok, t_ones])
        t = sop(lambda eng: eng.activation(st_r[:, sl], st_t1[:, sl],
                                           AF.Exp, scale=c_mhalf))
        return t

    LN_EXP = AF.Exp
    SILU = AF.Silu

    seq_tiles = [(sq, ti) for sq in range(n_seq) for ti in range(tiles_per_seq)]

    prefetched = [None]

    def emit_ln1(g2, waits_extra):
        tok0n = (g2 // tiles_per_seq) * T_SEQ + (g2 % tiles_per_seq) * TILE
        gx = G[8][:].rearrange("p (c f) -> p c f", f=HID)
        gxn1 = G[0][:].rearrange("p (c f) -> p c f", f=HID)
        px = S.dma("gpsimd", gx[:, :, :],
                   xd[tok0n:tok0n + TILE, :]
                   .rearrange("(c p) f -> p c f", p=128),
                   waits=waits_extra, sem="d_x")
        for c in range(4):
            sop(lambda eng, c=c, gx=gx: eng.activation(
                sdump[:, :], gx[:, c], AF.Identity,
                accum_out=st_sum[:, 4 + c:5 + c]),
                waits=[px] + (wwait("st47") if c == 0 else [])
                + (wwait("rs1") if c == 0 else []))
            sop(lambda eng, c=c, gx=gx: eng.activation(
                sdump[:, :], gx[:, c], AF.Square,
                accum_out=st_sq[:, 4 + c:5 + c]))
        t_st = S.tok("s")
        t_ln1 = ln_finalize(slice(4, 8), c_lneps, t_st, t_st)
        war_add("st47", t_ln1)
        tnorm = None
        for c in range(4):
            tnorm = vop(lambda eng, c=c, gx=gx, gxn1=gxn1: eng.tensor_scalar(
                gxn1[:, c], gx[:, c], st_mu[:, 4 + c:5 + c],
                st_r[:, 4 + c:5 + c], ALU.subtract, ALU.mult),
                waits=[t_ln1] + wwait("G0"))
        prefetched[0] = (px, tnorm)

    front = [None]

    def emit_front(g2):
        old_dbg = cur_dbg[0]
        cur_dbg[0] = g2 == 0
        t_xn1 = prefetched[0][1]
        gxn1 = G[0][:].rearrange("p (c f) -> p c f", f=HID)
        gF2 = G[2][:].bitcast(BF16).rearrange("p (u k n) -> p u k n",
                                              u=2, k=8, n=TILE)
        for c in range(4):
            for half in range(2):
                b = half
                last = None
                for j in range(4):
                    f = half * 4 + j
                    last = top(lambda eng, c=c, f=f, b=b, j=j: eng.transpose(
                        ps[:, b, j * 128:(j + 1) * 128],
                        gxn1[:, c, f * 128:(f + 1) * 128], iden[:]),
                        waits=[t_xn1, t_pre] + PS.use(b))
                th = sop(lambda eng, c=c, half=half, b=b: eng.activation(
                    gF2[:, 0, half * 4:half * 4 + 4, c * 128:(c + 1) * 128],
                    ps[:, b, :].rearrange("p (j n) -> p j n", j=4),
                    AF.Identity),
                    waits=[last] + (wwait("G2") if (c == 0 and half == 0)
                                    else []))
                ev = vop(lambda eng, c=c, half=half, b=b: eng.tensor_tensor(
                    gF2[:, 1, half * 4:half * 4 + 4, c * 128:(c + 1) * 128],
                    ps[:, b, :].rearrange("p (j n) -> p j n", j=4),
                    gF2[:, 0, half * 4:half * 4 + 4, c * 128:(c + 1) * 128],
                    ALU.subtract),
                    waits=[th])
                PS.done(b, ev)
        t_xn1F = S.tok("v")
        war_add("G0", S.tok("t"))
        for c in range(4):
            for k in range(8):
                for (ux, uw) in [(0, 0), (1, 0), (0, 1)]:
                    top(lambda eng, c=c, k=k, ux=ux, uw=uw: eng.matmul(
                        ps[:, 2, c * 16:c * 16 + 16],
                        gF2[:, ux, k, c * 128:(c + 1) * 128],
                        wbtile[:, uw, k, :],
                        start=(k == 0 and ux == 0 and uw == 0), stop=False),
                        waits=[t_xn1F, t_wb] + PS.use(2))
            top(lambda eng, c=c: eng.matmul(
                ps[:, 2, c * 16:c * 16 + 16], ones1[:], bb_row[:],
                start=False, stop=True), waits=[t_ones])
        t_bmm = S.tok("t")
        sop(lambda eng: eng.activation(bt_beta[:, :], ps[:, 2, 0:64],
                                       AF.Exp, scale=c_mone),
            waits=[t_bmm, t_ones] + wwait("bt_beta"))
        t_e = S.tok("s")
        ev = vop(lambda eng: eng.tensor_scalar(bt_beta[:, :], bt_beta[:, :],
                                               1.0, None, ALU.add), waits=[t_e])
        PS.done(2, ev)
        vop(lambda eng: eng.reciprocal(bt_beta[:, :], bt_beta[:, :]))
        vop(lambda eng: eng.tensor_scalar(bt_beta[:, :], bt_beta[:, :],
                                          0.8, 0.999, ALU.max, ALU.min))
        t_beta = S.tok("v")
        dump("beta", bt_beta[:, :], [t_beta], "bt_beta")
        sop(lambda eng: eng.activation(
            bt_ln[:].rearrange("p c h -> p (c h)"), bt_beta[:, :], AF.Ln),
            waits=[t_beta] + wwait("bt_ln"))
        t_lnb = S.tok("s")
        cur_dbg[0] = old_dbg
        front[0] = (t_xn1F, t_lnb)

    for gti, (sq, ti) in enumerate(seq_tiles):
        tok0 = sq * T_SEQ + ti * TILE
        DBG = dbg and gti == 0
        cur_dbg[0] = gti == 0

        g_x = G[8][:].rearrange("p (c f) -> p c f", f=HID)
        g_xn1 = G[0][:].rearrange("p (c f) -> p c f", f=HID)
        g_qt = G[3][:].rearrange("p (c f) -> p c f", f=HID)
        g_qt4 = G[3][:].rearrange("p (c h e) -> p c h e", h=H, e=D)
        g_kt = G[4][:].rearrange("p (c f) -> p c f", f=HID)
        g_kt4 = G[4][:].rearrange("p (c h e) -> p c h e", h=H, e=D)
        g_vt4 = G[5][:].rearrange("p (c h e) -> p c h e", h=H, e=D)
        g_attnF2 = G[6][:].bitcast(BF16).rearrange("p (u k n) -> p u k n",
                                                   u=2, k=8, n=TILE)
        g_h = G[7][:].rearrange("p (c f) -> p c f", f=HID)

        use_table(LN_EXP)
        S.mark(f"tile{gti}_start")
        if gti == 0:
            emit_ln1(0, [])
            emit_front(0)
        t_x, t_xn1 = prefetched[0]
        g_xn1F2 = G[2][:].bitcast(BF16).rearrange("p (u k n) -> p u k n",
                                                  u=2, k=8, n=TILE)
        t_xn1F, t_lnb = front[0]

        S.mark(f"tile{gti}_xn1F")
        def proj(wi, dst3, evac_waits=()):
            evw = list(evac_waits)
            ew = []
            for half in range(2):
                wts = [None] * 4
                for c in range(4):
                    b = 2 + c
                    for t_ in PS.use(b):
                        ew.append(t_)
                last = None
                for k in range(8):
                    kg = k // 2
                    slot = kg % 2
                    if k % 2 == 0:
                        wts[kg] = S.dma(
                            "sync", wbuf[:, slot, :, :, :],
                            whl_d[wi, half, :, :, kg * 2:(kg + 1) * 2, :],
                            waits=[wbuf_free[slot], S.tok("d_pre")],
                            sem=f"d_w{slot}")
                    for c in range(4):
                        b = 2 + c
                        for (ux, uw) in [(0, 0), (1, 0), (0, 1)]:
                            last = top(
                                lambda eng, c=c, k=k, b=b, slot=slot, ux=ux,
                                uw=uw: eng.matmul(
                                    ps[:, b, :],
                                    g_xn1F2[:, ux, k, c * 128:(c + 1) * 128],
                                    wbuf[:, slot, uw, k % 2, :],
                                    start=(k == 0 and ux == 0 and uw == 0),
                                    stop=(k == 7 and uw == 1)),
                                waits=[wts[kg], t_xn1F] + ew)
                    if k % 2 == 1:
                        wbuf_free[slot] = last
                    ew = []
                for c in range(4):
                    b = 2 + c
                    ev = sop(lambda eng, c=c, b=b, half=half, dst3=dst3:
                             eng.activation(
                                 dst3[:, c, half * 512:(half + 1) * 512],
                                 ps[:, b, :], AF.Identity),
                             waits=[last] + evw)
                    evw = []
                    PS.done(b, ev)
            return S.tok("s")

        t_k = proj(1, g_kt, wwait("G4"))
        for c in range(4):
            top(lambda eng, c=c: eng.matmul(
                ps[:, 3, c * 16:c * 16 + 16], utri[:], bt_ln[:, c, :],
                start=True, stop=True),
                waits=[t_lnb] + PS.use(3))
        t_cum = S.tok("t")
        war_add("bt_ln", t_cum)
        sop(lambda eng: eng.activation(
            bt_P[:].rearrange("p c h -> p (c h)"), ps[:, 3, 0:64], AF.Exp),
            waits=[t_cum] + wwait("bt_P"))
        t_p = S.tok("s")
        PS.done(3, t_p)
        vop(lambda eng: eng.reciprocal(
            bt_Pi[:].rearrange("p c h -> p (c h)"),
            bt_P[:].rearrange("p c h -> p (c h)")), waits=[t_p])
        t_pi = S.tok("v")
        t_zp = S.dma("sync", pcrow[:, :].rearrange("q (c h) -> q c h", h=16),
                     bt_P[127:128, :, :], waits=[t_p] + wwait("pcrow"),
                     sem="d_zp")
        war_add("bt_P", t_zp)
        dump("pp", bt_P[:].rearrange("p c h -> p (c h)"), [t_p], "bt_P")
        dump("ppi", bt_Pi[:].rearrange("p c h -> p (c h)"), [t_pi])
        S.mark(f"tile{gti}_beta")

        scrA = G[6][:, 0:2048].rearrange("p (c h e) -> p c h e", h=H, e=32)
        scrB = G[6][:, 2048:4096].rearrange("p (c h e) -> p c h e", h=H, e=32)

        def rope(g4, dep):
            lo = g4[:, :, :, 0:32]
            hi = g4[:, :, :, 32:64]
            cosb = costab[:, ti * 4:ti * 4 + 4, :].unsqueeze(2).broadcast_to(
                [128, 4, H, 32])
            sinb = sintab[:, ti * 4:ti * 4 + 4, :].unsqueeze(2).broadcast_to(
                [128, 4, H, 32])
            vop(lambda eng: eng.tensor_tensor(scrA[:], lo, cosb, ALU.mult),
                waits=[dep, t_pre] + wwait("G6"))
            vop(lambda eng: eng.tensor_tensor(scrB[:], lo, sinb, ALU.mult))
            vop(lambda eng: eng.tensor_tensor(lo, hi, sinb, ALU.mult))
            vop(lambda eng: eng.tensor_tensor(hi, hi, cosb, ALU.mult))
            vop(lambda eng: eng.tensor_tensor(lo, scrA[:], lo, ALU.subtract))
            return vop(lambda eng: eng.tensor_tensor(hi, hi, scrB[:], ALU.add))

        def phi(gflat, scr, dep):
            vop(lambda eng, g=gflat: eng.tensor_scalar(
                scr, g, 0.0, None, ALU.min), waits=[dep])
            tv = S.tok("v")
            te = sop(lambda eng, scr=scr: eng.activation(scr, scr, AF.Exp),
                     waits=[tv])
            vop(lambda eng, g=gflat: eng.tensor_scalar(
                g, g, 0.0, None, ALU.max))
            return vop(lambda eng, g=gflat, scr=scr: eng.tensor_tensor(
                g, g, scr, ALU.add), waits=[te])

        def rope_c(g4, c, dep, scr2, extra=()):
            sA = scr2[:, 0:512].rearrange("p (h e) -> p h e", e=32)
            sB = scr2[:, 512:1024].rearrange("p (h e) -> p h e", e=32)
            lo = g4[:, c, :, 0:32]
            hi = g4[:, c, :, 32:64]
            cosb = costab[:, ti * 4 + c, :].unsqueeze(1).broadcast_to(
                [128, H, 32])
            sinb = sintab[:, ti * 4 + c, :].unsqueeze(1).broadcast_to(
                [128, H, 32])
            vop(lambda eng: eng.tensor_tensor(sA, lo, cosb, ALU.mult),
                waits=[dep, t_pre] + list(extra))
            vop(lambda eng: eng.tensor_tensor(sB, lo, sinb, ALU.mult))
            vop(lambda eng: eng.tensor_tensor(lo, hi, sinb, ALU.mult))
            vop(lambda eng: eng.tensor_tensor(hi, hi, cosb, ALU.mult))
            vop(lambda eng: eng.tensor_tensor(lo, sA, lo, ALU.subtract))
            return vop(lambda eng: eng.tensor_tensor(hi, hi, sB, ALU.add))

        S.mark(f"tile{gti}_qk")
        t_kr = rope(g_kt4, t_k)
        t_kp = phi(G[4][:], G[6][:], t_kr)
        t_ktd = vop(lambda eng: eng.tensor_tensor(
            g_kt4[:, :, :, :], g_kt4[:, :, :, :],
            bt_Pi[:].unsqueeze(3).broadcast_to([128, 4, H, D]), ALU.mult),
            waits=[t_kp, t_pi])
        t_q = proj(0, g_qt, wwait("G3"))
        t_pcmm = top(lambda eng: eng.matmul(
            ps[:, 3, 0:64], ones1[:], pcrow[:, :], start=True, stop=True),
            waits=[t_zp, t_ones] + PS.use(3))
        t_pc = vop(lambda eng: eng.tensor_copy(
            bt_PC[:].rearrange("p c h -> p (c h)"), ps[:, 3, 0:64]),
            waits=[t_pcmm] + wwait("bt_PC"))
        PS.done(3, t_pc)
        war_add("pcrow", t_pcmm)
        t_qp_c = [None] * 4
        tq_r = rope_c(g_qt4, 0, t_q, G[6][:, 0:1024])
        t_qp_c[0] = phi(g_qt[:, 0, :], G[6][:, 1024:2048], tq_r)
        t_v = proj(2, G[5][:].rearrange("p (c f) -> p c f", f=HID),
                   wwait("G5"))
        war_add("G2", wbuf_free[0])
        war_add("G2", wbuf_free[1])
        dump("kt", G[4][:], [t_ktd], "G4")
        dump("vt", G[5][:], [t_v], "G5")

        S.mark(f"tile{gti}_ropephi")
        if ti == 0:
            t_s0 = vop(lambda eng: eng.memset(Sdup[:, :, :], 0.0),
                       waits=wwait("Sdup"))
            t_z0 = vop(lambda eng: eng.memset(z0row[:, :], 0.0),
                       waits=wwait("z0row"))
        else:
            t_s0 = war.get("Sdup_ready", [S.tok("v")])[0]
            t_z0 = war.get("z0_ready", [S.tok("v")])[0]
        t_supd = t_s0

        def transF(src3, dstF, dep, b0, warname):
            evs = []
            for half in range(2):
                b = b0 + half
                last = None
                for j in range(4):
                    f = half * 4 + j
                    last = top(lambda eng, f=f, b=b, j=j, src3=src3:
                               eng.transpose(
                                   ps[:, b, j * 128:(j + 1) * 128],
                                   src3[:, f * 128:(f + 1) * 128], iden[:]),
                               waits=[dep] + PS.use(b))
                ev = sop(lambda eng, half=half, b=b, dstF=dstF:
                         eng.activation(
                             dstF[:, half * 4:half * 4 + 4, :],
                             ps[:, b, :].rearrange("p (j n) -> p j n", j=4),
                             AF.Identity),
                         waits=[last] + wwait(warname))
                PS.done(b, ev)
                evs.append(ev)
            return evs[-1]

        def prep_kcz_z(c):
            nonlocal t_z0
            kcz = []
            for hh in range(2):
                bb_ = 6 + hh
                top(lambda eng, c=c, hh=hh, bb_=bb_: eng.matmul(
                    ps[:, bb_, :], utri[:],
                    g_kt[:, c, hh * 512:(hh + 1) * 512],
                    start=True, stop=False),
                    waits=[t_ktd, t_z0, t_ones] + PS.use(bb_))
                m = top(lambda eng, hh=hh, bb_=bb_: eng.matmul(
                    ps[:, bb_, :], ones1[:],
                    z0row[0:1, hh * 512:(hh + 1) * 512],
                    start=False, stop=True))
                kcz.append(m)
            war_add("z0row", kcz[-1])
            zmm = []
            for hh in range(2):
                bb_ = hh
                m = top(lambda eng, c=c, hh=hh, bb_=bb_: eng.matmul(
                    ps[0:1, bb_, :], ones_col[:],
                    g_kt[:, c, hh * 512:(hh + 1) * 512],
                    start=True, stop=True),
                    waits=[t_ktd, t_ones] + PS.use(bb_))
                zmm.append(m)
            vop(lambda eng: eng.tensor_tensor(
                z0row[0:1, :],
                ps[0:1, 0:2, :].rearrange("q b n -> q (b n)"),
                z0row[0:1, :], ALU.add),
                waits=[zmm[0], zmm[1], t_zp] + wwait("z0row"))
            t_z0 = vop(lambda eng, c=c: eng.tensor_tensor(
                z0row[0:1, :].rearrange("q (h e) -> q h e", e=64),
                z0row[0:1, :].rearrange("q (h e) -> q h e", e=64),
                pcrow[0:1, c * 16:(c + 1) * 16].unsqueeze(2)
                .broadcast_to([1, 16, 64]), ALU.mult))
            PS.done(0, t_z0)
            PS.done(1, t_z0)
            return kcz

        def prep_den(c, kcz):
            tden0 = vop(lambda eng, c=c: eng.tensor_tensor(
                rs1[:, 0:512], g_qt[:, c, 0:512], ps[:, 6, :], ALU.mult),
                waits=[kcz[0], t_qp_c[c]] + wwait("rs1"))
            tden1 = vop(lambda eng, c=c: eng.tensor_tensor(
                rs1[:, 512:1024], g_qt[:, c, 512:1024], ps[:, 7, :],
                ALU.mult), waits=[kcz[1]])
            PS.done(6, tden0)
            PS.done(7, tden1)
            vop(lambda eng: eng.tensor_reduce(
                den_s[:, :],
                rs1[:, 0:1024].rearrange("p (h e) -> p h e", e=64),
                AX.X, ALU.add))
            vop(lambda eng, c=c: eng.tensor_tensor(
                den_s[:, :], den_s[:, :], bt_P[:, c, :], ALU.mult))
            vop(lambda eng: eng.tensor_scalar(
                den_s[:, :], den_s[:, :], float(EPS), None, ALU.add))
            vop(lambda eng: eng.reciprocal(drcp[:, :], den_s[:, :]))
            tq = vop(lambda eng, c=c: eng.tensor_tensor(
                qsc[:, :], bt_P[:, c, :], drcp[:, :], ALU.mult))
            if DBG and c == 0:
                dump("den", den_s[:, :], [tq])
            t_qhat = vop(lambda eng, c=c: eng.tensor_tensor(
                g_qt4[:, c], g_qt4[:, c],
                qsc[:].unsqueeze(2).broadcast_to([128, H, D]), ALU.mult),
                waits=[tq] + wwait("G3"))
            return t_qhat

        def scores_all(t_qF, t_kF):
            toksA, toksB = [], []
            for par in range(2):
                for h in range(par, 16, 2):
                    b0 = 0 if h < 8 else 2
                    hi_ = h % 8
                    bb_ = b0 + hi_ // 4
                    part0 = par * 64
                    extra = ([toksA[-1] if not toksB else toksB[-1]]
                             if (par == 1 and h == 1) else [])
                    m = top(lambda eng, h=h, bb_=bb_, hi_=hi_, part0=part0:
                            eng.matmul(
                                ps[:, bb_,
                                   (hi_ % 4) * 128:(hi_ % 4) * 128 + 128],
                                kF[part0:part0 + 64, h // 2, :],
                                qF[part0:part0 + 64, h // 2, :],
                                start=True, stop=True),
                            waits=[t_qF, t_kF] + PS.use(bb_) + extra)
                    (toksA if h < 8 else toksB).append(m)
            return toksA, toksB

        def mask_evac(b0, waits):
            ev = vop(lambda eng, b0=b0: eng.tensor_tensor(
                AT[:, :, :],
                ps[:, b0:b0 + 2, :].rearrange("p b n -> p (b n)")
                .rearrange("p (h n) -> p h n", n=128),
                utri[:].unsqueeze(1).broadcast_to([128, 8, 128]),
                ALU.mult), waits=waits + wwait("AT"))
            PS.done(b0, ev)
            PS.done(b0 + 1, ev)
            return ev

        def av_inter(c, h0, bk, evt):
            toks = []
            for p_ in range(4):
                he = h0 + 2 * p_
                ho = he + 1
                fc = (h0 // 2) + p_
                sl = slice(p_ * 128, p_ * 128 + 128)
                top(lambda eng, he=he, p_=p_, bk=bk, sl=sl, c=c:
                    eng.matmul(ps[0:64, bk, sl],
                               g_vt4[:, c, he, :], AT[:, 2 * p_, :],
                               start=True, stop=False),
                    waits=[evt, t_v, t_supd]
                    + (PS.use(bk) if p_ == 0 else []))
                t1 = top(lambda eng, he=he, fc=fc, bk=bk, sl=sl:
                         eng.matmul(ps[0:64, bk, sl],
                                    Sdup[0:64, fc, :], qF[0:64, fc, :],
                                    start=False, stop=True))
                top(lambda eng, ho=ho, p_=p_, bk=bk, sl=sl, c=c:
                    eng.matmul(ps[64:128, bk, sl],
                               g_vt4[:, c, ho, :], AT[:, 2 * p_ + 1, :],
                               start=True, stop=False,
                               tile_position=(0, 64)))
                t2 = top(lambda eng, ho=ho, fc=fc, bk=bk, sl=sl:
                         eng.matmul(ps[64:128, bk, sl],
                                    Sdup[64:128, fc, :], qF[64:128, fc, :],
                                    start=False, stop=True,
                                    tile_position=(64, 64)))
                toks += [t1, t2]
            th = sop(lambda eng, h0=h0, bk=bk, c=c: eng.activation(
                g_attnF2[:, 0, (h0 // 2):(h0 // 2) + 4,
                         c * 128:(c + 1) * 128],
                ps[:, bk, :].rearrange("p (j n) -> p j n", j=4),
                AF.Identity),
                waits=[toks[-1]])
            ev = vop(lambda eng, h0=h0, bk=bk, c=c: eng.tensor_tensor(
                g_attnF2[:, 1, (h0 // 2):(h0 // 2) + 4,
                         c * 128:(c + 1) * 128],
                ps[:, bk, :].rearrange("p (j n) -> p j n", j=4),
                g_attnF2[:, 0, (h0 // 2):(h0 // 2) + 4,
                         c * 128:(c + 1) * 128], ALU.subtract),
                waits=[th])
            PS.done(bk, ev)
            return ev, toks

        kcz_n = prep_kcz_z(0)
        t_kF = transF(g_kt[:, 0, :], kF[:], t_ktd, 2, "kF")
        t_qhat = prep_den(0, kcz_n)
        t_qF = transF(g_qt[:, 0, :], qF[:], t_qhat, 0, "qF")
        for c in range(4):
            S.mark(f"tile{gti}_chunk{c}")
            mmA, mmB = scores_all(t_qF, t_kF)
            evA = mask_evac(0, [mmA[-1], mmB[-1]])
            if DBG and c == 0:
                dump("at0", AT[:, :, :], [evA], "AT")
            if c < 3:
                kcz_n = prep_kcz_z(c + 1)
            evAVa, mmAVa = av_inter(c, 0, 4, evA)
            war_add("AT", mmAVa[-1])
            evB = mask_evac(2, [mmB[-1]])
            if c < 3:
                tq_r = rope_c(g_qt4, c + 1, t_q, G[7][:, 0:1024],
                              extra=(wwait("G7") if c == 0 else []))
                t_qp_c[c + 1] = phi(g_qt[:, c + 1, :], G[7][:, 1024:2048],
                                    tq_r)
                t_qhat = prep_den(c + 1, kcz_n)
                t_kF = transF(g_kt[:, c + 1, :], kF[:], t_ktd, 2, "kF")
            evAVb, mmAVb = av_inter(c, 8, 5, evB)
            war_add("AT", mmAVb[-1])
            war_add("qF", mmAVa[-1])
            war_add("qF", mmAVb[-1])
            war_add("kF", mmA[-1])
            war_add("kF", mmB[-1])

            ds_toks = []
            for h in range(16):
                sl = slice((h // 2) * 64, (h // 2) * 64 + 64)
                if h % 2 == 0:
                    m = top(lambda eng, h=h, sl=sl, c=c: eng.matmul(
                        ps[0:64, 4, sl], g_kt4[:, c, h, :], g_vt4[:, c, h, :],
                        start=True, stop=True),
                        waits=[t_ktd, t_v] + (PS.use(4) if h == 0 else []))
                else:
                    m = top(lambda eng, h=h, sl=sl, c=c: eng.matmul(
                        ps[64:128, 4, sl], g_kt4[:, c, h, :],
                        g_vt4[:, c, h, :],
                        start=True, stop=True, tile_position=(0, 64)))
                ds_toks.append(m)
            vop(lambda eng: eng.tensor_tensor(
                rs1[:, 0:512],
                Sdup[:].rearrange("p j e -> p (j e)"),
                ps[:, 4, :], ALU.add),
                waits=[ds_toks[-1], mmAVa[-1], mmAVb[-1]] + wwait("rs1"))
            rs3 = rs1[:, 0:512].rearrange("p (j e) -> p j e", e=64)
            pce = bt_PC[:, c, :].rearrange("p (j two) -> p j two", two=2)
            vop(lambda eng, rs3=rs3, pce=pce: eng.tensor_tensor(
                Sdup[0:64, :, :], rs3[0:64],
                pce[0:64, :, 0].unsqueeze(2).broadcast_to([64, 8, 64]),
                ALU.mult))
            t_supd = vop(lambda eng, rs3=rs3, pce=pce: eng.tensor_tensor(
                Sdup[64:128, :, :], rs3[64:128],
                pce[64:128, :, 1].unsqueeze(2).broadcast_to([64, 8, 64]),
                ALU.mult))
            war_add("rs1", t_supd)
            PS.done(4, t_supd)
            if c < 3:
                t_qF = transF(g_qt[:, c + 1, :], qF[:], t_qhat, 0, "qF")

        war["Sdup_ready"] = [t_supd]
        war["z0_ready"] = [t_z0]
        war["Sdup"] = [t_supd]
        war_add("bt_PC", t_supd)
        dump("attnF", G[6][:], [S.tok("v")], "G6")

        S.mark(f"tile{gti}_chunks")
        t_attn_done = S.tok("v")
        ew = wwait("G7")
        h_ev = [None] * 4
        for c in range(4):
            vop(lambda eng, c=c: eng.tensor_tensor(
                g_xn1[:, c], g_xn1[:, c], bo_f[:, :], ALU.add),
                waits=[t_pre])
        t_xn1b = S.tok("v")
        for half in range(2):
            wts = [None] * 4
            pw = []
            for c in range(4):
                pw += PS.use(2 + c)
            last = None
            for k in range(8):
                kg = k // 2
                slot = kg % 2
                if k % 2 == 0:
                    wts[kg] = S.dma(
                        "sync", wbuf[:, slot, :, :, :],
                        whl_d[3, half, :, :, kg * 2:(kg + 1) * 2, :],
                        waits=[wbuf_free[slot], S.tok("d_pre")],
                        sem=f"d_w{slot}")
                for c in range(4):
                    b = 2 + c
                    for (ux, uw) in [(0, 0), (1, 0), (0, 1)]:
                        last = top(
                            lambda eng, c=c, k=k, b=b, slot=slot, ux=ux,
                            uw=uw: eng.matmul(
                                ps[:, b, :],
                                g_attnF2[:, ux, k, c * 128:(c + 1) * 128],
                                wbuf[:, slot, uw, k % 2, :],
                                start=(k == 0 and ux == 0 and uw == 0),
                                stop=(k == 7 and uw == 1)),
                            waits=[wts[kg], t_attn_done] + pw)
                if k % 2 == 1:
                    wbuf_free[slot] = last
                pw = []
            for c in range(4):
                b = 2 + c
                ev = vop(lambda eng, c=c, b=b, half=half:
                         eng.scalar_tensor_tensor(
                             g_h[:, c, half * 512:(half + 1) * 512],
                             ps[:, b, :], 1.0,
                             g_xn1[:, c, half * 512:(half + 1) * 512],
                             ALU.mult, ALU.add),
                         waits=[last, t_xn1b] + (ew if (c == 0 and half == 0)
                                                 else []))
                PS.done(b, ev)
                if half == 1:
                    h_ev[c] = ev
        war_add("G6", wbuf_free[0])
        war_add("G6", wbuf_free[1])
        t_h = S.tok("v")
        dump("h", G[7][:], [t_h], "G7")

        S.mark(f"tile{gti}_wo")
        stw = wwait("st")
        g_ab = G[3][:].rearrange("p (c f) -> p c f", f=HID)
        g_x1 = G[4][:].rearrange("p (c f) -> p c f", f=HID)
        g_xn2 = G[5][:].rearrange("p (c f) -> p c f", f=HID)
        g_xn2F2 = G[2][:].bitcast(BF16).rearrange("p (u k n) -> p u k n",
                                                  u=2, k=8, n=TILE)
        rsw = wwait("rs1")
        g3w = wwait("G3")
        g4w = wwait("G4")
        g5w = wwait("G5")
        g2w = wwait("G2")
        for c in range(4):
            cs = slice(c, c + 1)
            sop(lambda eng, c=c: eng.activation(
                sdump[:, :], g_h[:, c], AF.Square,
                accum_out=st_sq[:, c:c + 1]),
                waits=[h_ev[c]] + stw + rsw)
            t_rms = rms_finalize(cs, c_rmseps, S.tok("s"))
            t_ab = vop(lambda eng, c=c: eng.scalar_tensor_tensor(
                g_ab[:, c], g_h[:, c], st_r[:, c:c + 1], scale_f[:, :],
                ALU.mult, ALU.mult, accum_out=st_sum[:, c:c + 1]),
                waits=[t_rms, t_pre] + (g3w if c == 0 else []))
            sop(lambda eng, c=c: eng.activation(
                sdump[:, :], g_ab[:, c], AF.Square,
                accum_out=st_sq[:, c:c + 1]), waits=[t_ab])
            t_ln2 = ln_finalize(cs, c_lneps, S.tok("s"), t_ab)
            vop(lambda eng, c=c: eng.tensor_scalar(
                rs1[:, 0:1024], g_ab[:, c], st_mu[:, c:c + 1],
                st_r[:, c:c + 1], ALU.subtract, ALU.mult),
                waits=[t_ln2])
            t_x1 = vop(lambda eng, c=c: eng.scalar_tensor_tensor(
                g_x1[:, c], rs1[:, 0:1024], 1.0, g_x[:, c],
                ALU.mult, ALU.add, accum_out=st_sum[:, c:c + 1]),
                waits=(g4w if c == 0 else []))
            if c == 3:
                t_x1tok = t_x1
            sop(lambda eng, c=c: eng.activation(
                sdump[:, :], g_x1[:, c], AF.Square,
                accum_out=st_sq[:, c:c + 1]), waits=[t_x1])
            t_ln3 = ln_finalize(cs, c_lneps, S.tok("s"), t_x1)
            t_xn2 = vop(lambda eng, c=c: eng.tensor_scalar(
                g_xn2[:, c], g_x1[:, c], st_mu[:, c:c + 1], st_r[:, c:c + 1],
                ALU.subtract, ALU.mult),
                waits=[t_ln3] + (g5w if c == 0 else []))
            for half in range(2):
                b = half
                last = None
                for j in range(4):
                    f = half * 4 + j
                    last = top(lambda eng, c=c, f=f, b=b, j=j: eng.transpose(
                        ps[:, b, j * 128:(j + 1) * 128],
                        g_xn2[:, c, f * 128:(f + 1) * 128], iden[:]),
                        waits=[t_xn2] + PS.use(b))
                th = sop(lambda eng, c=c, half=half, b=b: eng.activation(
                    g_xn2F2[:, 0, half * 4:half * 4 + 4,
                            c * 128:(c + 1) * 128],
                    ps[:, b, :].rearrange("p (j n) -> p j n", j=4),
                    AF.Identity),
                    waits=[last] + (g2w if (c == 0 and half == 0) else []))
                ev = vop(lambda eng, c=c, half=half, b=b: eng.tensor_tensor(
                    g_xn2F2[:, 1, half * 4:half * 4 + 4,
                            c * 128:(c + 1) * 128],
                    ps[:, b, :].rearrange("p (j n) -> p j n", j=4),
                    g_xn2F2[:, 0, half * 4:half * 4 + 4,
                            c * 128:(c + 1) * 128], ALU.subtract),
                    waits=[th])
                PS.done(b, ev)
        war_add("rs1", S.tok("v"))
        war_add("st", S.tok("v"))
        t_xn2F = S.tok("v")
        dump("ab", G[3][:], [S.tok("v")], "G3")
        dump("x1", G[4][:], [S.tok("v")], "G4")
        dump("xn2", G[5][:], [S.tok("v")], "G5")

        S.mark(f"tile{gti}_xn2F")
        use_table(SILU)
        hm_toks = [None] * 24

        def hm_view(m, u):
            gi = 1 + (0 if m < 8 else (5 if m < 16 else 6))
            return G[gi][:].bitcast(BF16).rearrange(
                "p (v j n) -> p v j n", v=2, n=TILE)[:, u, m % 8, :]

        for m in range(24):
            slot = m % 2
            bg = (m % 2) * 2
            bu = bg + 1
            twg = S.dma("sync", wgbuf[:, slot, :, :], wgb_d[m, :, :, :],
                        waits=[wgbuf_free[slot], S.tok("d_pre")],
                        sem=f"d_g{slot}")
            twu = S.dma("sync", wubuf[:, slot, :, :], wub_d[m, :, :, :],
                        waits=[wubuf_free[slot], S.tok("d_pre")],
                        sem=f"d_u{slot}")
            lastg = lastu = None
            for k in range(8):
                for u in range(2):
                    lastg = top(lambda eng, k=k, u=u, bg=bg, slot=slot:
                                eng.matmul(
                                    ps[:, bg, :], wgbuf[:, slot, k, :],
                                    g_xn2F2[:, u, k, :],
                                    start=(k == 0 and u == 0),
                                    stop=(k == 7 and u == 1)),
                                waits=[twg, t_xn2F] + PS.use(bg))
            for k in range(8):
                for u in range(2):
                    lastu = top(lambda eng, k=k, u=u, bu=bu, slot=slot:
                                eng.matmul(
                                    ps[:, bu, :], wubuf[:, slot, k, :],
                                    g_xn2F2[:, u, k, :],
                                    start=(k == 0 and u == 0),
                                    stop=(k == 7 and u == 1)),
                                waits=[twu] + PS.use(bu))
            wgbuf_free[slot] = lastg
            wubuf_free[slot] = lastu
            tsil = sop(lambda eng, bg=bg: eng.activation(
                sdump[:, 0:512], ps[:, bg, :], AF.Silu, scale=c_s127),
                waits=[lastg, t_ones] + wwait("rs1"))
            PS.done(bg, tsil)
            warkey = "G" + str(1 + (0 if m < 8 else (5 if m < 16 else 6)))
            vop(lambda eng, bu=bu: eng.tensor_tensor(
                rs1[:, 512:1024], sdump[:, 0:512], ps[:, bu, :], ALU.mult),
                waits=[tsil, lastu])
            vop(lambda eng, m=m: eng.tensor_copy(
                hm_view(m, 0), rs1[:, 512:1024]),
                waits=(wwait(warkey) if m % 8 == 0 else []))
            tm = vop(lambda eng, m=m: eng.tensor_tensor(
                hm_view(m, 1), rs1[:, 512:1024], hm_view(m, 0), ALU.subtract))
            war_add("rs1", tm)
            PS.done(bu, tm)
            hm_toks[m] = tm

        war_add("G2", max(wgbuf_free, key=lambda t: t[1]))
        war_add("G2", max(wubuf_free, key=lambda t: t[1]))
        if gti + 1 < len(seq_tiles):
            emit_ln1(gti + 1, [t_x1tok])

        S.mark(f"tile{gti}_gateup")
        dn_last = {}
        for mk in range(24):
            for half in range(2):
                slot = (mk * 2 + half) % 4
                twd = S.dma("sync", wdbuf[:, slot, :],
                            wdb_d[mk, :, half * 512:(half + 1) * 512],
                            waits=[wdbuf_free[slot], S.tok("d_pre")],
                            sem=f"d_d{slot}")
                last = None
                for c in range(4):
                    b = c * 2 + half
                    for u in range(2):
                        last = top(lambda eng, mk=mk, c=c, u=u, b=b, slot=slot:
                                   eng.matmul(
                                       ps[:, b, :],
                                       hm_view(mk, u)[:, c * 128:(c + 1) * 128],
                                       wdbuf[:, slot, :],
                                       start=(mk == 0 and u == 0),
                                       stop=(mk == 23 and u == 1)),
                                   waits=([twd, hm_toks[mk]]
                                          + (PS.use(b) if mk == 0 else [])))
                    dn_last[(c, half)] = last
                wdbuf_free[slot] = last
        g_f = G[3][:].rearrange("p (c f) -> p c f", f=HID)
        g3w2 = wwait("G3")
        for c in range(4):
            evs = sop(lambda eng, c=c: eng.activation(
                g_f[:, c, 0:512], ps[:, c * 2, :], AF.Identity,
                accum_out=st_sum[:, c:c + 1]),
                waits=[dn_last[(c, 0)]] + (g3w2 if c == 0 else []))
            PS.done(c * 2, evs)
            evv = vop(lambda eng, c=c: eng.tensor_scalar(
                g_f[:, c, 512:1024], ps[:, c * 2 + 1, :],
                0.0, 0.0, ALU.add, ALU.add,
                accum_out=st_t2[:, c:c + 1]),
                waits=[dn_last[(c, 1)]] + (g3w2 if c == 0 else []))
            PS.done(c * 2 + 1, evv)
        t_sevac = S.tok("s")
        tvf = vop(lambda eng: eng.tensor_tensor(
            st_sum[:, 0:4], st_sum[:, 0:4], st_t2[:, 0:4], ALU.add),
            waits=[t_sevac])
        if gti + 1 < len(seq_tiles):
            emit_front(gti + 1)
        for c in range(4):
            sop(lambda eng, c=c: eng.activation(
                sdump[:, :], g_f[:, c], AF.Square,
                accum_out=st_sq[:, c:c + 1]), waits=[tvf] + wwait("rs1"))
        use_table(LN_EXP)
        t_ln4 = ln_finalize(slice(0, 4), c_lneps, S.tok("s"), tvf)
        dump("fout", G[3][:], [S.tok("v"), tvf], "G3")

        g_out = G[3][:].rearrange("p (c f) -> p c f", f=HID)
        for c in range(4):
            vop(lambda eng, c=c: eng.tensor_scalar(
                rs1[:, 0:1024], g_f[:, c], st_mu[:, c:c + 1], st_r[:, c:c + 1],
                ALU.subtract, ALU.mult), waits=[t_ln4] + wwait("rs1"))
            vop(lambda eng, c=c: eng.tensor_tensor(
                g_out[:, c], rs1[:, 0:1024], g_x1[:, c], ALU.add),
                waits=(wwait("G3") if c == 0 else []))
        t_out = S.tok("v")
        war_add("rs1", t_out)
        war_add("st", t_out)
        t_odma = S.op("gpsimd",
                      (lambda o, i: (lambda eng: eng.dma_start(out=o, in_=i)))(
                          outd[tok0:tok0 + TILE, :]
                          .rearrange("(c p) f -> p c f", p=128),
                          g_out[:, :, :]),
                      waits=[t_out], inc="d_out", inc_n=16)
        war_add("G3", t_odma)
        war_add("G3", S.tok("s"))
        war_add("G3", t_out)
        war_add("G1", wgbuf_free[0])
        war_add("G4", t_out)
        war_add("G5", t_xn2F)
        for sl_ in range(4):
            war_add("G6", wdbuf_free[sl_])
            war_add("G7", wdbuf_free[sl_])
            war_add("G1", wdbuf_free[sl_])

    if max_items is not None:
        S.items = S.items[:max_items]
    else:
        fin = [("d_out", S.ctr.get("d_out", 0))]
        if dbg:
            fin.append(("d_dbg", S.ctr.get("d_dbg", 0)))
        S.op("gpsimd", lambda eng: None, waits=fin)

    sem_names = ["d_pre", "d_ff0", "d_x", "d_zp", "d_w0", "d_w1", "d_g0",
                 "d_g1", "d_u0", "d_u1", "d_d0", "d_d1", "d_d2", "d_d3",
                 "d_out", "d_dbg",
                 "t", "v", "s", "g"]
    with ctx:
        sems = {}
        with ExitStack() as sctx:
            for nm in sem_names:
                sems[nm] = sctx.enter_context(nc.semaphore(nm))
            with nc.Block() as block:
                @block.sync
                def _(eng):
                    S.replay("sync", eng, sems)

                @block.tensor
                def _(eng):
                    S.replay("tensor", eng, sems)

                @block.vector
                def _(eng):
                    S.replay("vector", eng, sems)

                @block.scalar
                def _(eng):
                    S.replay("scalar", eng, sems)

                @block.gpsimd
                def _(eng):
                    S.replay("gpsimd", eng, sems)

    nc._sched_marks = S.marks
    return nc


def prep_weights(Wq, Wk, Wv, Wb, bb, Wo, bo, scale, w_gate, w_up, w_down):
    def proj_layout(W):
        WT = np.ascontiguousarray(W.T)
        return np.ascontiguousarray(
            WT.reshape(8, 128, 2, 512).transpose(2, 1, 0, 3))

    out = {
        "wq": proj_layout(Wq), "wk": proj_layout(Wk), "wv": proj_layout(Wv),
        "wo": proj_layout(Wo),
        "wb": np.ascontiguousarray(Wb.T.reshape(8, 128, 16).transpose(1, 0, 2)),
        "bb": np.ascontiguousarray(bb.reshape(1, 16)),
        "bo": np.ascontiguousarray(bo.reshape(1, HID)),
        "bo_full": np.ascontiguousarray(
            np.broadcast_to(bo.reshape(1, HID), (128, HID))),
        "lnscale": np.ascontiguousarray(
            np.broadcast_to(scale.reshape(1, HID), (128, HID))),
    }
    wg8 = w_gate.astype(np.int8)
    wu8 = w_up.astype(np.int8)
    wd8 = w_down.astype(np.int8)
    assert np.array_equal(wg8.astype(np.int32), w_gate)
    out["wg"] = np.ascontiguousarray(
        wg8.T.reshape(8, 128, 24, 128).transpose(2, 1, 0, 3))
    out["wu"] = np.ascontiguousarray(
        wu8.T.reshape(8, 128, 24, 128).transpose(2, 1, 0, 3))
    out["wd"] = np.ascontiguousarray(wd8.T.reshape(24, 128, HID))
    out["utri"] = np.triu(np.ones((128, 128), np.float32))
    out["iden"] = np.eye(128, dtype=np.float32)
    inv = 1.0 / 10000.0 ** (np.arange(0, D, 2, dtype=np.float64) / D)
    t = np.arange(2048, dtype=np.float64)
    fr = t[:, None] * inv[None, :]
    out["costab"] = np.ascontiguousarray(
        np.cos(fr).astype(np.float32).reshape(16, 128, 32).transpose(1, 0, 2))
    out["sintab"] = np.ascontiguousarray(
        np.sin(fr).astype(np.float32).reshape(16, 128, 32).transpose(1, 0, 2))
    return out


_NC_CACHE = {}


def kernel(L0_attn_x, L0_attn_Wq, L0_attn_Wk, L0_attn_Wv, L0_attn_Wb,
           L0_attn_bb, L0_attn_Wo, L0_attn_bo, L0_attn_scale,
           L0_ffn_w_gate, L0_ffn_w_up, L0_ffn_w_down):
    from concourse.bass_utils import run_bass_kernel_spmd

    x = np.asarray(L0_attn_x, dtype=np.float32)
    B, T, _ = x.shape
    n_cores = 8
    per = B // n_cores
    if "nc" not in _NC_CACHE:
        _NC_CACHE["nc"] = build_core_program(n_seq=per, tiles_per_seq=T // TILE,
                                             dbg=False)
    nc = _NC_CACHE["nc"]
    wmaps = prep_weights(np.asarray(L0_attn_Wq), np.asarray(L0_attn_Wk),
                         np.asarray(L0_attn_Wv), np.asarray(L0_attn_Wb),
                         np.asarray(L0_attn_bb), np.asarray(L0_attn_Wo),
                         np.asarray(L0_attn_bo), np.asarray(L0_attn_scale),
                         np.asarray(L0_ffn_w_gate), np.asarray(L0_ffn_w_up),
                         np.asarray(L0_ffn_w_down))
    in_maps = []
    for i in range(n_cores):
        m = dict(wmaps)
        m["x"] = np.ascontiguousarray(
            x[i * per:(i + 1) * per].reshape(per * T, HID))
        in_maps.append(m)
    res = run_bass_kernel_spmd(nc, in_maps, core_ids=list(range(n_cores)))
    out = np.concatenate([r["out"].reshape(per, T, HID) for r in res.results],
                         axis=0)
    return out


# version 15 (final)
# speedup vs baseline: 1.1464x; 1.0394x over previous
import numpy as np
from contextlib import ExitStack

import concourse.bass as bass
import concourse.mybir as mybir

F32 = mybir.dt.float32
BF16 = mybir.dt.bfloat16
I8 = mybir.dt.int8
AF = mybir.ActivationFunctionType
ALU = mybir.AluOpType
AX = mybir.AxisListType

H, D = 16, 64
HID = 1024
FF = 3072
TILE = 512
EPS = 1e-6
LNEPS = 1e-5
SCALE127 = float(np.float32(1.0 / 127.0))


class Sched:
    ENGINES = ("sync", "tensor", "vector", "scalar", "gpsimd")

    def __init__(self):
        self.items = []
        self.ctr = {}
        self.last_wait = {}
        self.marks = []

    def mark(self, name):
        self.marks.append((name, len(self.items)))

    def tok(self, sem):
        return (sem, self.ctr.get(sem, 0))

    def op(self, engine, fn, waits=(), inc=None, inc_n=1):
        assert engine in self.ENGINES
        w = []
        for t in waits:
            if t is None:
                continue
            (sem, val) = t
            if val <= 0:
                continue
            assert val <= self.ctr.get(sem, 0), (sem, val, self.ctr.get(sem, 0))
            key = (engine, sem)
            if self.last_wait.get(key, 0) >= val:
                continue
            self.last_wait[key] = val
            w.append((sem, val))
        tok = None
        incs = []
        if inc is not None:
            self.ctr[inc] = self.ctr.get(inc, 0) + inc_n
            tok = (inc, self.ctr[inc])
            incs.append((inc, inc_n))
        self.items.append((engine, w, fn, incs))
        return tok

    def dma(self, engine, out, in_, waits=(), sem=None):
        sem = sem or ("d_" + engine)
        return self.op(
            engine,
            (lambda o, i: (lambda eng: eng.dma_start(out=o, in_=i)))(out, in_),
            waits=waits, inc=sem, inc_n=16)

    def replay(self, engine_name, eng, sems):
        for (e, waits, fn, incs) in self.items:
            if e != engine_name:
                continue
            for (sem, val) in waits:
                eng.wait_ge(sems[sem], val)
            instr = fn(eng)
            for (sem, n) in incs:
                assert instr is not None
                instr.then_inc(sems[sem], n)


class Psum:
    def __init__(self):
        self.free_tok = [[] for _ in range(8)]

    def use(self, b):
        return list(self.free_tok[b])

    def done(self, b, toks):
        if not isinstance(toks, list):
            toks = [toks]
        self.free_tok[b] = toks


def build_core_program(n_seq=4, tiles_per_seq=4, dbg=False, max_items=None):
    T_SEQ = tiles_per_seq * TILE
    NTOK = n_seq * T_SEQ
    nc = bass.Bass("TRN2", target_bir_lowering=False)
    S = Sched()
    PS = Psum()
    war = {}

    def wwait(name):
        return war.pop(name, [])

    def war_add(name, tok):
        war.setdefault(name, []).append(tok)

    xd = nc.dram_tensor("x", [NTOK, HID], F32, kind="ExternalInput").ap()
    outd = nc.dram_tensor("out", [NTOK, HID], F32, kind="ExternalOutput").ap()
    wq_d = nc.dram_tensor("wq", [2, 128, 8, 512], F32, kind="ExternalInput").ap()
    wk_d = nc.dram_tensor("wk", [2, 128, 8, 512], F32, kind="ExternalInput").ap()
    wv_d = nc.dram_tensor("wv", [2, 128, 8, 512], F32, kind="ExternalInput").ap()
    wo_d = nc.dram_tensor("wo", [2, 128, 8, 512], F32, kind="ExternalInput").ap()
    wb_d = nc.dram_tensor("wb", [128, 8, 16], F32, kind="ExternalInput").ap()
    bb_d = nc.dram_tensor("bb", [1, 16], F32, kind="ExternalInput").ap()
    bo_d = nc.dram_tensor("bo", [1, HID], F32, kind="ExternalInput").ap()
    sc_d = nc.dram_tensor("lnscale", [128, HID], F32, kind="ExternalInput").ap()
    wg_d = nc.dram_tensor("wg", [24, 128, 8, 128], I8, kind="ExternalInput").ap()
    wu_d = nc.dram_tensor("wu", [24, 128, 8, 128], I8, kind="ExternalInput").ap()
    wd_d = nc.dram_tensor("wd", [24, 128, HID], I8, kind="ExternalInput").ap()
    u_d = nc.dram_tensor("utri", [128, 128], F32, kind="ExternalInput").ap()
    bof_d = nc.dram_tensor("bo_full", [128, HID], F32, kind="ExternalInput").ap()
    whl_d = nc.dram_tensor("whl", [4, 2, 128, 2, 8, 512], BF16).ap()
    wgb_d = nc.dram_tensor("wgb", [24, 128, 8, 128], BF16).ap()
    wub_d = nc.dram_tensor("wub", [24, 128, 8, 128], BF16).ap()
    wdb_d = nc.dram_tensor("wdb", [24, 128, HID], BF16).ap()
    id_d = nc.dram_tensor("iden", [128, 128], F32, kind="ExternalInput").ap()
    cos_d = nc.dram_tensor("costab", [128, 16, 32], F32, kind="ExternalInput").ap()
    sin_d = nc.dram_tensor("sintab", [128, 16, 32], F32, kind="ExternalInput").ap()

    dbg_t = {}
    if dbg:
        for nm in ["xn1", "qt", "kt", "vt", "beta", "pp", "ppi", "den",
                   "at0", "attnF", "h", "ab", "x1", "xn2", "fout", "st"]:
            dbg_t[nm] = nc.dram_tensor("dbg_" + nm, [128, 4096], F32,
                                       kind="ExternalOutput").ap()

    ctx = ExitStack()

    def sbt(name, shape, dtype=F32):
        return ctx.enter_context(nc.sbuf_tensor(name, shape, dtype))

    G = [sbt(f"g{i}", [128, 4096]) for i in range(9)]
    wbuf = sbt("wbuf", [128, 2, 2, 2, 512], BF16)
    wgbuf = sbt("wgbuf", [128, 2, 8, 128], BF16)
    wubuf = sbt("wubuf", [128, 2, 8, 128], BF16)
    wdbuf = sbt("wdbuf", [128, 4, 512], BF16)
    wbtile = sbt("wbtile", [128, 2, 8, 16], BF16)
    bo_f = sbt("bo_f", [128, HID])
    qF = sbt("qF", [128, 8, 128])
    kF = sbt("kF", [128, 8, 128])
    AT = sbt("AT", [128, 8, 128])
    Sdup = sbt("Sdup", [128, 8, 64])
    rs1 = sbt("rs1", [128, 1024])
    sdump = rs1
    z0row = sbt("z0row", [1, HID])
    bb_row = sbt("bb_row", [1, 16])
    ones1 = sbt("ones1", [1, 128])
    scale_f = sbt("scale_f", [128, HID])
    utri = sbt("utri_s", [128, 128])
    iden = sbt("iden_s", [128, 128])
    costab = sbt("costab_s", [128, 16, 32])
    sintab = sbt("sintab_s", [128, 16, 32])
    bt_beta = sbt("bt_beta", [128, 64])
    bt_ln = sbt("bt_ln", [128, 4, 16])
    bt_P = sbt("bt_P", [128, 4, 16])
    bt_Pi = sbt("bt_Pi", [128, 4, 16])
    bt_PC = sbt("bt_PC", [128, 4, 16])
    pcrow = sbt("pcrow", [1, 64])
    ones_col = sbt("ones_col", [128, 1])
    cpool = sbt("cpool", [128, 8])
    den_s = sbt("den_s", [128, 16])
    drcp = sbt("drcp", [128, 16])
    qsc = sbt("qsc", [128, 16])
    st_sum = sbt("st_sum", [128, 8])
    st_sq = sbt("st_sq", [128, 8])
    st_mu = sbt("st_mu", [128, 8])
    st_t1 = sbt("st_t1", [128, 8])
    st_t2 = sbt("st_t2", [128, 8])
    st_r = sbt("st_r", [128, 8])

    ps = ctx.enter_context(nc.psum_tensor("ps", [128, 8, 512], F32))

    def vop(fn, waits=(), inc="v"):
        return S.op("vector", fn, waits=list(waits) + [S.tok("v")], inc=inc)

    def sop(fn, waits=(), inc="s"):
        return S.op("scalar", fn, waits=list(waits) + [S.tok("s")], inc=inc)

    def top(fn, waits=(), inc="t"):
        return S.op("tensor", fn, waits=waits, inc=inc)

    for (dst, src) in [(bb_row[:], bb_d[:, :]),
                       (bo_f[:], bof_d[:, :]), (scale_f[:], sc_d[:, :]),
                       (utri[:], u_d[:, :]), (iden[:], id_d[:, :]),
                       (costab[:], cos_d[:, :, :]), (sintab[:], sin_d[:, :, :])]:
        S.dma("sync", dst, src, sem="d_pre")
    t_pre = S.tok("d_pre")
    vop(lambda eng: eng.memset(ones1[:], 1.0))
    vop(lambda eng: eng.memset(ones_col[:], 1.0))
    for i, val in enumerate([LNEPS, EPS, -0.5, -1.0, 1.0 / HID, SCALE127]):
        vop(lambda eng, i=i, val=val: eng.memset(cpool[:, i:i + 1], float(val)))
    t_ones = S.tok("v")
    c_lneps = cpool[:, 0:1]
    c_rmseps = cpool[:, 1:2]
    c_mhalf = cpool[:, 2:3]
    c_mone = cpool[:, 3:4]
    c_invh = cpool[:, 4:5]
    c_s127 = cpool[:, 5:6]

    g0f = G[0][:]
    g1b = G[1][:].bitcast(BF16)
    t_ph0 = None
    for wi, wdram in enumerate([wq_d, wk_d, wv_d, wo_d]):
        for half in range(2):
            tin = S.dma("sync", g0f,
                        wdram[half, :, :, :].rearrange("p k n -> p (k n)"),
                        waits=[t_ph0] if t_ph0 else [t_pre], sem="d_pre")
            vop(lambda eng: eng.tensor_copy(g1b[:, 0:4096], g0f), waits=[tin])
            tv = vop(lambda eng: eng.tensor_tensor(
                g1b[:, 4096:8192], g0f, g1b[:, 0:4096], ALU.subtract))
            t_ph0 = S.dma("sync",
                          whl_d[wi, half, :, :, :, :]
                          .rearrange("p u k n -> p (u k n)"),
                          g1b[:, :], waits=[tv], sem="d_pre")
    g0b = G[0][:].bitcast(BF16)
    for (src_d, dst_d, four) in [(wg_d, wgb_d, True), (wu_d, wub_d, True),
                                 (wd_d, wdb_d, False)]:
        for m0 in range(0, 24, 8):
            if four:
                sap = src_d[m0:m0 + 8].rearrange("m p k n -> p m (k n)")
                dap = dst_d[m0:m0 + 8].rearrange("m p k n -> p m (k n)")
            else:
                sap = src_d[m0:m0 + 8].rearrange("m p n -> p m n")
                dap = dst_d[m0:m0 + 8].rearrange("m p n -> p m n")
            tin = S.dma("gpsimd", g0b[:, :], sap, waits=[t_ph0], sem="d_ff0")
            t_ph0 = S.dma("gpsimd", dap, g0b[:, :], waits=[tin], sem="d_ff0")
    twb = S.dma("sync", rs1[:, 0:128],
                wb_d[:, :, :].rearrange("p k h -> p (k h)"),
                waits=[t_ph0], sem="d_pre")
    vop(lambda eng: eng.tensor_copy(
        wbtile[:, 0, :, :].rearrange("p k h -> p (k h)"), rs1[:, 0:128]),
        waits=[twb])
    t_wb = vop(lambda eng: eng.tensor_tensor(
        wbtile[:, 1, :, :].rearrange("p k h -> p (k h)"), rs1[:, 0:128],
        wbtile[:, 0, :, :].rearrange("p k h -> p (k h)"), ALU.subtract))
    war_add("G0", S.tok("d_pre"))
    war_add("G1", S.tok("d_pre"))
    war_add("rs1", t_wb)

    cur_table = [None]

    def use_table(func):
        return

    cur_dbg = [False]

    def dump(name, src_ap, waits, warname=None):
        if not (dbg and cur_dbg[0]):
            return
        n = src_ap.shape[-1] if src_ap.ndim == 2 else int(
            np.prod(src_ap.shape[1:]))
        flat = src_ap if src_ap.ndim == 2 else src_ap.rearrange(
            "p " + " ".join(f"a{i}" for i in range(src_ap.ndim - 1))
            + " -> p (" + " ".join(f"a{i}" for i in range(src_ap.ndim - 1)) + ")")
        tok = S.op("gpsimd",
                   (lambda d, s_: (lambda eng: eng.dma_start(out=d, in_=s_)))(
                       dbg_t[name][:, 0:n], flat),
                   waits=list(waits) + [S.tok("d_dbg")], inc="d_dbg",
                   inc_n=16)
        if warname:
            war_add(warname, tok)
        return tok

    wbuf_free = [None, None]
    wgbuf_free = [None, None]
    wubuf_free = [None, None]
    wdbuf_free = [None, None, None, None]

    def ln_finalize(sl, eps_ap, t_sq_tok, t_sum_tok, src=None):
        src = src if src is not None else sl
        vop(lambda eng: eng.tensor_scalar(
            st_mu[:, sl], st_sum[:, src], 1.0 / HID, None, ALU.mult),
            waits=[t_sum_tok])
        vop(lambda eng: eng.tensor_scalar(
            st_t1[:, sl], st_sq[:, src], 1.0 / HID, None, ALU.mult),
            waits=[t_sq_tok])
        vop(lambda eng: eng.tensor_tensor(
            st_t2[:, sl], st_mu[:, sl], st_mu[:, sl], ALU.mult))
        tv = vop(lambda eng: eng.tensor_tensor(
            st_t1[:, sl], st_t1[:, sl], st_t2[:, sl], ALU.subtract))
        sop(lambda eng: eng.activation(st_t1[:, sl], st_t1[:, sl],
                                       AF.Ln, bias=eps_ap),
            waits=[tv, t_ones])
        t = sop(lambda eng: eng.activation(st_r[:, sl], st_t1[:, sl],
                                           AF.Exp, scale=c_mhalf))
        return t

    def rms_finalize(sl, eps_ap, t_sq_tok):
        sop(lambda eng: eng.activation(st_t1[:, sl], st_sq[:, sl],
                                       AF.Ln, bias=eps_ap, scale=c_invh),
            waits=[t_sq_tok, t_ones])
        t = sop(lambda eng: eng.activation(st_r[:, sl], st_t1[:, sl],
                                           AF.Exp, scale=c_mhalf))
        return t

    LN_EXP = AF.Exp
    SILU = AF.Silu

    seq_tiles = [(sq, ti) for sq in range(n_seq) for ti in range(tiles_per_seq)]

    prefetched = [None]

    def emit_ln1(g2, waits_extra):
        tok0n = (g2 // tiles_per_seq) * T_SEQ + (g2 % tiles_per_seq) * TILE
        gx = G[8][:].rearrange("p (c f) -> p c f", f=HID)
        gxn1 = G[0][:].rearrange("p (c f) -> p c f", f=HID)
        px = S.dma("gpsimd", gx[:, :, :],
                   xd[tok0n:tok0n + TILE, :]
                   .rearrange("(c p) f -> p c f", p=128),
                   waits=waits_extra, sem="d_x")
        for c in range(4):
            sop(lambda eng, c=c, gx=gx: eng.activation(
                sdump[:, :], gx[:, c], AF.Identity,
                accum_out=st_sum[:, 4 + c:5 + c]),
                waits=[px] + (wwait("st47") if c == 0 else [])
                + (wwait("rs1") if c == 0 else []))
            sop(lambda eng, c=c, gx=gx: eng.activation(
                sdump[:, :], gx[:, c], AF.Square,
                accum_out=st_sq[:, 4 + c:5 + c]))
        t_st = S.tok("s")
        t_ln1 = ln_finalize(slice(4, 8), c_lneps, t_st, t_st)
        war_add("st47", t_ln1)
        tnorm = None
        for c in range(4):
            tnorm = vop(lambda eng, c=c, gx=gx, gxn1=gxn1: eng.tensor_scalar(
                gxn1[:, c], gx[:, c], st_mu[:, 4 + c:5 + c],
                st_r[:, 4 + c:5 + c], ALU.subtract, ALU.mult),
                waits=[t_ln1] + wwait("G0"))
        prefetched[0] = (px, tnorm)

    front = [None]

    def emit_front(g2):
        old_dbg = cur_dbg[0]
        cur_dbg[0] = g2 == 0
        t_xn1 = prefetched[0][1]
        gxn1 = G[0][:].rearrange("p (c f) -> p c f", f=HID)
        gF2 = G[2][:].bitcast(BF16).rearrange("p (u k n) -> p u k n",
                                              u=2, k=8, n=TILE)
        for c in range(4):
            for half in range(2):
                b = (c % 2) * 2 + half
                last = None
                for j in range(4):
                    f = half * 4 + j
                    last = top(lambda eng, c=c, f=f, b=b, j=j: eng.transpose(
                        ps[:, b, j * 128:(j + 1) * 128],
                        gxn1[:, c, f * 128:(f + 1) * 128], iden[:]),
                        waits=[t_xn1, t_pre] + PS.use(b))
                th = sop(lambda eng, c=c, half=half, b=b: eng.activation(
                    gF2[:, 0, half * 4:half * 4 + 4, c * 128:(c + 1) * 128],
                    ps[:, b, :].rearrange("p (j n) -> p j n", j=4),
                    AF.Identity),
                    waits=[last] + (wwait("G2") if (c == 0 and half == 0)
                                    else []))
                ev = vop(lambda eng, c=c, half=half, b=b: eng.tensor_tensor(
                    gF2[:, 1, half * 4:half * 4 + 4, c * 128:(c + 1) * 128],
                    ps[:, b, :].rearrange("p (j n) -> p j n", j=4),
                    gF2[:, 0, half * 4:half * 4 + 4, c * 128:(c + 1) * 128],
                    ALU.subtract),
                    waits=[th])
                PS.done(b, ev)
        t_xn1F = S.tok("v")
        war_add("G0", S.tok("t"))
        for c in range(4):
            for k in range(8):
                for (ux, uw) in [(0, 0), (1, 0), (0, 1)]:
                    top(lambda eng, c=c, k=k, ux=ux, uw=uw: eng.matmul(
                        ps[:, 2, c * 16:c * 16 + 16],
                        gF2[:, ux, k, c * 128:(c + 1) * 128],
                        wbtile[:, uw, k, :],
                        start=(k == 0 and ux == 0 and uw == 0), stop=False),
                        waits=[t_xn1F, t_wb] + PS.use(2))
            top(lambda eng, c=c: eng.matmul(
                ps[:, 2, c * 16:c * 16 + 16], ones1[:], bb_row[:],
                start=False, stop=True), waits=[t_ones])
        t_bmm = S.tok("t")
        sop(lambda eng: eng.activation(bt_beta[:, :], ps[:, 2, 0:64],
                                       AF.Exp, scale=c_mone),
            waits=[t_bmm, t_ones] + wwait("bt_beta"))
        t_e = S.tok("s")
        ev = vop(lambda eng: eng.tensor_scalar(bt_beta[:, :], bt_beta[:, :],
                                               1.0, None, ALU.add), waits=[t_e])
        PS.done(2, ev)
        vop(lambda eng: eng.reciprocal(bt_beta[:, :], bt_beta[:, :]))
        vop(lambda eng: eng.tensor_scalar(bt_beta[:, :], bt_beta[:, :],
                                          0.8, 0.999, ALU.max, ALU.min))
        t_beta = S.tok("v")
        dump("beta", bt_beta[:, :], [t_beta], "bt_beta")
        sop(lambda eng: eng.activation(
            bt_ln[:].rearrange("p c h -> p (c h)"), bt_beta[:, :], AF.Ln),
            waits=[t_beta] + wwait("bt_ln"))
        t_lnb = S.tok("s")
        cur_dbg[0] = old_dbg
        front[0] = (t_xn1F, t_lnb)

    for gti, (sq, ti) in enumerate(seq_tiles):
        tok0 = sq * T_SEQ + ti * TILE
        DBG = dbg and gti == 0
        cur_dbg[0] = gti == 0

        g_x = G[8][:].rearrange("p (c f) -> p c f", f=HID)
        g_xn1 = G[0][:].rearrange("p (c f) -> p c f", f=HID)
        g_qt = G[3][:].rearrange("p (c f) -> p c f", f=HID)
        g_qt4 = G[3][:].rearrange("p (c h e) -> p c h e", h=H, e=D)
        g_kt = G[4][:].rearrange("p (c f) -> p c f", f=HID)
        g_kt4 = G[4][:].rearrange("p (c h e) -> p c h e", h=H, e=D)
        g_vt4 = G[5][:].rearrange("p (c h e) -> p c h e", h=H, e=D)
        g_attnF2 = G[6][:].bitcast(BF16).rearrange("p (u k n) -> p u k n",
                                                   u=2, k=8, n=TILE)
        g_h = G[7][:].rearrange("p (c f) -> p c f", f=HID)

        use_table(LN_EXP)
        S.mark(f"tile{gti}_start")
        if gti == 0:
            emit_ln1(0, [])
            emit_front(0)
        t_x, t_xn1 = prefetched[0]
        g_xn1F2 = G[2][:].bitcast(BF16).rearrange("p (u k n) -> p u k n",
                                                  u=2, k=8, n=TILE)
        t_xn1F, t_lnb = front[0]

        S.mark(f"tile{gti}_xn1F")
        def proj(wi, dst3, evac_waits=()):
            evw = list(evac_waits)
            ew = []
            for half in range(2):
                wts = [None] * 4
                for c in range(4):
                    b = 2 + c
                    for t_ in PS.use(b):
                        ew.append(t_)
                last = None
                for k in range(8):
                    kg = k // 2
                    slot = kg % 2
                    if k % 2 == 0:
                        wts[kg] = S.dma(
                            "sync", wbuf[:, slot, :, :, :],
                            whl_d[wi, half, :, :, kg * 2:(kg + 1) * 2, :],
                            waits=[wbuf_free[slot], S.tok("d_pre")],
                            sem=f"d_w{slot}")
                    for c in range(4):
                        b = 2 + c
                        for (ux, uw) in [(0, 0), (1, 0), (0, 1)]:
                            last = top(
                                lambda eng, c=c, k=k, b=b, slot=slot, ux=ux,
                                uw=uw: eng.matmul(
                                    ps[:, b, :],
                                    g_xn1F2[:, ux, k, c * 128:(c + 1) * 128],
                                    wbuf[:, slot, uw, k % 2, :],
                                    start=(k == 0 and ux == 0 and uw == 0),
                                    stop=(k == 7 and uw == 1)),
                                waits=[wts[kg], t_xn1F] + ew)
                    if k % 2 == 1:
                        wbuf_free[slot] = last
                    ew = []
                for c in range(4):
                    b = 2 + c
                    ev = sop(lambda eng, c=c, b=b, half=half, dst3=dst3:
                             eng.activation(
                                 dst3[:, c, half * 512:(half + 1) * 512],
                                 ps[:, b, :], AF.Identity),
                             waits=[last] + evw)
                    evw = []
                    PS.done(b, ev)
            return S.tok("s")

        t_q = proj(0, g_qt, wwait("G3"))
        for c in range(4):
            top(lambda eng, c=c: eng.matmul(
                ps[:, 3, c * 16:c * 16 + 16], utri[:], bt_ln[:, c, :],
                start=True, stop=True),
                waits=[t_lnb] + PS.use(3))
        t_cum = S.tok("t")
        war_add("bt_ln", t_cum)
        sop(lambda eng: eng.activation(
            bt_P[:].rearrange("p c h -> p (c h)"), ps[:, 3, 0:64], AF.Exp),
            waits=[t_cum] + wwait("bt_P"))
        t_p = S.tok("s")
        PS.done(3, t_p)
        vop(lambda eng: eng.reciprocal(
            bt_Pi[:].rearrange("p c h -> p (c h)"),
            bt_P[:].rearrange("p c h -> p (c h)")), waits=[t_p])
        t_pi = S.tok("v")
        t_zp = S.dma("sync", pcrow[:, :].rearrange("q (c h) -> q c h", h=16),
                     bt_P[127:128, :, :], waits=[t_p] + wwait("pcrow"),
                     sem="d_zp")
        war_add("bt_P", t_zp)
        t_k = proj(1, g_kt, wwait("G4"))
        t_pcmm = top(lambda eng: eng.matmul(
            ps[:, 3, 0:64], ones1[:], pcrow[:, :], start=True, stop=True),
            waits=[t_zp, t_ones] + PS.use(3))
        t_pc = vop(lambda eng: eng.tensor_copy(
            bt_PC[:].rearrange("p c h -> p (c h)"), ps[:, 3, 0:64]),
            waits=[t_pcmm] + wwait("bt_PC"))
        PS.done(3, t_pc)
        war_add("pcrow", t_pcmm)
        dump("pp", bt_P[:].rearrange("p c h -> p (c h)"), [t_p], "bt_P")
        dump("ppi", bt_Pi[:].rearrange("p c h -> p (c h)"), [t_pi])
        S.mark(f"tile{gti}_beta")

        scrA = G[6][:, 0:2048].rearrange("p (c h e) -> p c h e", h=H, e=32)
        scrB = G[6][:, 2048:4096].rearrange("p (c h e) -> p c h e", h=H, e=32)

        def rope(g4, dep):
            lo = g4[:, :, :, 0:32]
            hi = g4[:, :, :, 32:64]
            cosb = costab[:, ti * 4:ti * 4 + 4, :].unsqueeze(2).broadcast_to(
                [128, 4, H, 32])
            sinb = sintab[:, ti * 4:ti * 4 + 4, :].unsqueeze(2).broadcast_to(
                [128, 4, H, 32])
            vop(lambda eng: eng.tensor_tensor(scrA[:], lo, cosb, ALU.mult),
                waits=[dep, t_pre] + wwait("G6"))
            vop(lambda eng: eng.tensor_tensor(scrB[:], lo, sinb, ALU.mult))
            vop(lambda eng: eng.tensor_tensor(lo, hi, sinb, ALU.mult))
            vop(lambda eng: eng.tensor_tensor(hi, hi, cosb, ALU.mult))
            vop(lambda eng: eng.tensor_tensor(lo, scrA[:], lo, ALU.subtract))
            return vop(lambda eng: eng.tensor_tensor(hi, hi, scrB[:], ALU.add))

        def phi(gflat, dep):
            scr = G[6][:]
            vop(lambda eng, g=gflat: eng.tensor_scalar(
                scr[:, :], g, 0.0, None, ALU.min), waits=[dep])
            tv = S.tok("v")
            te = sop(lambda eng: eng.activation(scr[:, :], scr[:, :], AF.Exp),
                     waits=[tv])
            vop(lambda eng, g=gflat: eng.tensor_scalar(
                g, g, 0.0, None, ALU.max))
            return vop(lambda eng, g=gflat: eng.tensor_tensor(
                g, g, scr[:, :], ALU.add), waits=[te])

        S.mark(f"tile{gti}_qk")
        t_qr = rope(g_qt4, t_q)
        t_qp = phi(G[3][:], t_qr)
        t_v = proj(2, G[5][:].rearrange("p (c f) -> p c f", f=HID),
                   wwait("G5"))
        war_add("G2", wbuf_free[0])
        war_add("G2", wbuf_free[1])
        t_kr = rope(g_kt4, t_k)
        t_kp = phi(G[4][:], t_kr)
        t_ktd = vop(lambda eng: eng.tensor_tensor(
            g_kt4[:, :, :, :], g_kt4[:, :, :, :],
            bt_Pi[:].unsqueeze(3).broadcast_to([128, 4, H, D]), ALU.mult),
            waits=[t_kp, t_pi])
        dump("qt", G[3][:], [t_qp], "G3")
        dump("kt", G[4][:], [t_ktd], "G4")
        dump("vt", G[5][:], [t_v], "G5")

        S.mark(f"tile{gti}_ropephi")
        if ti == 0:
            t_s0 = vop(lambda eng: eng.memset(Sdup[:, :, :], 0.0),
                       waits=wwait("Sdup"))
            t_z0 = vop(lambda eng: eng.memset(z0row[:, :], 0.0),
                       waits=wwait("z0row"))
        else:
            t_s0 = war.get("Sdup_ready", [S.tok("v")])[0]
            t_z0 = war.get("z0_ready", [S.tok("v")])[0]
        t_supd = t_s0

        def transF(src3, dstF, dep, b0, warname):
            evs = []
            for half in range(2):
                b = b0 + half
                last = None
                for j in range(4):
                    f = half * 4 + j
                    last = top(lambda eng, f=f, b=b, j=j, src3=src3:
                               eng.transpose(
                                   ps[:, b, j * 128:(j + 1) * 128],
                                   src3[:, f * 128:(f + 1) * 128], iden[:]),
                               waits=[dep] + PS.use(b))
                ev = sop(lambda eng, half=half, b=b, dstF=dstF:
                         eng.activation(
                             dstF[:, half * 4:half * 4 + 4, :],
                             ps[:, b, :].rearrange("p (j n) -> p j n", j=4),
                             AF.Identity),
                         waits=[last] + wwait(warname))
                PS.done(b, ev)
                evs.append(ev)
            return evs[-1]

        def prep_kcz_z(c):
            nonlocal t_z0
            kcz = []
            for hh in range(2):
                bb_ = 6 + hh
                top(lambda eng, c=c, hh=hh, bb_=bb_: eng.matmul(
                    ps[:, bb_, :], utri[:],
                    g_kt[:, c, hh * 512:(hh + 1) * 512],
                    start=True, stop=False),
                    waits=[t_ktd, t_z0, t_ones] + PS.use(bb_))
                m = top(lambda eng, hh=hh, bb_=bb_: eng.matmul(
                    ps[:, bb_, :], ones1[:],
                    z0row[0:1, hh * 512:(hh + 1) * 512],
                    start=False, stop=True))
                kcz.append(m)
            war_add("z0row", kcz[-1])
            zmm = []
            for hh in range(2):
                bb_ = hh
                m = top(lambda eng, c=c, hh=hh, bb_=bb_: eng.matmul(
                    ps[0:1, bb_, :], ones_col[:],
                    g_kt[:, c, hh * 512:(hh + 1) * 512],
                    start=True, stop=True),
                    waits=[t_ktd, t_ones] + PS.use(bb_))
                zmm.append(m)
            vop(lambda eng: eng.tensor_tensor(
                z0row[0:1, :],
                ps[0:1, 0:2, :].rearrange("q b n -> q (b n)"),
                z0row[0:1, :], ALU.add),
                waits=[zmm[0], zmm[1], t_zp] + wwait("z0row"))
            t_z0 = vop(lambda eng, c=c: eng.tensor_tensor(
                z0row[0:1, :].rearrange("q (h e) -> q h e", e=64),
                z0row[0:1, :].rearrange("q (h e) -> q h e", e=64),
                pcrow[0:1, c * 16:(c + 1) * 16].unsqueeze(2)
                .broadcast_to([1, 16, 64]), ALU.mult))
            PS.done(0, t_z0)
            PS.done(1, t_z0)
            return kcz

        def prep_den(c, kcz):
            tden0 = vop(lambda eng, c=c: eng.tensor_tensor(
                rs1[:, 0:512], g_qt[:, c, 0:512], ps[:, 6, :], ALU.mult),
                waits=[kcz[0], t_qp] + wwait("rs1"))
            tden1 = vop(lambda eng, c=c: eng.tensor_tensor(
                rs1[:, 512:1024], g_qt[:, c, 512:1024], ps[:, 7, :],
                ALU.mult), waits=[kcz[1]])
            PS.done(6, tden0)
            PS.done(7, tden1)
            vop(lambda eng: eng.tensor_reduce(
                den_s[:, :],
                rs1[:, 0:1024].rearrange("p (h e) -> p h e", e=64),
                AX.X, ALU.add))
            vop(lambda eng, c=c: eng.tensor_tensor(
                den_s[:, :], den_s[:, :], bt_P[:, c, :], ALU.mult))
            vop(lambda eng: eng.tensor_scalar(
                den_s[:, :], den_s[:, :], float(EPS), None, ALU.add))
            vop(lambda eng: eng.reciprocal(drcp[:, :], den_s[:, :]))
            tq = vop(lambda eng, c=c: eng.tensor_tensor(
                qsc[:, :], bt_P[:, c, :], drcp[:, :], ALU.mult))
            if DBG and c == 0:
                dump("den", den_s[:, :], [tq])
            t_qhat = vop(lambda eng, c=c: eng.tensor_tensor(
                g_qt4[:, c], g_qt4[:, c],
                qsc[:].unsqueeze(2).broadcast_to([128, H, D]), ALU.mult),
                waits=[tq] + wwait("G3"))
            return t_qhat

        def scores_all(t_qF, t_kF):
            toksA, toksB = [], []
            for par in range(2):
                for h in range(par, 16, 2):
                    b0 = 0 if h < 8 else 2
                    hi_ = h % 8
                    bb_ = b0 + hi_ // 4
                    part0 = par * 64
                    extra = ([toksA[-1] if not toksB else toksB[-1]]
                             if (par == 1 and h == 1) else [])
                    m = top(lambda eng, h=h, bb_=bb_, hi_=hi_, part0=part0:
                            eng.matmul(
                                ps[:, bb_,
                                   (hi_ % 4) * 128:(hi_ % 4) * 128 + 128],
                                kF[part0:part0 + 64, h // 2, :],
                                qF[part0:part0 + 64, h // 2, :],
                                start=True, stop=True),
                            waits=[t_qF, t_kF] + PS.use(bb_) + extra)
                    (toksA if h < 8 else toksB).append(m)
            return toksA, toksB

        def mask_evac(b0, waits):
            ev = vop(lambda eng, b0=b0: eng.tensor_tensor(
                AT[:, :, :],
                ps[:, b0:b0 + 2, :].rearrange("p b n -> p (b n)")
                .rearrange("p (h n) -> p h n", n=128),
                utri[:].unsqueeze(1).broadcast_to([128, 8, 128]),
                ALU.mult), waits=waits + wwait("AT"))
            PS.done(b0, ev)
            PS.done(b0 + 1, ev)
            return ev

        def av_inter(c, h0, bk, evt):
            toks = []
            for p_ in range(4):
                he = h0 + 2 * p_
                ho = he + 1
                fc = (h0 // 2) + p_
                sl = slice(p_ * 128, p_ * 128 + 128)
                top(lambda eng, he=he, p_=p_, bk=bk, sl=sl, c=c:
                    eng.matmul(ps[0:64, bk, sl],
                               g_vt4[:, c, he, :], AT[:, 2 * p_, :],
                               start=True, stop=False),
                    waits=[evt, t_v, t_supd]
                    + (PS.use(bk) if p_ == 0 else []))
                t1 = top(lambda eng, he=he, fc=fc, bk=bk, sl=sl:
                         eng.matmul(ps[0:64, bk, sl],
                                    Sdup[0:64, fc, :], qF[0:64, fc, :],
                                    start=False, stop=True))
                top(lambda eng, ho=ho, p_=p_, bk=bk, sl=sl, c=c:
                    eng.matmul(ps[64:128, bk, sl],
                               g_vt4[:, c, ho, :], AT[:, 2 * p_ + 1, :],
                               start=True, stop=False,
                               tile_position=(0, 64)))
                t2 = top(lambda eng, ho=ho, fc=fc, bk=bk, sl=sl:
                         eng.matmul(ps[64:128, bk, sl],
                                    Sdup[64:128, fc, :], qF[64:128, fc, :],
                                    start=False, stop=True,
                                    tile_position=(64, 64)))
                toks += [t1, t2]
            th = sop(lambda eng, h0=h0, bk=bk, c=c: eng.activation(
                g_attnF2[:, 0, (h0 // 2):(h0 // 2) + 4,
                         c * 128:(c + 1) * 128],
                ps[:, bk, :].rearrange("p (j n) -> p j n", j=4),
                AF.Identity),
                waits=[toks[-1]])
            ev = vop(lambda eng, h0=h0, bk=bk, c=c: eng.tensor_tensor(
                g_attnF2[:, 1, (h0 // 2):(h0 // 2) + 4,
                         c * 128:(c + 1) * 128],
                ps[:, bk, :].rearrange("p (j n) -> p j n", j=4),
                g_attnF2[:, 0, (h0 // 2):(h0 // 2) + 4,
                         c * 128:(c + 1) * 128], ALU.subtract),
                waits=[th])
            PS.done(bk, ev)
            return ev, toks

        kcz_n = prep_kcz_z(0)
        t_kF = transF(g_kt[:, 0, :], kF[:], t_ktd, 2, "kF")
        t_qhat = prep_den(0, kcz_n)
        t_qF = transF(g_qt[:, 0, :], qF[:], t_qhat, 0, "qF")
        for c in range(4):
            S.mark(f"tile{gti}_chunk{c}")
            mmA, mmB = scores_all(t_qF, t_kF)
            evA = mask_evac(0, [mmA[-1], mmB[-1]])
            if DBG and c == 0:
                dump("at0", AT[:, :, :], [evA], "AT")
            if c < 3:
                kcz_n = prep_kcz_z(c + 1)
            evAVa, mmAVa = av_inter(c, 0, 4, evA)
            war_add("AT", mmAVa[-1])
            evB = mask_evac(2, [mmB[-1]])
            if c < 3:
                t_qhat = prep_den(c + 1, kcz_n)
                t_kF = transF(g_kt[:, c + 1, :], kF[:], t_ktd, 2, "kF")
            evAVb, mmAVb = av_inter(c, 8, 5, evB)
            war_add("AT", mmAVb[-1])
            war_add("qF", mmAVa[-1])
            war_add("qF", mmAVb[-1])
            war_add("kF", mmA[-1])
            war_add("kF", mmB[-1])

            ds_toks = []
            for h in range(16):
                sl = slice((h // 2) * 64, (h // 2) * 64 + 64)
                if h % 2 == 0:
                    m = top(lambda eng, h=h, sl=sl, c=c: eng.matmul(
                        ps[0:64, 4, sl], g_kt4[:, c, h, :], g_vt4[:, c, h, :],
                        start=True, stop=True),
                        waits=[t_ktd, t_v] + (PS.use(4) if h == 0 else []))
                else:
                    m = top(lambda eng, h=h, sl=sl, c=c: eng.matmul(
                        ps[64:128, 4, sl], g_kt4[:, c, h, :],
                        g_vt4[:, c, h, :],
                        start=True, stop=True, tile_position=(0, 64)))
                ds_toks.append(m)
            vop(lambda eng: eng.tensor_tensor(
                rs1[:, 0:512],
                Sdup[:].rearrange("p j e -> p (j e)"),
                ps[:, 4, :], ALU.add),
                waits=[ds_toks[-1], mmAVa[-1], mmAVb[-1]] + wwait("rs1"))
            rs3 = rs1[:, 0:512].rearrange("p (j e) -> p j e", e=64)
            pce = bt_PC[:, c, :].rearrange("p (j two) -> p j two", two=2)
            vop(lambda eng, rs3=rs3, pce=pce: eng.tensor_tensor(
                Sdup[0:64, :, :], rs3[0:64],
                pce[0:64, :, 0].unsqueeze(2).broadcast_to([64, 8, 64]),
                ALU.mult))
            t_supd = vop(lambda eng, rs3=rs3, pce=pce: eng.tensor_tensor(
                Sdup[64:128, :, :], rs3[64:128],
                pce[64:128, :, 1].unsqueeze(2).broadcast_to([64, 8, 64]),
                ALU.mult))
            war_add("rs1", t_supd)
            PS.done(4, t_supd)
            if c < 3:
                t_qF = transF(g_qt[:, c + 1, :], qF[:], t_qhat, 0, "qF")

        war["Sdup_ready"] = [t_supd]
        war["z0_ready"] = [t_z0]
        war["Sdup"] = [t_supd]
        war_add("bt_PC", t_supd)
        dump("attnF", G[6][:], [S.tok("v")], "G6")

        S.mark(f"tile{gti}_chunks")
        t_attn_done = S.tok("v")
        ew = wwait("G7")
        h_ev = [None] * 4
        for c in range(4):
            vop(lambda eng, c=c: eng.tensor_tensor(
                g_xn1[:, c], g_xn1[:, c], bo_f[:, :], ALU.add),
                waits=[t_pre])
        t_xn1b = S.tok("v")
        for half in range(2):
            wts = [None] * 4
            pw = []
            for c in range(4):
                pw += PS.use(2 + c)
            last = None
            lastc = [None] * 4
            for k in range(8):
                kg = k // 2
                slot = kg % 2
                if k % 2 == 0:
                    wts[kg] = S.dma(
                        "sync", wbuf[:, slot, :, :, :],
                        whl_d[3, half, :, :, kg * 2:(kg + 1) * 2, :],
                        waits=[wbuf_free[slot], S.tok("d_pre")],
                        sem=f"d_w{slot}")
                for c in range(4):
                    b = 2 + c
                    for (ux, uw) in [(0, 0), (1, 0), (0, 1)]:
                        last = top(
                            lambda eng, c=c, k=k, b=b, slot=slot, ux=ux,
                            uw=uw: eng.matmul(
                                ps[:, b, :],
                                g_attnF2[:, ux, k, c * 128:(c + 1) * 128],
                                wbuf[:, slot, uw, k % 2, :],
                                start=(k == 0 and ux == 0 and uw == 0),
                                stop=(k == 7 and uw == 1)),
                            waits=[wts[kg], t_attn_done] + pw)
                    lastc[c] = last
                if k % 2 == 1:
                    wbuf_free[slot] = last
                pw = []
            for c in range(4):
                b = 2 + c
                ev = vop(lambda eng, c=c, b=b, half=half:
                         eng.scalar_tensor_tensor(
                             g_h[:, c, half * 512:(half + 1) * 512],
                             ps[:, b, :], 1.0,
                             g_xn1[:, c, half * 512:(half + 1) * 512],
                             ALU.mult, ALU.add),
                         waits=[lastc[c], t_xn1b]
                         + (ew if (c == 0 and half == 0) else []))
                PS.done(b, ev)
                if half == 1:
                    h_ev[c] = ev
        war_add("G6", wbuf_free[0])
        war_add("G6", wbuf_free[1])
        t_h = S.tok("v")
        dump("h", G[7][:], [t_h], "G7")

        S.mark(f"tile{gti}_wo")
        stw = wwait("st")
        g_ab = G[3][:].rearrange("p (c f) -> p c f", f=HID)
        g_x1 = G[4][:].rearrange("p (c f) -> p c f", f=HID)
        g_xn2 = G[5][:].rearrange("p (c f) -> p c f", f=HID)
        g_xn2F2 = G[2][:].bitcast(BF16).rearrange("p (u k n) -> p u k n",
                                                  u=2, k=8, n=TILE)
        rsw = wwait("rs1")
        g3w = wwait("G3")
        g4w = wwait("G4")
        g5w = wwait("G5")
        g2w = wwait("G2")
        for c in range(4):
            cs = slice(c, c + 1)
            sop(lambda eng, c=c: eng.activation(
                sdump[:, :], g_h[:, c], AF.Square,
                accum_out=st_sq[:, c:c + 1]),
                waits=[h_ev[c]] + stw + rsw)
            t_rms = rms_finalize(cs, c_rmseps, S.tok("s"))
            t_ab = vop(lambda eng, c=c: eng.scalar_tensor_tensor(
                g_ab[:, c], g_h[:, c], st_r[:, c:c + 1], scale_f[:, :],
                ALU.mult, ALU.mult, accum_out=st_sum[:, c:c + 1]),
                waits=[t_rms, t_pre] + (g3w if c == 0 else []))
            sop(lambda eng, c=c: eng.activation(
                sdump[:, :], g_ab[:, c], AF.Square,
                accum_out=st_sq[:, c:c + 1]), waits=[t_ab])
            t_ln2 = ln_finalize(cs, c_lneps, S.tok("s"), t_ab)
            vop(lambda eng, c=c: eng.tensor_scalar(
                rs1[:, 0:1024], g_ab[:, c], st_mu[:, c:c + 1],
                st_r[:, c:c + 1], ALU.subtract, ALU.mult),
                waits=[t_ln2])
            t_x1 = vop(lambda eng, c=c: eng.scalar_tensor_tensor(
                g_x1[:, c], rs1[:, 0:1024], 1.0, g_x[:, c],
                ALU.mult, ALU.add, accum_out=st_sum[:, c:c + 1]),
                waits=(g4w if c == 0 else []))
            if c == 3:
                t_x1tok = t_x1
            sop(lambda eng, c=c: eng.activation(
                sdump[:, :], g_x1[:, c], AF.Square,
                accum_out=st_sq[:, c:c + 1]), waits=[t_x1])
            t_ln3 = ln_finalize(cs, c_lneps, S.tok("s"), t_x1)
            t_xn2 = vop(lambda eng, c=c: eng.tensor_scalar(
                g_xn2[:, c], g_x1[:, c], st_mu[:, c:c + 1], st_r[:, c:c + 1],
                ALU.subtract, ALU.mult),
                waits=[t_ln3] + (g5w if c == 0 else []))
            for half in range(2):
                b = (c % 2) * 2 + half
                last = None
                for j in range(4):
                    f = half * 4 + j
                    last = top(lambda eng, c=c, f=f, b=b, j=j: eng.transpose(
                        ps[:, b, j * 128:(j + 1) * 128],
                        g_xn2[:, c, f * 128:(f + 1) * 128], iden[:]),
                        waits=[t_xn2] + PS.use(b))
                th = sop(lambda eng, c=c, half=half, b=b: eng.activation(
                    g_xn2F2[:, 0, half * 4:half * 4 + 4,
                            c * 128:(c + 1) * 128],
                    ps[:, b, :].rearrange("p (j n) -> p j n", j=4),
                    AF.Identity),
                    waits=[last] + (g2w if (c == 0 and half == 0) else []))
                ev = vop(lambda eng, c=c, half=half, b=b: eng.tensor_tensor(
                    g_xn2F2[:, 1, half * 4:half * 4 + 4,
                            c * 128:(c + 1) * 128],
                    ps[:, b, :].rearrange("p (j n) -> p j n", j=4),
                    g_xn2F2[:, 0, half * 4:half * 4 + 4,
                            c * 128:(c + 1) * 128], ALU.subtract),
                    waits=[th])
                PS.done(b, ev)
        war_add("rs1", S.tok("v"))
        war_add("st", S.tok("v"))
        t_xn2F = S.tok("v")
        dump("ab", G[3][:], [S.tok("v")], "G3")
        dump("x1", G[4][:], [S.tok("v")], "G4")
        dump("xn2", G[5][:], [S.tok("v")], "G5")

        S.mark(f"tile{gti}_xn2F")
        use_table(SILU)
        hm_toks = [None] * 24

        def hm_view(m, u):
            gi = 1 + (0 if m < 8 else (5 if m < 16 else 6))
            return G[gi][:].bitcast(BF16).rearrange(
                "p (v j n) -> p v j n", v=2, n=TILE)[:, u, m % 8, :]

        for m in range(24):
            slot = m % 2
            bg = (m % 2) * 2
            bu = bg + 1
            twg = S.dma("sync", wgbuf[:, slot, :, :], wgb_d[m, :, :, :],
                        waits=[wgbuf_free[slot], S.tok("d_pre")],
                        sem=f"d_g{slot}")
            twu = S.dma("sync", wubuf[:, slot, :, :], wub_d[m, :, :, :],
                        waits=[wubuf_free[slot], S.tok("d_pre")],
                        sem=f"d_u{slot}")
            lastg = lastu = None
            for k in range(8):
                for u in range(2):
                    lastg = top(lambda eng, k=k, u=u, bg=bg, slot=slot:
                                eng.matmul(
                                    ps[:, bg, :], wgbuf[:, slot, k, :],
                                    g_xn2F2[:, u, k, :],
                                    start=(k == 0 and u == 0),
                                    stop=(k == 7 and u == 1)),
                                waits=[twg, t_xn2F] + PS.use(bg))
            for k in range(8):
                for u in range(2):
                    lastu = top(lambda eng, k=k, u=u, bu=bu, slot=slot:
                                eng.matmul(
                                    ps[:, bu, :], wubuf[:, slot, k, :],
                                    g_xn2F2[:, u, k, :],
                                    start=(k == 0 and u == 0),
                                    stop=(k == 7 and u == 1)),
                                waits=[twu] + PS.use(bu))
            wgbuf_free[slot] = lastg
            wubuf_free[slot] = lastu
            tsil = sop(lambda eng, bg=bg: eng.activation(
                sdump[:, 0:512], ps[:, bg, :], AF.Silu, scale=c_s127),
                waits=[lastg, t_ones] + wwait("rs1"))
            PS.done(bg, tsil)
            warkey = "G" + str(1 + (0 if m < 8 else (5 if m < 16 else 6)))
            vop(lambda eng, bu=bu: eng.tensor_tensor(
                rs1[:, 512:1024], sdump[:, 0:512], ps[:, bu, :], ALU.mult),
                waits=[tsil, lastu])
            vop(lambda eng, m=m: eng.tensor_copy(
                hm_view(m, 0), rs1[:, 512:1024]),
                waits=(wwait(warkey) if m % 8 == 0 else []))
            tm = vop(lambda eng, m=m: eng.tensor_tensor(
                hm_view(m, 1), rs1[:, 512:1024], hm_view(m, 0), ALU.subtract))
            war_add("rs1", tm)
            PS.done(bu, tm)
            hm_toks[m] = tm

        war_add("G2", max(wgbuf_free, key=lambda t: t[1]))
        war_add("G2", max(wubuf_free, key=lambda t: t[1]))
        if gti + 1 < len(seq_tiles):
            emit_ln1(gti + 1, [t_x1tok])

        S.mark(f"tile{gti}_gateup")
        dn_last = {}
        for mk in range(24):
            for half in range(2):
                slot = (mk * 2 + half) % 4
                twd = S.dma("sync", wdbuf[:, slot, :],
                            wdb_d[mk, :, half * 512:(half + 1) * 512],
                            waits=[wdbuf_free[slot], S.tok("d_pre")],
                            sem=f"d_d{slot}")
                last = None
                for c in range(4):
                    b = c * 2 + half
                    for u in range(2):
                        last = top(lambda eng, mk=mk, c=c, u=u, b=b, slot=slot:
                                   eng.matmul(
                                       ps[:, b, :],
                                       hm_view(mk, u)[:, c * 128:(c + 1) * 128],
                                       wdbuf[:, slot, :],
                                       start=(mk == 0 and u == 0),
                                       stop=(mk == 23 and u == 1)),
                                   waits=([twd, hm_toks[mk]]
                                          + (PS.use(b) if mk == 0 else [])))
                    dn_last[(c, half)] = last
                wdbuf_free[slot] = last
        g_f = G[3][:].rearrange("p (c f) -> p c f", f=HID)
        g3w2 = wwait("G3")
        for c in range(4):
            evs = sop(lambda eng, c=c: eng.activation(
                g_f[:, c, 0:512], ps[:, c * 2, :], AF.Identity,
                accum_out=st_sum[:, c:c + 1]),
                waits=[dn_last[(c, 0)]] + (g3w2 if c == 0 else []))
            PS.done(c * 2, evs)
            evv = vop(lambda eng, c=c: eng.tensor_scalar(
                g_f[:, c, 512:1024], ps[:, c * 2 + 1, :],
                0.0, 0.0, ALU.add, ALU.add,
                accum_out=st_t2[:, c:c + 1]),
                waits=[dn_last[(c, 1)]] + (g3w2 if c == 0 else []))
            PS.done(c * 2 + 1, evv)
        t_sevac = S.tok("s")
        tvf = vop(lambda eng: eng.tensor_tensor(
            st_sum[:, 0:4], st_sum[:, 0:4], st_t2[:, 0:4], ALU.add),
            waits=[t_sevac])
        if gti + 1 < len(seq_tiles):
            emit_front(gti + 1)
        for c in range(4):
            sop(lambda eng, c=c: eng.activation(
                sdump[:, :], g_f[:, c], AF.Square,
                accum_out=st_sq[:, c:c + 1]), waits=[tvf] + wwait("rs1"))
        use_table(LN_EXP)
        t_ln4 = ln_finalize(slice(0, 4), c_lneps, S.tok("s"), tvf)
        dump("fout", G[3][:], [S.tok("v"), tvf], "G3")

        g_out = G[3][:].rearrange("p (c f) -> p c f", f=HID)
        for c in range(4):
            vop(lambda eng, c=c: eng.tensor_scalar(
                rs1[:, 0:1024], g_f[:, c], st_mu[:, c:c + 1], st_r[:, c:c + 1],
                ALU.subtract, ALU.mult), waits=[t_ln4] + wwait("rs1"))
            vop(lambda eng, c=c: eng.tensor_tensor(
                g_out[:, c], rs1[:, 0:1024], g_x1[:, c], ALU.add),
                waits=(wwait("G3") if c == 0 else []))
        t_out = S.tok("v")
        war_add("rs1", t_out)
        war_add("st", t_out)
        t_odma = S.op("gpsimd",
                      (lambda o, i: (lambda eng: eng.dma_start(out=o, in_=i)))(
                          outd[tok0:tok0 + TILE, :]
                          .rearrange("(c p) f -> p c f", p=128),
                          g_out[:, :, :]),
                      waits=[t_out], inc="d_out", inc_n=16)
        war_add("G3", t_odma)
        war_add("G3", S.tok("s"))
        war_add("G3", t_out)
        war_add("G1", wgbuf_free[0])
        war_add("G4", t_out)
        war_add("G5", t_xn2F)
        for sl_ in range(4):
            war_add("G6", wdbuf_free[sl_])
            war_add("G7", wdbuf_free[sl_])
            war_add("G1", wdbuf_free[sl_])

    if max_items is not None:
        S.items = S.items[:max_items]
    else:
        fin = [("d_out", S.ctr.get("d_out", 0))]
        if dbg:
            fin.append(("d_dbg", S.ctr.get("d_dbg", 0)))
        S.op("gpsimd", lambda eng: None, waits=fin)

    sem_names = ["d_pre", "d_ff0", "d_x", "d_zp", "d_w0", "d_w1", "d_g0",
                 "d_g1", "d_u0", "d_u1", "d_d0", "d_d1", "d_d2", "d_d3",
                 "d_out", "d_dbg",
                 "t", "v", "s", "g"]
    with ctx:
        sems = {}
        with ExitStack() as sctx:
            for nm in sem_names:
                sems[nm] = sctx.enter_context(nc.semaphore(nm))
            with nc.Block() as block:
                @block.sync
                def _(eng):
                    S.replay("sync", eng, sems)

                @block.tensor
                def _(eng):
                    S.replay("tensor", eng, sems)

                @block.vector
                def _(eng):
                    S.replay("vector", eng, sems)

                @block.scalar
                def _(eng):
                    S.replay("scalar", eng, sems)

                @block.gpsimd
                def _(eng):
                    S.replay("gpsimd", eng, sems)

    nc._sched_marks = S.marks
    return nc


def prep_weights(Wq, Wk, Wv, Wb, bb, Wo, bo, scale, w_gate, w_up, w_down):
    def proj_layout(W):
        WT = np.ascontiguousarray(W.T)
        return np.ascontiguousarray(
            WT.reshape(8, 128, 2, 512).transpose(2, 1, 0, 3))

    out = {
        "wq": proj_layout(Wq), "wk": proj_layout(Wk), "wv": proj_layout(Wv),
        "wo": proj_layout(Wo),
        "wb": np.ascontiguousarray(Wb.T.reshape(8, 128, 16).transpose(1, 0, 2)),
        "bb": np.ascontiguousarray(bb.reshape(1, 16)),
        "bo": np.ascontiguousarray(bo.reshape(1, HID)),
        "bo_full": np.ascontiguousarray(
            np.broadcast_to(bo.reshape(1, HID), (128, HID))),
        "lnscale": np.ascontiguousarray(
            np.broadcast_to(scale.reshape(1, HID), (128, HID))),
    }
    wg8 = w_gate.astype(np.int8)
    wu8 = w_up.astype(np.int8)
    wd8 = w_down.astype(np.int8)
    assert np.array_equal(wg8.astype(np.int32), w_gate)
    out["wg"] = np.ascontiguousarray(
        wg8.T.reshape(8, 128, 24, 128).transpose(2, 1, 0, 3))
    out["wu"] = np.ascontiguousarray(
        wu8.T.reshape(8, 128, 24, 128).transpose(2, 1, 0, 3))
    out["wd"] = np.ascontiguousarray(wd8.T.reshape(24, 128, HID))
    out["utri"] = np.triu(np.ones((128, 128), np.float32))
    out["iden"] = np.eye(128, dtype=np.float32)
    inv = 1.0 / 10000.0 ** (np.arange(0, D, 2, dtype=np.float64) / D)
    t = np.arange(2048, dtype=np.float64)
    fr = t[:, None] * inv[None, :]
    out["costab"] = np.ascontiguousarray(
        np.cos(fr).astype(np.float32).reshape(16, 128, 32).transpose(1, 0, 2))
    out["sintab"] = np.ascontiguousarray(
        np.sin(fr).astype(np.float32).reshape(16, 128, 32).transpose(1, 0, 2))
    return out


_NC_CACHE = {}


def kernel(L0_attn_x, L0_attn_Wq, L0_attn_Wk, L0_attn_Wv, L0_attn_Wb,
           L0_attn_bb, L0_attn_Wo, L0_attn_bo, L0_attn_scale,
           L0_ffn_w_gate, L0_ffn_w_up, L0_ffn_w_down):
    from concourse.bass_utils import run_bass_kernel_spmd

    x = np.asarray(L0_attn_x, dtype=np.float32)
    B, T, _ = x.shape
    n_cores = 8
    per = B // n_cores
    if "nc" not in _NC_CACHE:
        _NC_CACHE["nc"] = build_core_program(n_seq=per, tiles_per_seq=T // TILE,
                                             dbg=False)
    nc = _NC_CACHE["nc"]
    wmaps = prep_weights(np.asarray(L0_attn_Wq), np.asarray(L0_attn_Wk),
                         np.asarray(L0_attn_Wv), np.asarray(L0_attn_Wb),
                         np.asarray(L0_attn_bb), np.asarray(L0_attn_Wo),
                         np.asarray(L0_attn_bo), np.asarray(L0_attn_scale),
                         np.asarray(L0_ffn_w_gate), np.asarray(L0_ffn_w_up),
                         np.asarray(L0_ffn_w_down))
    in_maps = []
    for i in range(n_cores):
        m = dict(wmaps)
        m["x"] = np.ascontiguousarray(
            x[i * per:(i + 1) * per].reshape(per * T, HID))
        in_maps.append(m)
    res = run_bass_kernel_spmd(nc, in_maps, core_ids=list(range(n_cores)))
    out = np.concatenate([r["out"].reshape(per, T, HID) for r in res.results],
                         axis=0)
    return out
